# version 11
# baseline (speedup 1.0000x reference)
import math
import numpy as np
import concourse.bass as bass
import concourse.mybir as mybir
from concourse.bass_utils import run_bass_kernel_spmd

F32 = mybir.dt.float32
BF16 = mybir.dt.bfloat16
AF = mybir.ActivationFunctionType
ALU = mybir.AluOpType

D = 2048
NH = 16
HD = 128
TOWN = 1024
TFULL = 2048
NB = 8
EVEN_BLOCKS = [0, 2, 4, 6, 8, 10, 12, 14]
ODD_BLOCKS = [1, 3, 5, 7, 9, 11, 13, 15]
CSHIFT = 20.0
SCALE_ATT = 1.0 / math.sqrt(HD)

ENGINES = ("tensor", "vector", "scalar", "gpsimd", "sync")
DMA_QUEUES = ("sync", "scalar", "gpsimd")


class Builder:

    def __init__(self, nc):
        self.nc = nc
        self.items = {e: [] for e in ENGINES}
        self.last_w = {}
        self.readers = {}
        self.slot_ord = {}
        self.once = set()
        self.cc_ord = {}
        self.final_waits = []
        self.group_end = {}
        self.open_group = {}
        self.slot_last_ev = {}
        self.last_cc_ev = {}
        self.pending_bar = {}

    def once_slot(self, *names):
        self.once.update(names)

    def barrier(self):
        evs = set()
        for e in ENGINES:
            for it in reversed(self.items[e]):
                if not it["dma"] and not it["cc"]:
                    evs.add(it["ev"])
                    break
        evs.update(self.slot_last_ev.values())
        evs.update(self.last_cc_ev.values())
        for e in ENGINES:
            self.pending_bar[e] = set(evs)

    def op(self, eng, fn, reads=(), writes=(), dma=None, cc=False):
        idx = len(self.items[eng])
        if cc:
            ch = cc if isinstance(cc, str) else "cc0"
            o = self.cc_ord.get(ch, 0)
            self.cc_ord[ch] = o + 1
            ev = ("cc", ch, o)
            self.last_cc_ev[ch] = ev
        elif dma is not None:
            o = self.slot_ord.get(dma, 0)
            self.slot_ord[dma] = o + 1
            ev = ("dma", dma, o)
            self.slot_last_ev[dma] = ev
            if dma in self.once:
                self.open_group.setdefault(dma, []).append(o)
        else:
            ev = ("cmp", eng, idx)
        deps = set()
        bar = self.pending_bar.pop(eng, None)
        if bar:
            deps.update(bar)
        for r in reads:
            lw = self.last_w.get(r)
            if lw is not None:
                deps.add(lw)
            self.readers.setdefault(r, []).append(ev)
        for w in writes:
            lw = self.last_w.get(w)
            if lw is not None:
                deps.add(lw)
            for rd in self.readers.get(w, ()):
                if rd != ev:
                    deps.add(rd)
            self.readers[w] = []
            self.last_w[w] = ev
        if eng == "tensor":
            deps = {d for d in deps if not (d[0] == "cmp" and d[1] == "tensor")}
        for d in deps:
            if d[0] == "dma" and d[1] in self.once:
                grp = self.open_group.get(d[1])
                if grp:
                    end = self.slot_ord[d[1]] - 1
                    for o in grp:
                        self.group_end[(d[1], o)] = end
                    self.open_group[d[1]] = []
        self.items[eng].append(
            dict(fn=fn, deps=deps, ev=ev, dma=dma, cc=cc, inc=False))
        return ev

    def finish(self, events):
        self.final_waits.extend(events)

    def emit(self, es):
        nc = self.nc
        sem_cmp = {e: es.enter_context(nc.semaphore(f"s_{e}")) for e in ENGINES}
        sem_dma = {s: es.enter_context(nc.semaphore(f"d{i}"))
                   for i, s in enumerate(sorted(self.slot_ord))}
        sem_cc = {ch: es.enter_context(nc.semaphore(f"s_cc_{ch}"))
                  for ch in sorted(self.cc_ord)}

        need = set()
        for e in ENGINES:
            for it in self.items[e]:
                for d in it["deps"]:
                    if d[0] == "cmp":
                        need.add((d[1], d[2]))
        for d in self.final_waits:
            if d[0] == "cmp":
                need.add((d[1], d[2]))
        prefix = {}
        for e in ENGINES:
            c = 0
            vals = []
            for i, it in enumerate(self.items[e]):
                if it["dma"] is not None or it["cc"]:
                    vals.append(None)
                    continue
                if (e, i) in need:
                    it["inc"] = True
                    c += 1
                vals.append(c)
            prefix[e] = vals

        def ev_wait(ev):
            kind = ev[0]
            if kind == "cmp":
                return (sem_cmp[ev[1]], prefix[ev[1]][ev[2]])
            if kind == "dma":
                slot, o = ev[1], ev[2]
                if slot in self.once:
                    end = self.group_end.get((slot, o), o)
                    return (sem_dma[slot], 16 * (end + 1))
                return (sem_dma[slot], 16 * (o + 1))
            return (sem_cc[ev[1]], ev[2] + 1)

        def section(eng_name):
            items = self.items[eng_name]
            waited = {}

            def do_waits(eng, deps):
                ws = {}
                for d in deps:
                    sem, val = ev_wait(d)
                    key = id(sem)
                    if val > ws.get(key, (None, -1))[1]:
                        ws[key] = (sem, val)
                for sem, val in ws.values():
                    if waited.get(id(sem), -1) >= val:
                        continue
                    eng.wait_ge(sem, val)
                    waited[id(sem)] = val

            def run(eng):
                for it in items:
                    do_waits(eng, it["deps"])
                    inst = it["fn"](eng)
                    if it["cc"]:
                        ch = it["cc"] if isinstance(it["cc"], str) else "cc0"
                        inst.then_inc(sem_cc[ch], 1)
                    elif it["dma"] is not None:
                        inst.then_inc(sem_dma[it["dma"]], 16)
                    elif it["inc"]:
                        inst.then_inc(sem_cmp[eng_name], 1)
                if eng_name == "sync":
                    do_waits(eng, self.final_waits)
            return run

        with nc.Block() as block:
            block.sync(section("sync"))
            block.scalar(section("scalar"))
            block.vector(section("vector"))
            block.tensor(section("tensor"))
            block.gpsimd(section("gpsimd"))


def emit_ln(b, x_t, xn_t, cst, scr, ps, tag, pre=None):
    nc = b.nc
    xc, xsq = scr["ln_xc"], scr["ln_xsq"]
    rows = scr["rows"]
    A, Bb = scr["ln_A"], scr["ln_B"]
    R0, R1 = ps["R0"], ps["R1"]
    PB = ps["B"]
    PBname = ps["Bname"]

    mean = rows[0:1, 0:1024]
    var = rows[0:1, 1024:2048]
    if pre is None:
        for dc in range(16):
            b.op("scalar", lambda s, dc=dc: s.activation(
                out=xc[:], in_=x_t[dc][:], func=AF.Copy),
                reads=[f"x{dc}"], writes=[f"{tag}xc"])
            for th in range(2):
                b.op("tensor", lambda t, dc=dc, th=th: t.matmul(
                    (R0 if th == 0 else R1)[0:1, :],
                    cst["ones_bf"][:, 0:1], xc[:, th * 512:(th + 1) * 512],
                    start=(dc == 0), stop=(dc == 15)),
                    reads=[f"{tag}xc", "ones_bf"], writes=[f"psR{th}"])
        sx_src = [((R0 if th == 0 else R1)[0:1, :], f"psR{th}") for th in range(2)]
    else:
        sx_src = pre["sx"]
    for th in range(2):
        ap, rg = sx_src[th]
        b.op("vector", lambda v, th=th, ap=ap: v.tensor_scalar(
            mean[0:1, th * 512:(th + 1) * 512], ap,
            1.0 / 2048.0, None, ALU.mult),
            reads=[rg], writes=[f"{tag}mean"])
    if pre is None:
        for dc in range(16):
            b.op("scalar", lambda s, dc=dc: s.activation(
                out=xsq[:], in_=x_t[dc][:], func=AF.Square),
                reads=[f"x{dc}"], writes=[f"{tag}xsq"])
            for th in range(2):
                b.op("tensor", lambda t, dc=dc, th=th: t.matmul(
                    (R0 if th == 0 else R1)[0:1, :],
                    cst["ones_bf"][:, 0:1], xsq[:, th * 512:(th + 1) * 512],
                    start=(dc == 0), stop=(dc == 15)),
                    reads=[f"{tag}xsq", "ones_bf"], writes=[f"psR{th}"])
        sq_src = [((R0 if th == 0 else R1)[0:1, :], f"psR{th}") for th in range(2)]
    else:
        sq_src = pre["sq"]
    b.op("vector", lambda v: v.tensor_tensor(var, mean, mean, ALU.mult),
        reads=[f"{tag}mean"], writes=[f"{tag}var"])
    for th in range(2):
        ap, rg = sq_src[th]
        b.op("vector", lambda v, th=th, ap=ap: v.scalar_tensor_tensor(
            var[0:1, th * 512:(th + 1) * 512], ap,
            1.0 / 2048.0, var[0:1, th * 512:(th + 1) * 512],
            ALU.mult, ALU.subtract),
            reads=[rg, f"{tag}var"], writes=[f"{tag}var"])
    b.op("scalar", lambda s: s.activation(
        out=var, in_=var, func=AF.Ln, bias=cst["ccol_eps_row"]),
        reads=[f"{tag}var", "cst"], writes=[f"{tag}var"])
    b.op("scalar", lambda s: s.activation(
        out=var, in_=var, func=AF.Exp, scale=-0.5),
        reads=[f"{tag}var"], writes=[f"{tag}var"])
    b.op("vector", lambda v: v.scalar_tensor_tensor(
        mean, mean, -1.0, var, ALU.mult, ALU.mult),
        reads=[f"{tag}mean", f"{tag}var"], writes=[f"{tag}mean"])
    for th in range(2):
        b.op("tensor", lambda t, th=th: t.matmul(
            PB[:, th * 512:(th + 1) * 512], cst["ones_row"],
            var[0:1, th * 512:(th + 1) * 512], start=True, stop=True),
            reads=[f"{tag}var", "cst"], writes=[f"{PBname}:{th}"])
        b.op("vector", lambda v, th=th: v.tensor_copy(
            A[:, th * 512:(th + 1) * 512], PB[:, th * 512:(th + 1) * 512]),
            reads=[f"{PBname}:{th}"], writes=[f"{tag}A"])
        b.op("tensor", lambda t, th=th: t.matmul(
            PB[:, th * 512:(th + 1) * 512], cst["ones_row"],
            mean[0:1, th * 512:(th + 1) * 512], start=True, stop=True),
            reads=[f"{tag}mean"], writes=[f"{PBname}:{th}"])
        b.op("vector", lambda v, th=th: v.tensor_copy(
            Bb[:, th * 512:(th + 1) * 512], PB[:, th * 512:(th + 1) * 512]),
            reads=[f"{PBname}:{th}"], writes=[f"{tag}B"])
    for dc in range(16):
        b.op("vector", lambda v, dc=dc: v.tensor_tensor(
            scr["ln_t"][:], x_t[dc][:], A[:], ALU.mult),
            reads=[f"x{dc}", f"{tag}A"], writes=[f"{tag}t"])
        b.op("vector", lambda v, dc=dc: v.tensor_tensor(
            xn_t[dc][:], scr["ln_t"][:], Bb[:], ALU.add),
            reads=[f"{tag}t", f"{tag}B"], writes=[f"xn{dc}"])


def emit_load_x(b, xin, x_t, xrg="xin_d"):
    for dc in range(16):
        b.op("sync", lambda s, dc=dc: s.dma_start(
            out=x_t[dc][:], in_=xin[dc * 128:(dc + 1) * 128, :]),
            reads=[xrg], writes=[f"x{dc}"], dma=f"x{dc}")


def build_proj(layer, core, ctx=None):
    import contextlib
    if ctx is None:
        nc = bass.Bass("TRN2", target_bir_lowering=False, debug=False)
        es = contextlib.ExitStack()
        b = None
        pref = ""
        dram = None
        xrg_in, xrg_out = "xin_d", "xout_d"
    else:
        nc, b, pref, dram = ctx["nc"], ctx["b"], ctx["pref"], ctx["dram"]
        xrg_in, xrg_out = ctx["x_in_rg"], ctx["x_out_rg"]

    def DR(name, shape, dt, kind):
        if dram is not None:
            return dram[name]
        return nc.dram_tensor(name, shape, dt, kind=kind).ap()

    xin = DR("xin", [D, TOWN], F32, "ExternalInput")
    names = ["wq", "wk", "wv"] + (["wg"] if layer == 1 else [])
    wts = {n: DR(n, [4, 16, 128, 512], F32, "ExternalInput") for n in names}
    if layer == 0:
        bcols = DR("bcols", [128, 48], F32, "ExternalInput")
        bvrow = DR("bvrow", [1, D], F32, "ExternalInput")
    if layer == 1:
        cosin = DR("cosin", [128, 2 * TOWN], F32, "ExternalInput")
        protm = DR("protm", [128, 128], F32, "ExternalInput")
    consts = DR("consts", [128, 144], F32, "ExternalInput")

    q_out = DR("q_out", [NH, 128, TOWN], BF16, "ExternalOutput")
    k_out = DR("k_out", [NH, 128, TOWN], BF16, "ExternalOutput")
    v_out = DR("v_out", [TOWN, D], BF16, "ExternalOutput")
    if layer == 1:
        g_out = DR("g_out", [NH, 128, TOWN], BF16, "ExternalOutput")

    eph = contextlib.ExitStack()
    sb = lambda name, shape, dt: eph.enter_context(
        nc.sbuf_tensor(pref + name, shape, dt))
    x_t = [sb(f"x{i}", [128, TOWN], F32) for i in range(16)]
    xn_t = [sb(f"xn{i}", [128, TOWN], BF16) for i in range(16)]
    wbuf = [sb(f"wbuf{i}", [128, 8192], BF16) for i in range(2)]
    scr = {
        "ln_xc": sb("ln_xc", [128, TOWN], BF16),
        "ln_xsq": sb("ln_xsq", [128, TOWN], BF16),
        "rows": sb("rows", [1, 2048], F32),
        "ln_A": sb("ln_A", [128, TOWN], F32),
        "ln_B": sb("ln_B", [128, TOWN], F32),
        "ln_t": sb("ln_t", [128, TOWN], F32),
    }
    cstbuf = sb("cstbuf", [128, 144], F32)
    ones_bf = sb("ones_bf", [128, 128], BF16)
    qk_sb2 = [sb(f"qk_sb{i}", [128, TOWN], BF16) for i in range(2)]
    v_sb = [sb(f"v_sb{i}", [128, 512], BF16) for i in range(2)]
    if layer == 0:
        bcol_sb = sb("bcol_sb", [128, 48], F32)
        bvrow_sb = sb("bvrow_sb", [1, D], BF16)
        bvb_sb = sb("bvb_sb", [128, D], BF16)
    if layer == 1:
        cs_sb = sb("cs_sb", [128, 2 * TOWN], F32)
        pro_sb = sb("pro_sb", [128, 128], BF16)
        qbf_sb2 = [scr["ln_xc"], sb("qbf1", [128, TOWN], BF16)]
        r1_sb2 = [scr["ln_A"], sb("r1b", [128, TOWN], F32)]
        r2_sb2 = [scr["ln_B"], sb("r2b", [128, TOWN], F32)]
        qbn2 = ["lnxc", "qbf1"]
        r1n2 = ["lnA", "r1b"]
        r2n2 = ["lnB", "r2b"]

    if ctx is None:
        PP = [nc.alloc_psum_tensor(f"P{i}", [128, TOWN], F32) for i in range(4)]
        b = Builder(nc)
    else:
        PP = ctx["P"]
        b.barrier()
    psA, psB, psC = PP[0], PP[1], PP[2]
    psR0 = PP[3][0:1, 0:512]
    psR1 = PP[3][0:1, 512:1024]
    ps = {"R0": psR0, "R1": psR1, "B": psC, "Bname": "psC"}
    cst = {"ones_bf": ones_bf, "ones_row": cstbuf[0:1, 8:136],
           "ccol_eps_row": cstbuf[0:1, 1:2], "negC_col": cstbuf[:, 0:1]}

    b.once_slot("cst", "ones_bf", "bcol", "bvrow", "cs", "pro")
    b.op("sync", lambda s: s.dma_start(out=cstbuf[:], in_=consts),
         writes=["cst"], dma="cst")
    b.op("gpsimd", lambda g: g.dma_start(out=ones_bf[:], in_=consts[:, 8:136]),
         writes=["ones_bf"], dma="ones_bf")
    emit_load_x(b, xin, x_t, xrg_in)
    if layer == 0:
        b.op("sync", lambda s: s.dma_start(out=bcol_sb[:], in_=bcols),
             writes=["bcol"], dma="bcol")
        b.op("gpsimd", lambda g: g.dma_start(out=bvrow_sb[:], in_=bvrow),
             writes=["bvrow"], dma="bvrow")
        for j in range(4):
            b.op("tensor", lambda t, j=j: t.matmul(
                psA[:, 0:512], ones_bf[0:1, 0:128],
                bvrow_sb[0:1, j * 512:(j + 1) * 512], start=True, stop=True),
                reads=["bvrow", "ones_bf"], writes=["psA:0"])
            b.op("vector", lambda v, j=j: v.tensor_copy(
                bvb_sb[:, j * 512:(j + 1) * 512], psA[:, 0:512]),
                reads=["psA:0"], writes=["bvb"])
    if layer == 1:
        b.op("sync", lambda s: s.dma_start(out=cs_sb[:], in_=cosin),
             writes=["cs"], dma="cs")
        b.op("gpsimd", lambda g: g.dma_start(out=pro_sb[:], in_=protm),
             writes=["pro"], dma="pro")

    emit_ln(b, x_t, xn_t, cst, scr, ps, "ln",
            pre=(ctx.get("stats") if ctx is not None else None))

    out_evs = []

    def stream_wgroup(name, g, buf_i):
        return b.op("gpsimd", lambda gg, name=name, g=g, buf_i=buf_i: gg.dma_start(
            out=wbuf[buf_i][:].rearrange("p (c f) -> p c f", c=16),
            in_=wts[name][g].rearrange("c p f -> p c f")),
            reads=[], writes=[f"wbuf{buf_i}"], dma=f"wbuf{buf_i}")

    bufi_box = [0]

    def emit_fm_proj(name, out_ap, bias_off, gate=False, post_g0=None):
        for g in range(4):
            buf_i = bufi_box[0]
            stream_wgroup(name, g, buf_i)
            if g == 0 and post_g0 is not None:
                post_g0()
            for hcl in range(4):
                hc = g * 4 + hcl
                pst = psA if (hc % 2 == 0) else psB
                pname = "psA" if (hc % 2 == 0) else "psB"
                for th in range(2):
                    for dc in range(16):
                        b.op("tensor", lambda t, dc=dc, th=th, hcl=hcl, pst=pst, buf_i=buf_i: t.matmul(
                            pst[:, th * 512:(th + 1) * 512],
                            wbuf[buf_i][:, dc * 512 + hcl * 128: dc * 512 + hcl * 128 + 128],
                            xn_t[dc][:, th * 512:(th + 1) * 512],
                            start=(dc == 0), stop=(dc == 15)),
                            reads=[f"wbuf{buf_i}", f"xn{dc}"],
                            writes=[f"{pname}:{th}"])
                qs = qk_sb2[hc % 2]
                qsn = f"qk_sb{hc % 2}"
                if gate:
                    gsb = r2_sb2[hc % 2]
                    gsn = r2n2[hc % 2]
                    b.op("scalar", lambda s, pst=pst, gsb=gsb: s.activation(
                        out=gsb[:], in_=pst[:, :], func=AF.Sigmoid),
                        reads=[f"{pname}:0", f"{pname}:1"], writes=[gsn])
                    b.op("vector", lambda v, pst=pst, qs=qs, gsb=gsb: v.tensor_tensor(
                        qs[:], pst[:, :], gsb[:], ALU.mult),
                        reads=[f"{pname}:0", f"{pname}:1", gsn], writes=[qsn])
                elif layer == 0:
                    for th in range(2):
                        b.op("scalar", lambda s, th=th, pst=pst, hc=hc, bias_off=bias_off, qs=qs: s.activation(
                            out=qs[:, th * 512:(th + 1) * 512],
                            in_=pst[:, th * 512:(th + 1) * 512], func=AF.Identity,
                            bias=bcol_sb[:, bias_off + hc:bias_off + hc + 1]),
                            reads=[f"{pname}:{th}", "bcol"], writes=[qsn])
                else:
                    hp2 = hc % 2
                    qbf_sb, r1_sb, r2_sb = qbf_sb2[hp2], r1_sb2[hp2], r2_sb2[hp2]
                    qbn, r1n, r2n = qbn2[hp2], r1n2[hp2], r2n2[hp2]
                    for th in range(2):
                        b.op("scalar", lambda s, th=th, pst=pst, qbf_sb=qbf_sb: s.activation(
                            out=qbf_sb[:, th * 512:(th + 1) * 512],
                            in_=pst[:, th * 512:(th + 1) * 512], func=AF.Copy),
                            reads=[f"{pname}:{th}"], writes=[qbn])
                    for th in range(2):
                        b.op("tensor", lambda t, th=th, qbf_sb=qbf_sb, hp2=hp2: t.matmul(
                            (psC if hp2 == 0 else psA)[:, th * 512:(th + 1) * 512], pro_sb[:, :],
                            qbf_sb[:, th * 512:(th + 1) * 512], start=True, stop=True),
                            reads=["pro", qbn],
                            writes=[(f"psC:{th}" if hp2 == 0 else f"psA:{th}")])
                    b.op("vector", lambda v, qbf_sb=qbf_sb, r1_sb=r1_sb: v.tensor_tensor(
                        r1_sb[:], qbf_sb[:], cs_sb[:, 0:TOWN], ALU.mult),
                        reads=[qbn, "cs"], writes=[r1n])
                    b.op("vector", lambda v, r2_sb=r2_sb, hp2=hp2: v.tensor_tensor(
                        r2_sb[:], (psC if hp2 == 0 else psA)[:, :], cs_sb[:, TOWN:2 * TOWN], ALU.mult),
                        reads=[("psC:0" if hp2 == 0 else "psA:0"),
                               ("psC:1" if hp2 == 0 else "psA:1"), "cs"],
                        writes=[r2n])
                    b.op("vector", lambda v, qs=qs, r1_sb=r1_sb, r2_sb=r2_sb: v.tensor_tensor(
                        qs[:], r1_sb[:], r2_sb[:], ALU.add),
                        reads=[r1n, r2n], writes=[qsn])
                ev = b.op("sync", lambda s, out_ap=out_ap, hc=hc, qs=qs: s.dma_start(
                    out=out_ap[hc], in_=qs[:]),
                    reads=[qsn],
                    writes=["kv_d" if name == "wk" else ("g_d" if gate else "q_d")],
                    dma=f"st_{qsn}")
                out_evs.append(ev)
            bufi_box[0] ^= 1

    def emit_v_proj():
        for g in range(4):
            buf_i = bufi_box[0]
            stream_wgroup("wv", g, buf_i)
            for tc in range(8):
                pst = psA if (tc % 2 == 0) else psB
                pname = "psA" if (tc % 2 == 0) else "psB"
                vp = tc % 2
                for dc in range(16):
                    b.op("tensor", lambda t, dc=dc, tc=tc, pst=pst, buf_i=buf_i: t.matmul(
                        pst[:, 0:512],
                        xn_t[dc][:, tc * 128:(tc + 1) * 128],
                        wbuf[buf_i][:, dc * 512:(dc + 1) * 512],
                        start=(dc == 0), stop=(dc == 15)),
                        reads=[f"wbuf{buf_i}", f"xn{dc}"], writes=[f"{pname}:0"])
                if layer == 0:
                    b.op("vector", lambda v, g=g, vp=vp, pst=pst: v.tensor_tensor(
                        v_sb[vp][:], pst[:, 0:512],
                        bvb_sb[:, g * 512:(g + 1) * 512], ALU.add),
                        reads=[f"{pname}:0", "bvb"], writes=[f"v_sb{vp}"])
                else:
                    b.op("vector", lambda v, vp=vp, pst=pst: v.tensor_copy(
                        v_sb[vp][:], pst[:, 0:512]),
                        reads=[f"{pname}:0"], writes=[f"v_sb{vp}"])
                ev = b.op("sync", lambda s, g=g, tc=tc, vp=vp: s.dma_start(
                    out=v_out[tc * 128:(tc + 1) * 128, g * 512:(g + 1) * 512],
                    in_=v_sb[vp][:]),
                    reads=[f"v_sb{vp}"], writes=["v_d"], dma=f"st_v{vp}")
                out_evs.append(ev)
            bufi_box[0] ^= 1

    def emit_vg():
        b.op("gpsimd", lambda g: g.collective_compute(
            "AllGather", ALU.bypass,
            replica_groups=[[0, 1], [2, 3], [4, 5], [6, 7]],
            ins=[ctx["v_own"]], outs=[ctx["v_gth"]]),
            reads=["v_d"], writes=["vg_d"], cc="cca")

    emit_v_proj()
    emit_fm_proj("wk", k_out, 16)
    emit_fm_proj("wq", q_out, 0)
    if ctx is not None:
        emit_vg()
        b.op("gpsimd", lambda g: g.collective_compute(
            "AllGather", ALU.bypass,
            replica_groups=[[0, 1], [2, 3], [4, 5], [6, 7]],
            ins=[ctx["k_own"]], outs=[ctx["k_gth"]]),
            reads=["kv_d"], writes=["kg_d"], cc="cca")
    if layer == 1:
        emit_fm_proj("wg", g_out, 32, gate=True)

    if ctx is None:
        b.finish(out_evs)
        b.emit(es)
        eph.close()
        es.close()
        return nc
    eph.close()
    return out_evs


def build_attn(layer, ctx=None):
    import contextlib
    if ctx is None:
        nc = bass.Bass("TRN2", target_bir_lowering=False, debug=False)
        es = contextlib.ExitStack()
        b = None
        pref = ""
        dram = None
        xrg_in, xrg_out = "xin_d", "xout_d"
        kv_gath = None
    else:
        nc, b, pref, dram = ctx["nc"], ctx["b"], ctx["pref"], ctx["dram"]
        xrg_in, xrg_out = ctx["x_in_rg"], ctx["x_out_rg"]
        kv_gath = ctx["kv_gath"]

    def DR(name, shape, dt, kind):
        if dram is not None:
            return dram.get(name)
        return nc.dram_tensor(name, shape, dt, kind=kind).ap()

    xin = DR("xin", [D, TOWN], F32, "ExternalInput")
    q_in = DR("q_in", [NH, 128, TOWN], BF16, "ExternalInput")
    if kv_gath is None:
        k_in = DR("k_in", [NH, 128, TFULL], BF16, "ExternalInput")
        v_in = DR("v_in", [TFULL, D], BF16, "ExternalInput")
    wo = DR("wo", [4, 16, 128, 512], F32, "ExternalInput")
    consts = DR("consts", [128, 144], F32, "ExternalInput")
    if layer == 0:
        bocols = DR("bocols", [128, 16], F32, "ExternalInput")
    else:
        g_in = DR("g_in", [NH, 128, TOWN], BF16, "ExternalInput")
        normw = DR("normw", [128, 1], F32, "ExternalInput")
        maskA = DR("maskA", [128, 128], F32, "ExternalInput")
        maskB = DR("maskB", [128, 128], F32, "ExternalInput")
    xout = DR("xout", [D, TOWN], F32, "ExternalOutput")

    eph = contextlib.ExitStack()
    sb = lambda name, shape, dt: eph.enter_context(
        nc.sbuf_tensor(pref + name, shape, dt))
    v_t = [sb(f"v{i}", [128, D], BF16) for i in range(16)]
    k_h = [sb(f"k_h{i}", [128, TFULL], BF16) for i in range(2)]
    q_h = [sb(f"q_h{i}", [128, TOWN], BF16) for i in range(2)]
    probs = [sb(f"pr{i}", [128, TOWN], BF16) for i in range(2)]
    ao_t = [sb(f"ao{i}", [128, TOWN], BF16) for i in range(16)]
    wbuf = [sb(f"wbuf{i}", [128, 8192], BF16) for i in range(2)]
    cstbuf = sb("cstbuf", [128, 144], F32)
    ones_bf = sb("ones_bf", [128, 128], BF16)
    invrow = sb("invrow", [1, TOWN], F32)
    invrowb = sb("invrowb", [1, TOWN], BF16)
    osb_sb = sb("osb_sb", [128, TOWN], F32)
    stat_xc = sb("stat_xc", [128, TOWN], BF16)
    stat_sq = sb("stat_sq", [128, TOWN], BF16)
    xc_sb = [sb(f"xc{i}", [128, TOWN], F32) for i in range(2)]
    xo_sb = [sb(f"xo{i}", [128, TOWN], F32) for i in range(2)]
    if layer == 0:
        boc_sb = sb("boc_sb", [128, 16], F32)
    else:
        g_hb = [sb(f"g_h{i}", [128, TOWN], BF16) for i in range(2)]
        nw_sb = sb("nw_sb", [128, 1], F32)
        mA_sb = sb("mA_sb", [128, 128], F32)
        mB_sb = sb("mB_sb", [128, 128], F32)
        onorm = sb("onorm", [128, TOWN], F32)
        sq_sb = sb("sq_sb", [128, TOWN], BF16)
        rrow = sb("rrow", [1, TOWN], F32)
        rrowb = sb("rrowb", [1, TOWN], BF16)

    if ctx is None:
        PP = [nc.alloc_psum_tensor(f"P{i}", [128, TOWN], F32) for i in range(4)]
        b = Builder(nc)
    else:
        PP = ctx["P"]
        b.barrier()
    SC = [PP[0], PP[1]]
    OUT = PP[2]
    R0 = PP[3][0:1, 0:512]
    R1 = PP[3][0:1, 512:1024]
    b.once_slot("cst", "ones_bf", "boc", "nw", "mA", "mB")

    b.op("sync", lambda s: s.dma_start(out=cstbuf[:], in_=consts),
         writes=["cst"], dma="cst")
    b.op("gpsimd", lambda g: g.dma_start(out=ones_bf[:], in_=consts[:, 8:136]),
         writes=["ones_bf"], dma="ones_bf")
    for i in range(16):
        if kv_gath is None:
            b.op("scalar", lambda s, i=i: s.dma_start(
                out=v_t[i][:], in_=v_in[i * 128:(i + 1) * 128, :]),
                writes=[f"v{i}"], dma=f"v{i}")
        else:
            r, j = i // 8, i % 8
            src_ap = kv_gath[1][r * 2048 + j * 256: r * 2048 + (j + 1) * 256, :]
            b.op("scalar", lambda s, i=i, src_ap=src_ap: s.dma_start(
                out=v_t[i][:], in_=src_ap.rearrange("(p a) c -> p (a c)", a=2)),
                reads=["vg_d"], writes=[f"v{i}"], dma=f"v{i}")
    if layer == 0:
        b.op("sync", lambda s: s.dma_start(out=boc_sb[:], in_=bocols),
             writes=["boc"], dma="boc")
    else:
        b.op("sync", lambda s: s.dma_start(out=nw_sb[:], in_=normw),
             writes=["nw"], dma="nw")
        b.op("sync", lambda s: s.dma_start(out=mA_sb[:], in_=maskA),
             writes=["mA"], dma="mA")
        b.op("sync", lambda s: s.dma_start(out=mB_sb[:], in_=maskB),
             writes=["mB"], dma="mB")

    def kc_info(kc):
        if layer == 0:
            return 0, None
        jj = kc % 8
        qf = jj * 128
        return qf, jj

    kcs = list(range(16))

    def pieces(qf):
        if qf < 512:
            return [(qf, 512 - qf), (512, 512)]
        return [(qf, 1024 - qf)]

    touch = {0: [], 1: []}
    for kc in kcs:
        qf, _ = kc_info(kc)
        for (qo, w) in pieces(qf):
            bank = 0 if qo < 512 else 1
            touch[bank].append(kc)
    first_t = {bk: (t[0] if t else None) for bk, t in touch.items()}
    last_t = {bk: (t[-1] if t else None) for bk, t in touch.items()}

    def head_dmas(h):
        b.op("sync", lambda s, h=h: s.dma_start(out=q_h[h % 2][:], in_=q_in[h]),
             reads=["q_d"], writes=[f"q_h{h % 2}"], dma=f"q{h % 2}")
        if kv_gath is None:
            b.op("sync", lambda s, h=h: s.dma_start(out=k_h[h % 2][:], in_=k_in[h]),
                 writes=[f"k_h{h % 2}"], dma=f"k{h % 2}")
        else:
            for r in range(2):
                b.op("sync", lambda s, h=h, r=r: s.dma_start(
                    out=k_h[h % 2][:, r * TOWN:(r + 1) * TOWN],
                    in_=kv_gath[0][r * 2048 + h * 128: r * 2048 + (h + 1) * 128, :]),
                    reads=["kg_d"], writes=[f"k_h{h % 2}"], dma=f"k{h % 2}")
        if layer == 1:
            b.op("sync", lambda s, h=h: s.dma_start(out=g_hb[h % 2][:], in_=g_in[h]),
                 reads=["g_d"], writes=[f"g_h{h % 2}"], dma=f"g{h % 2}")

    def emit_scores(h, kc):
        par = kc % 2
        hb = h % 2
        qf, dq = kc_info(kc)
        for (qo, w) in pieces(qf):
            bank = 0 if qo < 512 else 1
            b.op("tensor", lambda t, kc=kc, hb=hb, qo=qo, w=w, par=par: t.matmul(
                SC[par][:, qo:qo + w], k_h[hb][:, kc * 128:(kc + 1) * 128],
                q_h[hb][:, qo:qo + w], start=True, stop=True),
                reads=[f"k_h{hb}", f"q_h{hb}"], writes=[f"SC{par}:{bank}"])
        if layer == 1 and dq is not None:
            mt = mA_sb if kc < 8 else mB_sb
            mr = "mA" if kc < 8 else "mB"
            bank = 0 if dq * 128 < 512 else 1
            b.op("vector", lambda v, dq=dq, par=par, mt=mt: v.tensor_tensor(
                SC[par][:, dq * 128:(dq + 1) * 128],
                SC[par][:, dq * 128:(dq + 1) * 128], mt[:], ALU.add),
                reads=[f"SC{par}:{bank}", mr], writes=[f"SC{par}:{bank}"])
        rd = [f"SC{par}:0", f"SC{par}:1"] if qf < 512 else [f"SC{par}:1"]
        b.op("scalar", lambda s, par=par, qf=qf: s.activation(
            out=probs[par][:, qf:TOWN], in_=SC[par][:, qf:TOWN], func=AF.Exp,
            scale=SCALE_ATT, bias=cstbuf[:, 0:1]),
            reads=rd + ["cst"], writes=[f"pr{par}"])

    def emit_sumout(h, kc):
        par = kc % 2
        qf, _ = kc_info(kc)
        for (qo, w) in pieces(qf):
            bank = 0 if qo < 512 else 1
            Rp = R0 if bank == 0 else R1
            st = (kc == first_t[bank])
            sp = (kc == last_t[bank])
            b.op("tensor", lambda t, qo=qo, w=w, par=par, Rp=Rp, st=st, sp=sp, bank=bank: t.matmul(
                Rp[0:1, qo - bank * 512:qo - bank * 512 + w],
                ones_bf[:, 0:1], probs[par][:, qo:qo + w],
                start=st, stop=sp, skip_group_check=True),
                reads=[f"pr{par}", "ones_bf"], writes=[f"R{bank}"])
            b.op("tensor", lambda t, kc=kc, h=h, qo=qo, w=w, par=par, st=st, sp=sp: t.matmul(
                OUT[:, qo:qo + w], v_t[kc][:, h * 128:(h + 1) * 128],
                probs[par][:, qo:qo + w],
                start=st, stop=sp, skip_group_check=True),
                reads=[f"pr{par}", f"v{kc}"], writes=[f"OUT:{bank}"])

    head_dmas(0)
    for h in range(NH):
        if h + 1 < NH:
            head_dmas(h + 1)
        prev = None
        for kc in kcs:
            emit_scores(h, kc)
            if prev is not None:
                emit_sumout(h, prev)
            prev = kc
        emit_sumout(h, prev)
        b.op("vector", lambda v: v.tensor_copy(osb_sb[:], OUT[:, :]),
             reads=["OUT:0", "OUT:1"], writes=["osb"])
        b.op("scalar", lambda s: s.activation(
            out=invrow[0:1, 0:512], in_=R0[0:1, :], func=AF.Ln),
            reads=["R0"], writes=["invrow"])
        b.op("scalar", lambda s: s.activation(
            out=invrow[0:1, 512:1024], in_=R1[0:1, :], func=AF.Ln),
            reads=["R1"], writes=["invrow"])
        b.op("scalar", lambda s: s.activation(
            out=invrowb[0:1, :], in_=invrow[0:1, :], func=AF.Exp, scale=-1.0),
            reads=["invrow"], writes=["invrowb"])
        for th in range(2):
            b.op("tensor", lambda t, th=th: t.matmul(
                OUT[:, th * 512:(th + 1) * 512], ones_bf[0:1, 0:128],
                invrowb[0:1, th * 512:(th + 1) * 512], start=True, stop=True),
                reads=["invrowb", "ones_bf"], writes=[f"OUT:{th}"])
        if layer == 0:
            b.op("vector", lambda v, h=h: v.tensor_tensor(
                ao_t[h][:], osb_sb[:], OUT[:, :], ALU.mult),
                reads=["osb", "OUT:0", "OUT:1"], writes=[f"ao{h}"])
        else:
            hb = h % 2
            b.op("vector", lambda v: v.tensor_tensor(
                onorm[:], osb_sb[:], OUT[:, :], ALU.mult),
                reads=["osb", "OUT:0", "OUT:1"], writes=["onorm"])
            b.op("scalar", lambda s: s.activation(
                out=sq_sb[:], in_=onorm[:], func=AF.Square),
                reads=["onorm"], writes=["sq"])
            for th in range(2):
                Rp = R0 if th == 0 else R1
                b.op("tensor", lambda t, th=th, Rp=Rp: t.matmul(
                    Rp[0:1, :], ones_bf[:, 0:1], sq_sb[:, th * 512:(th + 1) * 512],
                    start=True, stop=True),
                    reads=["sq", "ones_bf"], writes=[f"R{th}"])
                b.op("scalar", lambda s, th=th, Rp=Rp: s.activation(
                    out=rrow[0:1, th * 512:(th + 1) * 512], in_=Rp[0:1, :],
                    func=AF.Ln, scale=1.0 / 128.0, bias=cstbuf[0:1, 2:3]),
                    reads=[f"R{th}", "cst"], writes=["rrow"])
            b.op("scalar", lambda s: s.activation(
                out=rrowb[0:1, :], in_=rrow[0:1, :], func=AF.Exp, scale=-0.5),
                reads=["rrow"], writes=["rrowb"])
            for th in range(2):
                b.op("tensor", lambda t, th=th: t.matmul(
                    SC[1][:, th * 512:(th + 1) * 512], ones_bf[0:1, 0:128],
                    rrowb[0:1, th * 512:(th + 1) * 512], start=True, stop=True),
                    reads=["rrowb", "ones_bf"], writes=[f"SC1:{th}"])
            b.op("vector", lambda v: v.tensor_tensor(
                onorm[:], onorm[:], SC[1][:, :], ALU.mult),
                reads=["onorm", "SC1:0", "SC1:1"], writes=["onorm"])
            b.op("vector", lambda v, h=h, hb=hb: v.scalar_tensor_tensor(
                ao_t[h][:], onorm[:], nw_sb[:, 0:1], g_hb[hb][:],
                ALU.mult, ALU.mult),
                reads=["onorm", "nw", f"g_h{hb}"], writes=[f"ao{h}"])

    out_evs = []
    for g in range(4):
        b.op("gpsimd", lambda gg, g=g: gg.dma_start(
            out=wbuf[g % 2][:].rearrange("p (c f) -> p c f", c=16),
            in_=wo[g].rearrange("c p f -> p c f")),
            writes=[f"wbuf{g % 2}"], dma=f"wbuf{g % 2}")
        for hcl in range(4):
            dcog = g * 4 + hcl
            par = dcog % 2
            pp = SC[0] if par == 0 else OUT
            ppn = "SC0" if par == 0 else "OUT"
            b.op("sync", lambda s, dcog=dcog, par=par: s.dma_start(
                out=xc_sb[par][:], in_=xin[dcog * 128:(dcog + 1) * 128, :]),
                reads=[xrg_in], writes=[f"xc{par}"], dma=f"xi{par}")
            for th in range(2):
                for hc in range(16):
                    b.op("tensor", lambda t, g=g, hcl=hcl, th=th, hc=hc, pp=pp: t.matmul(
                        pp[:, th * 512:(th + 1) * 512],
                        wbuf[g % 2][:, hc * 512 + hcl * 128: hc * 512 + hcl * 128 + 128],
                        ao_t[hc][:, th * 512:(th + 1) * 512],
                        start=(hc == 0), stop=(hc == 15)),
                        reads=[f"wbuf{g % 2}", f"ao{hc}"], writes=[f"{ppn}:{th}"])
            if layer == 0:
                b.op("vector", lambda v, dcog=dcog, par=par, pp=pp, ppn=ppn: v.scalar_tensor_tensor(
                    xo_sb[par][:], pp[:, :], boc_sb[:, dcog:dcog + 1],
                    xc_sb[par][:], ALU.add, ALU.add),
                    reads=[f"{ppn}:0", f"{ppn}:1", "boc", f"xc{par}"],
                    writes=[f"xo{par}"])
            else:
                b.op("vector", lambda v, par=par, pp=pp, ppn=ppn: v.tensor_tensor(
                    xo_sb[par][:], pp[:, :], xc_sb[par][:], ALU.add),
                    reads=[f"{ppn}:0", f"{ppn}:1", f"xc{par}"],
                    writes=[f"xo{par}"])
            if ctx is not None:
                b.op("scalar", lambda s, par=par: s.activation(
                    out=stat_xc[:], in_=xo_sb[par][:], func=AF.Copy),
                    reads=[f"xo{par}"], writes=["stat_xc"])
                b.op("scalar", lambda s, par=par: s.activation(
                    out=stat_sq[:], in_=xo_sb[par][:], func=AF.Square),
                    reads=[f"xo{par}"], writes=["stat_sq"])
                for th in range(2):
                    b.op("tensor", lambda t, th=th, dcog=dcog: t.matmul(
                        (R0 if th == 0 else R1)[0:1, :],
                        ones_bf[:, 0:1], stat_xc[:, th * 512:(th + 1) * 512],
                        start=(dcog == 0), stop=(dcog == 15), skip_group_check=True),
                        reads=["stat_xc", "ones_bf"], writes=[f"R{th}"])
                    b.op("tensor", lambda t, th=th, dcog=dcog: t.matmul(
                        SC[1][0:1, th * 512:(th + 1) * 512],
                        ones_bf[:, 0:1], stat_sq[:, th * 512:(th + 1) * 512],
                        start=(dcog == 0), stop=(dcog == 15), skip_group_check=True),
                        reads=["stat_sq", "ones_bf"], writes=[f"SC1:{th}"])
            ev = b.op("sync", lambda s, dcog=dcog, par=par: s.dma_start(
                out=xout[dcog * 128:(dcog + 1) * 128, :], in_=xo_sb[par][:]),
                reads=[f"xo{par}"], writes=[xrg_out], dma=f"st_xo{par}")
            out_evs.append(ev)

    if ctx is None:
        b.finish(out_evs)
        b.emit(es)
        eph.close()
        es.close()
        return nc
    eph.close()
    return out_evs


def build_moe(ctx=None):
    import contextlib
    if ctx is None:
        nc = bass.Bass("TRN2", target_bir_lowering=False, debug=False)
        es = contextlib.ExitStack()
        b = None
        pref = ""
        dram = None
        xrg_in, xrg_out = "xin_d", "xout_d"
    else:
        nc, b, pref, dram = ctx["nc"], ctx["b"], ctx["pref"], ctx["dram"]
        xrg_in, xrg_out = ctx["x_in_rg"], ctx["x_out_rg"]

    def DR(name, shape, dt, kind):
        if dram is not None:
            return dram[name]
        return nc.dram_tensor(name, shape, dt, kind=kind).ap()

    xin = DR("xin", [D, TOWN], F32, "ExternalInput")
    wg = DR("wg", [16, 128, 8], F32, "ExternalInput")
    mbias = DR("mbias", [8, 1], F32, "ExternalInput")
    w1 = DR("w1", [8, 4, 16, 128, 256], F32, "ExternalInput")
    w3 = DR("w3", [8, 4, 16, 128, 256], F32, "ExternalInput")
    w2 = DR("w2", [8, 4, 8, 128, 512], F32, "ExternalInput")
    ident = DR("ident", [128, 128], F32, "ExternalInput")
    consts = DR("consts", [128, 144], F32, "ExternalInput")
    xout = DR("xout", [D, TOWN], F32, "ExternalOutput")

    eph = contextlib.ExitStack()
    sb = lambda name, shape, dt: eph.enter_context(
        nc.sbuf_tensor(pref + name, shape, dt))
    xn_t = [sb(f"xn{i}", [128, TOWN], BF16) for i in range(16)]
    xb = [sb(f"xb{i}", [128, TOWN], F32) for i in range(2)]
    xc = sb("lxc", [128, TOWN], BF16)
    xsq = sb("lxsq", [128, TOWN], BF16)
    rows = sb("rows", [1, 2048], F32)
    A_sb = sb("lnA", [128, TOWN], F32)
    B_sb = sb("lnB", [128, TOWN], F32)
    T_sb = sb("lnT", [128, TOWN], F32)
    rw_sb = sb("rw", [8, TOWN], F32)
    cstbuf = sb("cstbuf", [128, 144], F32)
    ones_bf = sb("ones_bf", [128, 128], BF16)
    id_sb = sb("id_sb", [128, 128], F32)
    mb_sb = sb("mb_sb", [8, 1], F32)
    wgt_sb = sb("wgt_sb", [128, 16 * 8], F32)
    sct = sb("sct", [128, 64], F32)
    rwt = sb("rwt", [128, 64], F32)
    ewt = sb("ewt", [128, 64], F32)
    s2_sb = sb("s2t", [128, 8], F32)
    col1 = sb("col1", [128, 1], F32)
    col2 = sb("col2", [128, 1], F32)
    Bf = sb("Bf", [8, 1], F32)
    ew_fm = sb("ew_fm", [8, TOWN], F32)
    ewrow = sb("ewrow", [1, TOWN], F32)
    ew_bc = [sb(f"ewb{e}", [128, TOWN], BF16) for e in range(8)]
    w1buf = [sb(f"w1b{i}", [128, 4096], BF16) for i in range(2)]
    w3buf = [sb(f"w3b{i}", [128, 4096], BF16) for i in range(2)]
    w2buf = [sb(f"w2b{i}", [128, 4096], BF16) for i in range(2)]
    s1_sb = sb("s1", [128, TOWN], BF16)
    t3_sb = sb("t3", [128, TOWN], BF16)
    g_t = [sb(f"g{i}", [128, TOWN], BF16) for i in range(8)]
    op_t = [sb(f"op{i}", [128, TOWN], BF16) for i in range(16)]

    if ctx is None:
        P = [nc.alloc_psum_tensor(f"P{i}", [128, TOWN], F32) for i in range(4)]
        b = Builder(nc)
    else:
        P = ctx["P"]
        b.barrier()
    b.once_slot("cst", "ones_bf", "id", "mb", "wgt")
    b.op("sync", lambda s: s.dma_start(out=cstbuf[:], in_=consts),
         writes=["cst"], dma="cst")
    b.op("gpsimd", lambda g: g.dma_start(out=ones_bf[:], in_=consts[:, 8:136]),
         writes=["ones_bf"], dma="ones_bf")
    b.op("sync", lambda s: s.dma_start(out=id_sb[:], in_=ident),
         writes=["id"], dma="id")
    b.op("sync", lambda s: s.dma_start(out=mb_sb[:], in_=mbias),
         writes=["mb"], dma="mb")
    b.op("sync", lambda s: s.dma_start(
        out=wgt_sb[:].rearrange("p (c e) -> p c e", c=16),
        in_=wg.rearrange("c p e -> p c e")), writes=["wgt"], dma="wgt")

    stats = ctx.get("stats") if ctx is not None else None
    mean = rows[0:1, 0:1024]
    var = rows[0:1, 1024:2048]
    if stats is not None:
        for th in range(2):
            ap, rg = stats["sx"][th]
            b.op("vector", lambda v, th=th, ap=ap: v.tensor_scalar(
                mean[0:1, th * 512:(th + 1) * 512], ap,
                1.0 / 2048.0, None, ALU.mult),
                reads=[rg], writes=["mean"])
        b.op("vector", lambda v: v.tensor_tensor(var, mean, mean, ALU.mult),
             reads=["mean"], writes=["var"])
        for th in range(2):
            ap, rg = stats["sq"][th]
            b.op("vector", lambda v, th=th, ap=ap: v.scalar_tensor_tensor(
                var[0:1, th * 512:(th + 1) * 512], ap,
                1.0 / 2048.0, var[0:1, th * 512:(th + 1) * 512], ALU.mult, ALU.subtract),
                reads=[rg, "var"], writes=["var"])
    for dc in range(16):
        par = dc % 2
        b.op("sync", lambda s, dc=dc, par=par: s.dma_start(
            out=xb[par][:], in_=xin[dc * 128:(dc + 1) * 128, :]),
            reads=[xrg_in], writes=[f"xb{par}"], dma=f"x{par}")
        if stats is None:
            b.op("scalar", lambda s, par=par: s.activation(
                out=xc[:], in_=xb[par][:], func=AF.Copy),
                reads=[f"xb{par}"], writes=["xc"])
            b.op("scalar", lambda s, par=par: s.activation(
                out=xsq[:], in_=xb[par][:], func=AF.Square),
                reads=[f"xb{par}"], writes=["xsq"])
            for th in range(2):
                b.op("tensor", lambda t, dc=dc, th=th: t.matmul(
                    P[0][0:1, th * 512:(th + 1) * 512],
                    ones_bf[:, 0:1], xc[:, th * 512:(th + 1) * 512],
                    start=(dc == 0), stop=(dc == 15), skip_group_check=True),
                    reads=["xc", "ones_bf"], writes=[f"P0:{th}"])
                b.op("tensor", lambda t, dc=dc, th=th: t.matmul(
                    P[1][0:1, th * 512:(th + 1) * 512],
                    ones_bf[:, 0:1], xsq[:, th * 512:(th + 1) * 512],
                    start=(dc == 0), stop=(dc == 15), skip_group_check=True),
                    reads=["xsq", "ones_bf"], writes=[f"P1:{th}"])
        for th in range(2):
            b.op("tensor", lambda t, dc=dc, th=th, par=par: t.matmul(
                P[2][0:8, th * 512:(th + 1) * 512],
                wgt_sb[:, dc * 8:(dc + 1) * 8], xb[par][:, th * 512:(th + 1) * 512],
                start=(dc == 0), stop=(dc == 15), skip_group_check=True),
                reads=["wgt", f"xb{par}"], writes=[f"P2:{th}"])
        b.op("tensor", lambda t, dc=dc: t.matmul(
            P[3][0:8, 0:1], wgt_sb[:, dc * 8:(dc + 1) * 8], cstbuf[:, 8:9],
            start=(dc == 0), stop=(dc == 15), skip_group_check=True),
            reads=(["wgt", "cst"] + ([stats["sx"][0][1], stats["sx"][1][1]]
                                     if stats is not None else [])),
            writes=["P3:0"])

    if stats is None:
        for th in range(2):
            b.op("vector", lambda v, th=th: v.tensor_scalar(
                mean[0:1, th * 512:(th + 1) * 512], P[0][0:1, th * 512:(th + 1) * 512],
                1.0 / 2048.0, None, ALU.mult),
                reads=[f"P0:{th}"], writes=["mean"])
        b.op("vector", lambda v: v.tensor_tensor(var, mean, mean, ALU.mult),
             reads=["mean"], writes=["var"])
        for th in range(2):
            b.op("vector", lambda v, th=th: v.scalar_tensor_tensor(
                var[0:1, th * 512:(th + 1) * 512], P[1][0:1, th * 512:(th + 1) * 512],
                1.0 / 2048.0, var[0:1, th * 512:(th + 1) * 512], ALU.mult, ALU.subtract),
                reads=[f"P1:{th}", "var"], writes=["var"])
    b.op("scalar", lambda s: s.activation(
        out=var, in_=var, func=AF.Ln, bias=cstbuf[0:1, 1:2]),
        reads=["var", "cst"], writes=["var"])
    b.op("scalar", lambda s: s.activation(
        out=var, in_=var, func=AF.Exp, scale=-0.5),
        reads=["var"], writes=["var"])
    b.op("vector", lambda v: v.tensor_copy(Bf[:], P[3][0:8, 0:1]),
         reads=["P3:0"], writes=["Bf"])
    b.op("vector", lambda v: v.tensor_tensor(mean, mean, var, ALU.mult),
         reads=["mean", "var"], writes=["mean"])
    b.op("vector", lambda v: v.tensor_copy(T_sb[0:8, :], P[2][0:8, :]),
         reads=["P2:0", "P2:1"], writes=["lnT"])
    for th in range(2):
        b.op("tensor", lambda t, th=th: t.matmul(
            P[3][0:8, th * 512:(th + 1) * 512], cstbuf[0:1, 8:16],
            var[0:1, th * 512:(th + 1) * 512], start=True, stop=True),
            reads=["var", "cst"], writes=[f"P3:{th}"])
    b.op("vector", lambda v: v.tensor_tensor(
        A_sb[0:8, :], T_sb[0:8, :], P[3][0:8, :], ALU.mult),
        reads=["lnT", "P3:0", "P3:1"], writes=["lnA"])
    for th in range(2):
        b.op("tensor", lambda t, th=th: t.matmul(
            P[3][0:8, th * 512:(th + 1) * 512], cstbuf[0:1, 8:16],
            mean[0:1, th * 512:(th + 1) * 512], start=True, stop=True),
            reads=["mean", "cst"], writes=[f"P3:{th}"])
    b.op("vector", lambda v: v.tensor_scalar(
        B_sb[0:8, :], P[3][0:8, :], Bf[:, 0:1], None, ALU.mult),
        reads=["P3:0", "P3:1", "Bf"], writes=["lnB"])
    b.op("vector", lambda v: v.tensor_tensor(
        A_sb[0:8, :], A_sb[0:8, :], B_sb[0:8, :], ALU.subtract),
        reads=["lnA", "lnB"], writes=["lnA"])
    b.op("scalar", lambda s: s.activation(
        out=rw_sb[:], in_=A_sb[0:8, :], func=AF.Exp, scale=-1.0),
        reads=["lnA"], writes=["rw"])
    b.op("vector", lambda v: v.tensor_scalar(
        rw_sb[:], rw_sb[:], 1.0, None, ALU.add),
        reads=["rw"], writes=["rw"])
    b.op("vector", lambda v: v.reciprocal(rw_sb[:], rw_sb[:]),
         reads=["rw"], writes=["rw"])
    b.op("vector", lambda v: v.tensor_scalar(
        T_sb[0:8, :], rw_sb[:], mb_sb[0:8, 0:1], None, ALU.add),
        reads=["rw", "mb"], writes=["lnT"])

    for tc in range(8):
        b.op("tensor", lambda t, tc=tc: t.transpose(
            P[3][:, 0:8], T_sb[0:8, tc * 128:(tc + 1) * 128], id_sb[0:8, 0:8]),
            reads=["lnT", "id"], writes=["P3:0"])
        b.op("vector", lambda v, tc=tc: v.tensor_copy(
            sct[:, tc * 8:(tc + 1) * 8], P[3][:, 0:8]),
            reads=["P3:0"], writes=["sct"])
        b.op("tensor", lambda t, tc=tc: t.transpose(
            P[3][:, 0:8], rw_sb[0:8, tc * 128:(tc + 1) * 128], id_sb[0:8, 0:8]),
            reads=["rw", "id"], writes=["P3:0"])
        b.op("vector", lambda v, tc=tc: v.tensor_copy(
            rwt[:, tc * 8:(tc + 1) * 8], P[3][:, 0:8]),
            reads=["P3:0"], writes=["rwt"])
    for tc in range(8):
        s_ = sct[:, tc * 8:(tc + 1) * 8]
        r_ = rwt[:, tc * 8:(tc + 1) * 8]
        e_ = ewt[:, tc * 8:(tc + 1) * 8]
        b.op("vector", lambda v, s_=s_: v.reduce_max(
            col1[:, 0:1], s_, axis=mybir.AxisListType.X),
            reads=["sct"], writes=["col1"])
        b.op("vector", lambda v, s_=s_, e_=e_: v.tensor_scalar(
            e_, s_, col1[:, 0:1], None, ALU.is_ge),
            reads=["sct", "col1"], writes=["ewt"])
        b.op("vector", lambda v, s_=s_, e_=e_: v.scalar_tensor_tensor(
            s2_sb[:], e_, -1e30, s_, ALU.mult, ALU.add),
            reads=["ewt", "sct"], writes=["s2t"])
        b.op("vector", lambda v: v.reduce_max(
            col2[:, 0:1], s2_sb[:], axis=mybir.AxisListType.X),
            reads=["s2t"], writes=["col2"])
        b.op("vector", lambda v: v.tensor_scalar(
            s2_sb[:], s2_sb[:], col2[:, 0:1], None, ALU.is_ge),
            reads=["s2t", "col2"], writes=["s2t"])
        b.op("vector", lambda v, e_=e_: v.tensor_tensor(
            e_, e_, s2_sb[:], ALU.add),
            reads=["ewt", "s2t"], writes=["ewt"])
        b.op("vector", lambda v, r_=r_, e_=e_: v.tensor_tensor(
            e_, e_, r_, ALU.mult),
            reads=["ewt", "rwt"], writes=["ewt"])
        b.op("vector", lambda v, e_=e_: v.reduce_sum(
            col1[:, 0:1], e_, axis=mybir.AxisListType.X),
            reads=["ewt"], writes=["col1"])
        b.op("vector", lambda v: v.reciprocal(col1[:, 0:1], col1[:, 0:1]),
             reads=["col1"], writes=["col1"])
        b.op("vector", lambda v, e_=e_: v.tensor_scalar(
            e_, e_, col1[:, 0:1], None, ALU.mult),
            reads=["ewt", "col1"], writes=["ewt"])
    for tc in range(8):
        b.op("tensor", lambda t, tc=tc: t.transpose(
            P[3][0:8, 0:128], ewt[:, tc * 8:(tc + 1) * 8], id_sb[:, 0:128]),
            reads=["ewt", "id"], writes=["P3:0"])
        b.op("vector", lambda v, tc=tc: v.tensor_copy(
            ew_fm[0:8, tc * 128:(tc + 1) * 128], P[3][0:8, 0:128]),
            reads=["P3:0"], writes=["ew_fm"])
    for e in range(8):
        b.op("gpsimd", lambda g, e=e: g.dma_start(
            out=ewrow[0:1, :], in_=ew_fm[e:e + 1, :]),
            reads=["ew_fm"], writes=["ewrow"], dma="ewrow")
        for th in range(2):
            b.op("tensor", lambda t, th=th: t.matmul(
                P[3][:, th * 512:(th + 1) * 512], cstbuf[0:1, 8:136],
                ewrow[0:1, th * 512:(th + 1) * 512], start=True, stop=True),
                reads=["ewrow", "cst"], writes=[f"P3:{th}"])
        b.op("vector", lambda v, e=e: v.tensor_copy(
            ew_bc[e][:], P[3][:, :]),
            reads=["P3:0", "P3:1"], writes=[f"ewb{e}"])

    for th in range(2):
        b.op("tensor", lambda t, th=th: t.matmul(
            P[0][:, th * 512:(th + 1) * 512], cstbuf[0:1, 8:136],
            var[0:1, th * 512:(th + 1) * 512], start=True, stop=True),
            reads=["var", "cst"], writes=[f"P0:{th}"])
        b.op("vector", lambda v, th=th: v.tensor_copy(
            A_sb[:, th * 512:(th + 1) * 512], P[0][:, th * 512:(th + 1) * 512]),
            reads=[f"P0:{th}"], writes=["lnA"])
        b.op("tensor", lambda t, th=th: t.matmul(
            P[1][:, th * 512:(th + 1) * 512], cstbuf[0:1, 8:136],
            mean[0:1, th * 512:(th + 1) * 512], start=True, stop=True),
            reads=["mean", "cst"], writes=[f"P1:{th}"])
        b.op("vector", lambda v, th=th: v.tensor_scalar(
            B_sb[:, th * 512:(th + 1) * 512], P[1][:, th * 512:(th + 1) * 512],
            -1.0, None, ALU.mult),
            reads=[f"P1:{th}"], writes=["lnB"])
    for dc in range(16):
        par = dc % 2
        b.op("sync", lambda s, dc=dc, par=par: s.dma_start(
            out=xb[par][:], in_=xin[dc * 128:(dc + 1) * 128, :]),
            reads=[xrg_in], writes=[f"xb{par}"], dma=f"x{par}")
        b.op("vector", lambda v, par=par: v.tensor_tensor(
            T_sb[:], xb[par][:], A_sb[:], ALU.mult),
            reads=[f"xb{par}", "lnA"], writes=["lnT"])
        b.op("vector", lambda v, dc=dc: v.tensor_tensor(
            xn_t[dc][:], T_sb[:], B_sb[:], ALU.add),
            reads=["lnT", "lnB"], writes=[f"xn{dc}"])

    for e in range(8):
        for gq in range(4):
            wpar = gq % 2
            b.op("gpsimd", lambda g, e=e, gq=gq, wpar=wpar: g.dma_start(
                out=w1buf[wpar][:].rearrange("p (c f) -> p c f", c=16),
                in_=w1[e, gq].rearrange("c p f -> p c f")),
                writes=[f"w1b{wpar}"], dma=f"w1b{wpar}")
            b.op("gpsimd", lambda g, e=e, gq=gq, wpar=wpar: g.dma_start(
                out=w3buf[wpar][:].rearrange("p (c f) -> p c f", c=16),
                in_=w3[e, gq].rearrange("c p f -> p c f")),
                writes=[f"w3b{wpar}"], dma=f"w3b{wpar}")
            for fcl in range(2):
                fc = gq * 2 + fcl
                for th in range(2):
                    for dc in range(16):
                        b.op("tensor", lambda t, dc=dc, th=th, fcl=fcl, wpar=wpar: t.matmul(
                            P[0][:, th * 512:(th + 1) * 512],
                            w1buf[wpar][:, dc * 256 + fcl * 128: dc * 256 + fcl * 128 + 128],
                            xn_t[dc][:, th * 512:(th + 1) * 512],
                            start=(dc == 0), stop=(dc == 15)),
                            reads=[f"w1b{wpar}", f"xn{dc}"], writes=[f"P0:{th}"])
                    for dc in range(16):
                        b.op("tensor", lambda t, dc=dc, th=th, fcl=fcl, wpar=wpar: t.matmul(
                            P[1][:, th * 512:(th + 1) * 512],
                            w3buf[wpar][:, dc * 256 + fcl * 128: dc * 256 + fcl * 128 + 128],
                            xn_t[dc][:, th * 512:(th + 1) * 512],
                            start=(dc == 0), stop=(dc == 15)),
                            reads=[f"w3b{wpar}", f"xn{dc}"], writes=[f"P1:{th}"])
                b.op("scalar", lambda s: s.activation(
                    out=s1_sb[:], in_=P[0][:, :], func=AF.Silu),
                    reads=["P0:0", "P0:1"], writes=["s1"])
                b.op("vector", lambda v, e=e: v.tensor_tensor(
                    t3_sb[:], P[1][:, :], ew_bc[e][:], ALU.mult),
                    reads=["P1:0", "P1:1", f"ewb{e}"], writes=["t3"])
                b.op("vector", lambda v, fc=fc: v.tensor_tensor(
                    g_t[fc][:], s1_sb[:], t3_sb[:], ALU.mult),
                    reads=["s1", "t3"], writes=[f"g{fc}"])
        for g4 in range(4):
            wpar = g4 % 2
            b.op("gpsimd", lambda g, e=e, g4=g4, wpar=wpar: g.dma_start(
                out=w2buf[wpar][:].rearrange("p (c f) -> p c f", c=8),
                in_=w2[e, g4].rearrange("c p f -> p c f")),
                writes=[f"w2b{wpar}"], dma=f"w2b{wpar}")
            for dco in range(4):
                dcog = g4 * 4 + dco
                pp = 2 + (dco % 2)
                for th in range(2):
                    for fc in range(8):
                        b.op("tensor", lambda t, fc=fc, th=th, dco=dco, wpar=wpar, pp=pp: t.matmul(
                            P[pp][:, th * 512:(th + 1) * 512],
                            w2buf[wpar][:, fc * 512 + dco * 128: fc * 512 + dco * 128 + 128],
                            g_t[fc][:, th * 512:(th + 1) * 512],
                            start=(fc == 0), stop=(fc == 7)),
                            reads=[f"w2b{wpar}", f"g{fc}"], writes=[f"P{pp}:{th}"])
                if e == 0:
                    b.op("vector", lambda v, dcog=dcog, pp=pp: v.tensor_copy(
                        op_t[dcog][:], P[pp][:, :]),
                        reads=[f"P{pp}:0", f"P{pp}:1"], writes=[f"op{dcog}"])
                else:
                    b.op("vector", lambda v, dcog=dcog, pp=pp: v.tensor_tensor(
                        op_t[dcog][:], P[pp][:, :], op_t[dcog][:], ALU.add),
                        reads=[f"P{pp}:0", f"P{pp}:1", f"op{dcog}"],
                        writes=[f"op{dcog}"])

    out_evs = []
    for dcog in range(16):
        par = dcog % 2
        b.op("sync", lambda s, dcog=dcog, par=par: s.dma_start(
            out=xb[par][:], in_=xin[dcog * 128:(dcog + 1) * 128, :]),
            reads=[xrg_in], writes=[f"xb{par}"], dma=f"x{par}")
        b.op("vector", lambda v, dcog=dcog, par=par: v.tensor_tensor(
            xb[par][:], xb[par][:], op_t[dcog][:], ALU.add),
            reads=[f"xb{par}", f"op{dcog}"], writes=[f"xb{par}"])
        if ctx is not None:
            b.op("scalar", lambda s, par=par: s.activation(
                out=xc[:], in_=xb[par][:], func=AF.Copy),
                reads=[f"xb{par}"], writes=["xc"])
            b.op("scalar", lambda s, par=par: s.activation(
                out=xsq[:], in_=xb[par][:], func=AF.Square),
                reads=[f"xb{par}"], writes=["xsq"])
            for th in range(2):
                b.op("tensor", lambda t, th=th, dcog=dcog: t.matmul(
                    P[3][0:1, th * 512:(th + 1) * 512],
                    ones_bf[:, 0:1], xc[:, th * 512:(th + 1) * 512],
                    start=(dcog == 0), stop=(dcog == 15), skip_group_check=True),
                    reads=["xc", "ones_bf"], writes=[f"P3:{th}"])
                b.op("tensor", lambda t, th=th, dcog=dcog: t.matmul(
                    P[1][0:1, th * 512:(th + 1) * 512],
                    ones_bf[:, 0:1], xsq[:, th * 512:(th + 1) * 512],
                    start=(dcog == 0), stop=(dcog == 15), skip_group_check=True),
                    reads=["xsq", "ones_bf"], writes=[f"P1:{th}"])
        ev = b.op("sync", lambda s, dcog=dcog, par=par: s.dma_start(
            out=xout[dcog * 128:(dcog + 1) * 128, :], in_=xb[par][:]),
            reads=[f"xb{par}"], writes=[xrg_out], dma=f"st_xo{par}")
        out_evs.append(ev)

    if ctx is None:
        b.finish(out_evs)
        b.emit(es)
        eph.close()
        es.close()
        return nc
    eph.close()
    return out_evs


def emit_ffn_scan_early(b, w1pa, w2pa, cc_in, cc_out, scanb, acc1, acc2):
    nb = len(scanb)
    for m, (pa, acc, accn) in enumerate([(w1pa, acc1, "facc1"), (w2pa, acc2, "facc2")]):
        for i in range(16):
            par = i % nb
            b.op("gpsimd", lambda s, pa=pa, i=i, par=par: s.dma_start(
                out=scanb[par][:, 0:1024], in_=pa[:, i * 1024:(i + 1) * 1024]),
                writes=[f"fscan{par}"], dma=f"fscan{par}")
            b.op("vector", lambda v, par=par, acc=acc, i=i: v.tensor_reduce(
                acc[:, i:i + 1], scanb[par][:, 0:1024], mybir.AxisListType.X, ALU.add,
                apply_absolute_value=True),
                reads=[f"fscan{par}"], writes=[accn])
        b.op("vector", lambda v, acc=acc: v.tensor_reduce(
            acc[:, 16:17], acc[:, 0:16], mybir.AxisListType.X, ALU.add),
            reads=[accn], writes=[accn])
    for m, acc, accn in [(0, acc1, "facc1"), (1, acc2, "facc2")]:
        b.op("gpsimd", lambda g, m=m, acc=acc: g.dma_start(
            out=cc_in.ap()[m:m + 1, :], in_=acc[:, 16:17]),
            reads=[accn], writes=["ccin_d"], dma="ccin")
    b.op("gpsimd", lambda g: g.collective_compute(
        "AllReduce", ALU.add, replica_groups=[list(range(8))],
        ins=[cc_in.ap()], outs=[cc_out.ap()]),
        reads=["ccin_d"], writes=["ccout_d"], cc="ccr")


def emit_ffn_scales(b, cc_out, cstbuf, srow, allred, inv1c, inv2c, P,
                    extra_reads=()):
    b.op("sync", lambda s: s.dma_start(out=allred[0:1, 0:256], in_=cc_out.ap()),
         reads=["ccout_d"], writes=["fallred"], dma="fallred")
    for m in range(2):
        b.op("vector", lambda v, m=m: v.reduce_sum(
            srow[0:1, m:m + 1], allred[0:1, m * 128:(m + 1) * 128],
            axis=mybir.AxisListType.X),
            reads=["fallred"], writes=["fsrow"])
    b.op("vector", lambda v: v.tensor_scalar(
        srow[0:1, 2:4], srow[0:1, 0:2], 1.0 / 16777216.0, 1e-8, ALU.mult, ALU.max),
        reads=["fsrow"], writes=["fsrow2"])
    b.op("vector", lambda v: v.reciprocal(srow[0:1, 2:4], srow[0:1, 2:4]),
         reads=["fsrow2"], writes=["fsrow2"])
    for m, col in [(0, inv1c), (1, inv2c)]:
        b.op("tensor", lambda t, m=m: t.matmul(
            P[3][:, 0:1], cstbuf[0:1, 8:136], srow[0:1, 2 + m:3 + m],
            start=True, stop=True),
            reads=["fsrow2", "cst"] + list(extra_reads), writes=["P3:0"])
        b.op("vector", lambda v, col=col, m=m: v.tensor_copy(col[:, 0:1], P[3][:, 0:1]),
             reads=["P3:0"], writes=[f"finv{m + 1}c"])


def build_ffn(ctx=None):
    import contextlib
    if ctx is None:
        nc = bass.Bass("TRN2", target_bir_lowering=False, debug=False)
        es = contextlib.ExitStack()
        b = None
        pref = ""
        dram = None
        xrg_in, xrg_out = "xin_d", "xout_d"
    else:
        nc, b, pref, dram = ctx["nc"], ctx["b"], ctx["pref"], ctx["dram"]
        xrg_in, xrg_out = ctx["x_in_rg"], ctx["x_out_rg"]

    def DR(name, shape, dt, kind):
        if dram is not None:
            return dram[name]
        return nc.dram_tensor(name, shape, dt, kind=kind).ap()

    xin = DR("xin", [D, TOWN], F32, "ExternalInput")
    w1 = DR("w1", [16, 16, 128, 512], F32, "ExternalInput")
    w2 = DR("w2", [4, 64, 128, 512], F32, "ExternalInput")
    w1pa = DR("w1pa", [128, 16384], F32, "ExternalInput")
    w2pa = DR("w2pa", [128, 16384], F32, "ExternalInput")
    b1c = DR("b1c", [128, 64], F32, "ExternalInput")
    b2c = DR("b2c", [128, 16], F32, "ExternalInput")
    consts = DR("consts", [128, 144], F32, "ExternalInput")
    xout = DR("xout", [D, TOWN], F32, "ExternalOutput")
    if dram is not None:
        h_dram, cc_in, cc_out = dram["h_dram"], dram["cc_in"], dram["cc_out"]
    else:
        h_dram = nc.dram_tensor("h_dram", [64, 128, TOWN], BF16)
        cc_in = nc.dram_tensor("cc_in", [2, 128], F32)
        cc_out = nc.dram_tensor("cc_out", [2, 128], F32)

    eph = contextlib.ExitStack()
    sb = lambda name, shape, dt: eph.enter_context(
        nc.sbuf_tensor(pref + name, shape, dt))
    xn_t = [sb(f"xn{i}", [128, TOWN], BF16) for i in range(16)]
    xb = [sb(f"xb{i}", [128, TOWN], F32) for i in range(2)]
    xo_sb = [sb(f"xo{i}", [128, TOWN], F32) for i in range(2)]
    xc = sb("lxc", [128, TOWN], BF16)
    xsq = sb("lxsq", [128, TOWN], BF16)
    rows = sb("rows", [1, 2048], F32)
    A_sb = sb("lnA", [128, TOWN], F32)
    B_sb = sb("lnB", [128, TOWN], F32)
    T_sb = sb("lnT", [128, TOWN], F32)
    cstbuf = sb("cstbuf", [128, 144], F32)
    ones_bf = sb("ones_bf", [128, 128], BF16)
    acc1 = sb("acc1", [128, 17], F32)
    acc2 = sb("acc2", [128, 17], F32)
    srow = sb("srow", [1, 4], F32)
    allred = sb("allred", [1, 256], F32)
    inv1c = sb("inv1c", [128, 1], F32)
    inv2c = sb("inv2c", [128, 1], F32)
    w1f = [sb(f"w1f{i}", [128, 4096], F32) for i in range(2)]
    q2h = [sb(f"q2h{i}", [128, 4096], BF16) for i in range(4)]
    scanb = w1f
    w2f = w1f
    q2b = q2h
    btmp = sb("btmp", [128, 4096], BF16)
    h2_sb = [sb(f"h2{i}", [128, TOWN], BF16) for i in range(2)]
    h_in = [sb(f"hin{i}", [128, TOWN], BF16) for i in range(8)]
    b1_sb = sb("b1_sb", [128, 64], F32)
    b2_sb = sb("b2_sb", [128, 16], F32)

    if ctx is None:
        P = [nc.alloc_psum_tensor(f"P{i}", [128, TOWN], F32) for i in range(4)]
        b = Builder(nc)
    else:
        P = ctx["P"]
        b.barrier()
    b.once_slot("cst", "ones_bf", "b1", "b2")
    b.op("sync", lambda s: s.dma_start(out=cstbuf[:], in_=consts),
         writes=["cst"], dma="cst")
    b.op("gpsimd", lambda g: g.dma_start(out=ones_bf[:], in_=consts[:, 8:136]),
         writes=["ones_bf"], dma="ones_bf")
    b.op("sync", lambda s: s.dma_start(out=b1_sb[:], in_=b1c),
         writes=["b1"], dma="b1")
    b.op("sync", lambda s: s.dma_start(out=b2_sb[:], in_=b2c),
         writes=["b2"], dma="b2")

    stats = ctx.get("stats") if ctx is not None else None
    mean = rows[0:1, 0:1024]
    var = rows[0:1, 1024:2048]
    if ctx is None or "scales_done" not in ctx:
        emit_ffn_scan_early(b, w1pa, w2pa, cc_in, cc_out, scanb, acc1, acc2)
    if stats is not None:
        for th in range(2):
            ap, rg = stats["sx"][th]
            b.op("vector", lambda v, th=th, ap=ap: v.tensor_scalar(
                mean[0:1, th * 512:(th + 1) * 512], ap,
                1.0 / 2048.0, None, ALU.mult),
                reads=[rg], writes=["mean"])
        b.op("vector", lambda v: v.tensor_tensor(var, mean, mean, ALU.mult),
             reads=["mean"], writes=["var"])
        for th in range(2):
            ap, rg = stats["sq"][th]
            b.op("vector", lambda v, th=th, ap=ap: v.scalar_tensor_tensor(
                var[0:1, th * 512:(th + 1) * 512], ap,
                1.0 / 2048.0, var[0:1, th * 512:(th + 1) * 512], ALU.mult, ALU.subtract),
                reads=[rg, "var"], writes=["var"])
    emit_ffn_scales(b, cc_out, cstbuf, srow, allred, inv1c, inv2c, P,
                    extra_reads=([stats["sx"][0][1], stats["sx"][1][1]]
                                 if stats is not None else ()))

    if stats is not None:
        pass
    else:
        for dc in range(16):
            par = dc % 2
            b.op("sync", lambda s, dc=dc, par=par: s.dma_start(
                out=xb[par][:], in_=xin[dc * 128:(dc + 1) * 128, :]),
                reads=[xrg_in], writes=[f"xb{par}"], dma=f"x{par}")
            b.op("scalar", lambda s, par=par: s.activation(
                out=xc[:], in_=xb[par][:], func=AF.Copy),
                reads=[f"xb{par}"], writes=["xc"])
            b.op("scalar", lambda s, par=par: s.activation(
                out=xsq[:], in_=xb[par][:], func=AF.Square),
                reads=[f"xb{par}"], writes=["xsq"])
            for th in range(2):
                b.op("tensor", lambda t, dc=dc, th=th: t.matmul(
                    P[0][0:1, th * 512:(th + 1) * 512],
                    ones_bf[:, 0:1], xc[:, th * 512:(th + 1) * 512],
                    start=(dc == 0), stop=(dc == 15), skip_group_check=True),
                    reads=["xc", "ones_bf"], writes=[f"P0:{th}"])
                b.op("tensor", lambda t, dc=dc, th=th: t.matmul(
                    P[1][0:1, th * 512:(th + 1) * 512],
                    ones_bf[:, 0:1], xsq[:, th * 512:(th + 1) * 512],
                    start=(dc == 0), stop=(dc == 15), skip_group_check=True),
                    reads=["xsq", "ones_bf"], writes=[f"P1:{th}"])
        for th in range(2):
            b.op("vector", lambda v, th=th: v.tensor_scalar(
                mean[0:1, th * 512:(th + 1) * 512], P[0][0:1, th * 512:(th + 1) * 512],
                1.0 / 2048.0, None, ALU.mult),
                reads=[f"P0:{th}"], writes=["mean"])
        b.op("vector", lambda v: v.tensor_tensor(var, mean, mean, ALU.mult),
             reads=["mean"], writes=["var"])
        for th in range(2):
            b.op("vector", lambda v, th=th: v.scalar_tensor_tensor(
                var[0:1, th * 512:(th + 1) * 512], P[1][0:1, th * 512:(th + 1) * 512],
                1.0 / 2048.0, var[0:1, th * 512:(th + 1) * 512], ALU.mult, ALU.subtract),
                reads=[f"P1:{th}", "var"], writes=["var"])
    b.op("scalar", lambda s: s.activation(
        out=var, in_=var, func=AF.Ln, bias=cstbuf[0:1, 1:2]),
        reads=["var", "cst"], writes=["var"])
    b.op("scalar", lambda s: s.activation(
        out=var, in_=var, func=AF.Exp, scale=-0.5),
        reads=["var"], writes=["var"])
    for th in range(2):
        b.op("tensor", lambda t, th=th: t.matmul(
            P[0][:, th * 512:(th + 1) * 512], cstbuf[0:1, 8:136],
            var[0:1, th * 512:(th + 1) * 512], start=True, stop=True),
            reads=["var", "cst"], writes=[f"P0:{th}"])
        b.op("vector", lambda v, th=th: v.tensor_copy(
            A_sb[:, th * 512:(th + 1) * 512], P[0][:, th * 512:(th + 1) * 512]),
            reads=[f"P0:{th}"], writes=["lnA"])
        b.op("tensor", lambda t, th=th: t.matmul(
            P[1][:, th * 512:(th + 1) * 512], cstbuf[0:1, 8:136],
            mean[0:1, th * 512:(th + 1) * 512], start=True, stop=True),
            reads=["mean", "cst"], writes=[f"P1:{th}"])
        b.op("vector", lambda v, th=th: v.tensor_scalar(
            B_sb[:, th * 512:(th + 1) * 512], P[1][:, th * 512:(th + 1) * 512],
            -1.0, None, ALU.mult),
            reads=[f"P1:{th}"], writes=["lnB"])
    for dc in range(16):
        par = dc % 2
        b.op("sync", lambda s, dc=dc, par=par: s.dma_start(
            out=xb[par][:], in_=xin[dc * 128:(dc + 1) * 128, :]),
            reads=[xrg_in], writes=[f"xb{par}"], dma=f"x{par}")
        b.op("vector", lambda v, par=par: v.tensor_tensor(
            T_sb[:], xb[par][:], A_sb[:], ALU.mult),
            reads=[f"xb{par}", "lnA"], writes=["lnT"])
        b.op("vector", lambda v, dc=dc: v.tensor_tensor(
            xn_t[dc][:], T_sb[:], B_sb[:], ALU.add),
            reads=["lnT", "lnB"], writes=[f"xn{dc}"])

    def quant(src_buf, src_rg, dst_buf, dst_rg, invcol, invrg):
        b.op("scalar", lambda s: s.activation(
            out=dst_buf[:], in_=src_buf[:], func=AF.Sign,
            scale=invcol[:, 0:1], bias=cstbuf[:, 4:5]),
            reads=[src_rg, invrg, "cst"], writes=[dst_rg])
        b.op("scalar", lambda s: s.activation(
            out=btmp[:], in_=src_buf[:], func=AF.Sign,
            scale=invcol[:, 0:1], bias=cstbuf[:, 3:4]),
            reads=[src_rg, invrg, "cst"], writes=["btmp"])
        b.op("vector", lambda v: v.tensor_tensor(
            dst_buf[:], dst_buf[:], btmp[:], ALU.add),
            reads=[dst_rg, "btmp"], writes=[dst_rg])

    for fg in range(16):
        qb = (fg % 2) * 2
        for half in range(2):
            par = half
            b.op("gpsimd", lambda s, fg=fg, half=half, par=par: s.dma_start(
                out=w1f[par][:].rearrange("p (c f) -> p c f", c=8),
                in_=w1[fg, half * 8:(half + 1) * 8].rearrange("c p f -> p c f")),
                writes=[f"w1f{par}"], dma=f"w1f{par}")
            quant(w1f[par], f"w1f{par}", q2h[qb + par], f"q2h{qb + par}", inv1c, "finv1c")
        for fsub in range(4):
            fc = fg * 4 + fsub
            pp = fsub % 2
            for th in range(2):
                for dc in range(16):
                    b.op("tensor", lambda t, dc=dc, th=th, fsub=fsub, pp=pp, qb=qb: t.matmul(
                        P[pp][:, th * 512:(th + 1) * 512],
                        q2h[qb + dc // 8][:, (dc % 8) * 512 + fsub * 128: (dc % 8) * 512 + fsub * 128 + 128],
                        xn_t[dc][:, th * 512:(th + 1) * 512],
                        start=(dc == 0), stop=(dc == 15)),
                        reads=[f"q2h{qb + dc // 8}", f"xn{dc}"], writes=[f"P{pp}:{th}"])
            hp = fc % 2
            b.op("scalar", lambda s, fc=fc, pp=pp, hp=hp: s.activation(
                out=h2_sb[hp][:], in_=P[pp][:, :], func=AF.Gelu,
                scale=0.5, bias=b1_sb[:, fc:fc + 1]),
                reads=[f"P{pp}:0", f"P{pp}:1", "b1"], writes=[f"h2{hp}"])
            b.op("sync", lambda s, fc=fc, hp=hp: s.dma_start(
                out=h_dram[fc], in_=h2_sb[hp][:]),
                reads=[f"h2{hp}"], writes=[f"hd{fc}"], dma=f"st_h{hp}")

    out_evs = []
    for S in range(4):
        for f8 in range(8):
            par = f8 % 2
            qp = f8 % 4
            b.op("gpsimd", lambda s, S=S, f8=f8, par=par: s.dma_start(
                out=w2f[par][:].rearrange("p (c f) -> p c f", c=8),
                in_=w2[S, f8 * 8:(f8 + 1) * 8].rearrange("c p f -> p c f")),
                writes=[f"w1f{par}"], dma=f"w1f{par}")
            quant(w2f[par], f"w1f{par}", q2b[qp], f"q2h{qp}", inv2c, "finv2c")
            if f8 < 7:
                for fc_l in range(8):
                    fcg = f8 * 8 + fc_l
                    hp = fcg % 8
                    b.op("sync", lambda s, fcg=fcg, hp=hp: s.dma_start(
                        out=h_in[hp][:], in_=h_dram[fcg]),
                        reads=[f"hd{fcg}"], writes=[f"hin{hp}"], dma=f"hi{hp}")
                    for dco in range(4):
                        for th in range(2):
                            b.op("tensor", lambda t, fc_l=fc_l, dco=dco, th=th, qp=qp, hp=hp, fcg=fcg: t.matmul(
                                P[dco][:, th * 512:(th + 1) * 512],
                                q2b[qp][:, fc_l * 512 + dco * 128: fc_l * 512 + dco * 128 + 128],
                                h_in[hp][:, th * 512:(th + 1) * 512],
                                start=(fcg == 0), stop=False),
                                reads=[f"q2h{qp}", f"hin{hp}"], writes=[f"P{dco}:{th}"])
            else:
                for fc_l in range(8):
                    fcg = f8 * 8 + fc_l
                    hp = fcg % 8
                    b.op("sync", lambda s, fcg=fcg, hp=hp: s.dma_start(
                        out=h_in[hp][:], in_=h_dram[fcg]),
                        reads=[f"hd{fcg}"], writes=[f"hin{hp}"], dma=f"hi{hp}")
                for dco in range(4):
                    for fc_l in range(8):
                        fcg = f8 * 8 + fc_l
                        hp = fcg % 8
                        for th in range(2):
                            b.op("tensor", lambda t, fc_l=fc_l, dco=dco, th=th, qp=qp, hp=hp: t.matmul(
                                P[dco][:, th * 512:(th + 1) * 512],
                                q2b[qp][:, fc_l * 512 + dco * 128: fc_l * 512 + dco * 128 + 128],
                                h_in[hp][:, th * 512:(th + 1) * 512],
                                start=False, stop=(fc_l == 7)),
                                reads=[f"q2h{qp}", f"hin{hp}"], writes=[f"P{dco}:{th}"])
        for dco in range(4):
            dcog = S * 4 + dco
            par = dco % 2
            b.op("sync", lambda s, dcog=dcog, par=par: s.dma_start(
                out=xb[par][:], in_=xin[dcog * 128:(dcog + 1) * 128, :]),
                reads=[xrg_in], writes=[f"xb{par}"], dma=f"x{par}")
            b.op("vector", lambda v, dcog=dcog, par=par: v.tensor_scalar(
                xb[par][:], xb[par][:], b2_sb[:, dcog:dcog + 1], None, ALU.add),
                reads=[f"xb{par}", "b2"], writes=[f"xb{par}"])
            b.op("vector", lambda v, dco=dco, par=par: v.scalar_tensor_tensor(
                xo_sb[par][:], P[dco][:, :], 0.5, xb[par][:], ALU.mult, ALU.add),
                reads=[f"P{dco}:0", f"P{dco}:1", f"xb{par}"], writes=[f"xo{par}"])
            ev = b.op("sync", lambda s, dcog=dcog, par=par: s.dma_start(
                out=xout[dcog * 128:(dcog + 1) * 128, :], in_=xo_sb[par][:]),
                reads=[f"xo{par}"], writes=[xrg_out], dma=f"st_xo{par}")
            out_evs.append(ev)

    if ctx is None:
        b.finish(out_evs)
        b.emit(es)
        eph.close()
        es.close()
        return nc
    eph.close()
    return out_evs


def build_mega():
    import contextlib
    nc = bass.Bass("TRN2", target_bir_lowering=False, debug=False)
    es = contextlib.ExitStack()

    def EI(name, shape, dt):
        return nc.dram_tensor(name, shape, dt, kind="ExternalInput").ap()

    xin0 = EI("xin0", [D, TOWN], F32)
    consts = EI("consts", [128, 144], F32)
    ident = EI("ident", [128, 128], F32)
    l0 = {n: EI(f"l0_{n}", [4, 16, 128, 512], F32) for n in ("wq", "wk", "wv", "wo")}
    bcols = EI("bcols", [128, 48], F32)
    bvrow = EI("bvrow", [1, D], F32)
    bocols = EI("bocols", [128, 16], F32)
    wg_m = EI("wg_m", [16, 128, 8], F32)
    mbias = EI("mbias", [8, 1], F32)
    w1m = EI("w1m", [8, 4, 16, 128, 256], F32)
    w3m = EI("w3m", [8, 4, 16, 128, 256], F32)
    w2m = EI("w2m", [8, 4, 8, 128, 512], F32)
    l1 = {n: EI(f"l1_{n}", [4, 16, 128, 512], F32) for n in ("wq", "wk", "wv", "wg", "wo")}
    cosin = EI("cosin", [128, 2 * TOWN], F32)
    protm = EI("protm", [128, 128], F32)
    normw = EI("normw", [128, 1], F32)
    maskA = EI("maskA", [128, 128], F32)
    maskB = EI("maskB", [128, 128], F32)
    w1f_ = EI("w1f_", [16, 16, 128, 512], F32)
    w2f_ = EI("w2f_", [4, 64, 128, 512], F32)
    w1pa = EI("w1pa", [128, 16384], F32)
    w2pa = EI("w2pa", [128, 16384], F32)
    b1c = EI("b1c", [128, 64], F32)
    b2c = EI("b2c", [128, 16], F32)

    xout_ext = nc.dram_tensor("xout_ext", [D, TOWN], F32, kind="ExternalOutput").ap()

    xa = nc.dram_tensor("xa", [D, TOWN], F32).ap()
    xbd = nc.dram_tensor("xbd", [D, TOWN], F32).ap()
    q_st = nc.dram_tensor("q_st", [NH, 128, TOWN], BF16).ap()
    g_st = nc.dram_tensor("g_st", [NH, 128, TOWN], BF16).ap()
    k_own = nc.dram_tensor("k_own", [2048, TOWN], BF16)
    v_own = nc.dram_tensor("v_own", [2048, TOWN], BF16)
    k_gth = nc.dram_tensor("k_gth", [4096, TOWN], BF16)
    v_gth = nc.dram_tensor("v_gth", [4096, TOWN], BF16)
    h_dram = nc.dram_tensor("h_dram", [64, 128, TOWN], BF16)
    cc_in = nc.dram_tensor("cc_in", [2, 128], F32)
    cc_out = nc.dram_tensor("cc_out", [2, 128], F32)

    kv_k = k_own.ap().rearrange("(h p) t -> h p t", h=16)
    kv_v = v_own.ap().rearrange("(a b) c -> a (b c)", b=2)

    P = [nc.alloc_psum_tensor(f"P{i}", [128, TOWN], F32) for i in range(4)]
    b = Builder(nc)

    m_scan = [nc.alloc_sbuf_tensor(f"m_scan{i}", [128, 1024], F32) for i in range(2)]
    m_acc1 = nc.alloc_sbuf_tensor("m_acc1", [128, 17], F32)
    m_acc2 = nc.alloc_sbuf_tensor("m_acc2", [128, 17], F32)

    out_evs = None
    xseq = {0: (xin0, "x_ext", xa, "xa_d"),
            1: (xbd, "xbd_d", xa, "xa_d")}
    attn_stats = {"sx": [(P[3][0:1, 0:512], "R0"), (P[3][0:1, 512:1024], "R1")],
                  "sq": [(P[1][0:1, 0:512], "SC1:0"), (P[1][0:1, 512:1024], "SC1:1")]}
    moe_stats = {"sx": [(P[3][0:1, 0:512], "P3:0"), (P[3][0:1, 512:1024], "P3:1")],
                 "sq": [(P[1][0:1, 0:512], "P1:0"), (P[1][0:1, 512:1024], "P1:1")]}
    for layer in (0, 1):
        x_in, x_in_rg, x_mid, x_mid_rg = xseq[layer]
        dram = {"xin": x_in, "consts": consts,
                "q_out": q_st, "k_out": kv_k, "v_out": kv_v}
        if layer == 0:
            dram.update(wq=l0["wq"], wk=l0["wk"], wv=l0["wv"],
                        bcols=bcols, bvrow=bvrow)
        else:
            dram.update(wq=l1["wq"], wk=l1["wk"], wv=l1["wv"], wg=l1["wg"],
                        cosin=cosin, protm=protm, g_out=g_st)
        pctx = dict(nc=nc, b=b, P=P, pref=f"p{layer}_",
                    dram=dram, x_in_rg=x_in_rg,
                    x_out_rg="unused", kv_gath=None,
                    k_own=k_own.ap(), v_own=v_own.ap(),
                    k_gth=k_gth.ap(), v_gth=v_gth.ap())
        if layer == 1:
            pctx["stats"] = moe_stats
        build_proj(layer, 0, ctx=pctx)

        dram = {"xin": x_in, "q_in": q_st, "consts": consts,
                "wo": l0["wo"] if layer == 0 else l1["wo"],
                "xout": x_mid}
        if layer == 0:
            dram["bocols"] = bocols
        else:
            dram.update(g_in=g_st, normw=normw, maskA=maskA, maskB=maskB)
        if layer == 0:
            emit_ffn_scan_early(b, w1pa, w2pa, cc_in, cc_out, m_scan,
                                m_acc1, m_acc2)
        build_attn(layer, ctx=dict(nc=nc, b=b, P=P, pref=f"a{layer}_",
                                   dram=dram, x_in_rg=x_in_rg,
                                   x_out_rg=x_mid_rg,
                                   kv_gath=(k_gth.ap(), v_gth.ap())))
        if layer == 0:
            dram = {"xin": x_mid, "wg": wg_m, "mbias": mbias, "w1": w1m,
                    "w3": w3m, "w2": w2m, "ident": ident, "consts": consts,
                    "xout": xbd}
            build_moe(ctx=dict(nc=nc, b=b, P=P, pref="m_", dram=dram,
                               x_in_rg=x_mid_rg, x_out_rg="xbd_d",
                               stats=attn_stats))
        else:
            dram = {"xin": x_mid, "w1": w1f_, "w2": w2f_, "w1pa": w1pa,
                    "w2pa": w2pa, "b1c": b1c, "b2c": b2c, "consts": consts,
                    "xout": xout_ext, "h_dram": h_dram, "cc_in": cc_in,
                    "cc_out": cc_out}
            out_evs = build_ffn(ctx=dict(nc=nc, b=b, P=P, pref="f_", dram=dram,
                                         x_in_rg=x_mid_rg, x_out_rg="xfin_d",
                                         scales_done=True, stats=attn_stats))

    b.finish(out_evs)
    b.emit(es)
    es.close()
    return nc


def _own_token_idx(core):
    blocks = EVEN_BLOCKS if core % 2 == 0 else ODD_BLOCKS
    return np.concatenate([np.arange(blk * 128, blk * 128 + 128) for blk in blocks])


def _consts_array():
    c = np.zeros((128, 144), dtype=np.float32)
    c[:, 0] = -CSHIFT
    c[:, 1] = 1e-5
    c[:, 2] = 1e-5
    c[:, 3] = 0.5
    c[:, 4] = -0.5
    c[:, 8:136] = 1.0
    return c


def _block_w(wT):
    return np.ascontiguousarray(
        wT.reshape(16, 128, 4, 512).transpose(2, 0, 1, 3))


def _rope_tables(core):
    idx = _own_token_idx(core).astype(np.float32)
    inv_freq = 1.0 / (10000.0 ** (np.arange(0, HD, 2, dtype=np.float32) / HD))
    freqs = np.outer(idx, inv_freq)
    emb = np.concatenate([freqs, freqs], axis=1)
    cos = np.cos(emb).T.astype(np.float32)
    sin = np.sin(emb).T.astype(np.float32)
    return np.ascontiguousarray(np.concatenate([cos, sin], axis=1))


def _rot_matrix():
    P = np.zeros((128, 128), dtype=np.float32)
    for m in range(64):
        P[m + 64, m] = -1.0
    for m in range(64, 128):
        P[m - 64, m] = 1.0
    return P


def _prep_shared(inp):
    d = {}
    d["l0_wq"] = _block_w(np.ascontiguousarray(inp["L0_attn_Wq"].T))
    d["l0_wk"] = _block_w(np.ascontiguousarray(inp["L0_attn_Wk"].T))
    d["l0_wv"] = _block_w(np.ascontiguousarray(inp["L0_attn_Wv"].T))
    d["l0_wo"] = _block_w(np.ascontiguousarray(inp["L0_attn_Wo"].T))
    bcols = np.zeros((128, 48), np.float32)
    bcols[:, 0:16] = inp["L0_attn_bq"].reshape(16, 128).T
    bcols[:, 16:32] = inp["L0_attn_bk"].reshape(16, 128).T
    d["l0_bcols"] = bcols
    d["l0_bvrow"] = np.ascontiguousarray(inp["L0_attn_bv"].reshape(1, -1))
    d["l0_bocols"] = np.ascontiguousarray(inp["L0_attn_bo"].reshape(16, 128).T)
    d["wg"] = np.ascontiguousarray(inp["L0_ffn_Wg"].T.reshape(16, 128, 8))
    d["mbias"] = np.ascontiguousarray(inp["L0_ffn_bias"].reshape(8, 1))
    d["w1"] = np.ascontiguousarray(inp["L0_ffn_W1"].transpose(0, 2, 1).reshape(8, 16, 128, 4, 256).transpose(0, 3, 1, 2, 4))
    d["w3"] = np.ascontiguousarray(inp["L0_ffn_W3"].transpose(0, 2, 1).reshape(8, 16, 128, 4, 256).transpose(0, 3, 1, 2, 4))
    d["w2"] = np.ascontiguousarray(inp["L0_ffn_W2"].transpose(0, 2, 1).reshape(8, 8, 128, 4, 512).transpose(0, 3, 1, 2, 4))
    d["l1_wq"] = _block_w(np.ascontiguousarray(inp["L1_attn_wq"].T))
    d["l1_wk"] = _block_w(np.ascontiguousarray(inp["L1_attn_wk"].T))
    d["l1_wv"] = _block_w(np.ascontiguousarray(inp["L1_attn_wv"].T))
    d["l1_wg"] = _block_w(np.ascontiguousarray(inp["L1_attn_wg"].T))
    d["l1_wo"] = _block_w(np.ascontiguousarray(inp["L1_attn_wo"].T))
    d["normw"] = np.ascontiguousarray(inp["L1_attn_norm_w"].reshape(128, 1))
    fw1 = inp["L1_ffn_w1"]
    fw2 = inp["L1_ffn_w2"]
    d["f_w1"] = np.ascontiguousarray(fw1.T.reshape(16, 128, 16, 512).transpose(2, 0, 1, 3))
    d["f_w2"] = np.ascontiguousarray(fw2.T.reshape(64, 128, 4, 512).transpose(2, 0, 1, 3))
    n = 16777216 // 8
    d["f_w1pa"] = [np.ascontiguousarray(fw1.reshape(-1)[c * n:(c + 1) * n].reshape(128, -1)) for c in range(8)]
    d["f_w2pa"] = [np.ascontiguousarray(fw2.reshape(-1)[c * n:(c + 1) * n].reshape(128, -1)) for c in range(8)]
    d["f_b1c"] = np.ascontiguousarray(inp["L1_ffn_b1"].reshape(64, 128).T)
    d["f_b2c"] = np.ascontiguousarray(inp["L1_ffn_b2"].reshape(16, 128).T)
    d["consts"] = _consts_array()
    d["ident"] = np.eye(128, dtype=np.float32)
    tri = np.where(np.arange(128)[:, None] > np.arange(128)[None, :],
                   np.float32(-1e30), np.float32(0.0))
    d["tri"] = tri
    d["fullmask"] = np.full((128, 128), -1e30, np.float32)
    d["zeromask"] = np.zeros((128, 128), np.float32)
    return d


_NC_CACHE = {}


def _get_nc(name, fn):
    if name not in _NC_CACHE:
        _NC_CACHE[name] = fn()
    return _NC_CACHE[name]


FUSED = True


def kernel(**inputs):
    if FUSED:
        return kernel_fused(**inputs)
    return kernel_unfused(**inputs)


def kernel_fused(**inputs):
    P = _prep_shared(inputs)
    nc = _get_nc("mega", build_mega)
    in_maps = []
    for c in range(8):
        im = {
            "xin0": np.ascontiguousarray(
                inputs["L0_attn_x"][c // 2][_own_token_idx(c)].T),
            "consts": P["consts"], "ident": P["ident"],
            "l0_wq": P["l0_wq"], "l0_wk": P["l0_wk"], "l0_wv": P["l0_wv"],
            "l0_wo": P["l0_wo"], "bcols": P["l0_bcols"], "bvrow": P["l0_bvrow"],
            "bocols": P["l0_bocols"],
            "wg_m": P["wg"], "mbias": P["mbias"], "w1m": P["w1"],
            "w3m": P["w3"], "w2m": P["w2"],
            "l1_wq": P["l1_wq"], "l1_wk": P["l1_wk"], "l1_wv": P["l1_wv"],
            "l1_wg": P["l1_wg"], "l1_wo": P["l1_wo"],
            "cosin": _rope_tables(c), "protm": _rot_matrix(),
            "normw": P["normw"],
            "maskA": P["tri"] if c % 2 == 0 else P["zeromask"],
            "maskB": P["fullmask"] if c % 2 == 0 else P["tri"],
            "w1f_": P["f_w1"], "w2f_": P["f_w2"],
            "w1pa": P["f_w1pa"][c], "w2pa": P["f_w2pa"][c],
            "b1c": P["f_b1c"], "b2c": P["f_b2c"],
        }
        in_maps.append(im)
    res = run_bass_kernel_spmd(nc, in_maps, core_ids=list(range(8)))
    out = np.empty((4, TFULL, D), dtype=np.float32)
    for c in range(8):
        out[c // 2, _own_token_idx(c), :] = np.asarray(res.results[c]["xout_ext"]).T
    return out


def kernel_unfused(**inputs):
    P = _prep_shared(inputs)
    x_fm = [np.ascontiguousarray(
        inputs["L0_attn_x"][c // 2][_own_token_idx(c)].T) for c in range(8)]

    def run(nc, in_maps):
        res = run_bass_kernel_spmd(nc, in_maps, core_ids=list(range(8)))
        return res.results

    for layer in (0, 1):
        nc = _get_nc(f"proj{layer}", lambda layer=layer: build_proj(layer, 0))
        ims = []
        for c in range(8):
            im = {"xin": x_fm[c], "consts": P["consts"]}
            if layer == 0:
                im.update(wq=P["l0_wq"], wk=P["l0_wk"], wv=P["l0_wv"],
                          bcols=P["l0_bcols"], bvrow=P["l0_bvrow"])
            else:
                im.update(wq=P["l1_wq"], wk=P["l1_wk"], wv=P["l1_wv"],
                          wg=P["l1_wg"], cosin=_rope_tables(c), protm=_rot_matrix())
            ims.append(im)
        r = run(nc, ims)
        kf, vf, qf, gf = {}, {}, {}, {}
        for p in range(4):
            e, o = 2 * p, 2 * p + 1
            kf[p] = np.concatenate([r[e]["k_out"], r[o]["k_out"]], axis=2)
            vf[p] = np.concatenate([r[e]["v_out"], r[o]["v_out"]], axis=0)
        for c in range(8):
            qf[c] = r[c]["q_out"]
            if layer == 1:
                gf[c] = r[c]["g_out"]
        nc = _get_nc(f"attn{layer}", lambda layer=layer: build_attn(layer))
        ims = []
        for c in range(8):
            im = {"xin": x_fm[c], "q_in": qf[c], "k_in": kf[c // 2],
                  "v_in": vf[c // 2], "consts": P["consts"],
                  "wo": P["l0_wo"] if layer == 0 else P["l1_wo"]}
            if layer == 0:
                im["bocols"] = P["l0_bocols"]
            else:
                im.update(g_in=gf[c], normw=P["normw"],
                          maskA=P["tri"] if c % 2 == 0 else P["zeromask"],
                          maskB=P["fullmask"] if c % 2 == 0 else P["tri"])
            ims.append(im)
        r = run(nc, ims)
        x_fm = [np.asarray(r[c]["xout"]) for c in range(8)]
        if layer == 0:
            nc = _get_nc("moe", build_moe)
            ims = [{"xin": x_fm[c], "wg": P["wg"], "mbias": P["mbias"],
                    "w1": P["w1"], "w3": P["w3"], "w2": P["w2"],
                    "ident": P["ident"], "consts": P["consts"]}
                   for c in range(8)]
        else:
            nc = _get_nc("ffn", build_ffn)
            ims = [{"xin": x_fm[c], "w1": P["f_w1"], "w2": P["f_w2"],
                    "w1pa": P["f_w1pa"][c], "w2pa": P["f_w2pa"][c],
                    "b1c": P["f_b1c"], "b2c": P["f_b2c"],
                    "consts": P["consts"]} for c in range(8)]
        r = run(nc, ims)
        x_fm = [np.asarray(r[c]["xout"]) for c in range(8)]

    out = np.empty((4, TFULL, D), dtype=np.float32)
    for c in range(8):
        out[c // 2, _own_token_idx(c), :] = x_fm[c].T
    return out


# version 12
# speedup vs baseline: 1.0033x; 1.0033x over previous
import math
import numpy as np
import concourse.bass as bass
import concourse.mybir as mybir
from concourse.bass_utils import run_bass_kernel_spmd

F32 = mybir.dt.float32
BF16 = mybir.dt.bfloat16
AF = mybir.ActivationFunctionType
ALU = mybir.AluOpType

D = 2048
NH = 16
HD = 128
TOWN = 1024
TFULL = 2048
NB = 8
EVEN_BLOCKS = [0, 2, 4, 6, 8, 10, 12, 14]
ODD_BLOCKS = [1, 3, 5, 7, 9, 11, 13, 15]
CSHIFT = 20.0
SCALE_ATT = 1.0 / math.sqrt(HD)

ENGINES = ("tensor", "vector", "scalar", "gpsimd", "sync")
DMA_QUEUES = ("sync", "scalar", "gpsimd")


class Builder:

    def __init__(self, nc):
        self.nc = nc
        self.items = {e: [] for e in ENGINES}
        self.last_w = {}
        self.readers = {}
        self.slot_ord = {}
        self.once = set()
        self.cc_ord = {}
        self.final_waits = []
        self.group_end = {}
        self.open_group = {}
        self.slot_last_ev = {}
        self.last_cc_ev = {}
        self.pending_bar = {}

    def once_slot(self, *names):
        self.once.update(names)

    def barrier(self):
        evs = set()
        for e in ENGINES:
            for it in reversed(self.items[e]):
                if not it["dma"] and not it["cc"]:
                    evs.add(it["ev"])
                    break
        evs.update(self.slot_last_ev.values())
        evs.update(self.last_cc_ev.values())
        for e in ENGINES:
            self.pending_bar[e] = set(evs)

    def op(self, eng, fn, reads=(), writes=(), dma=None, cc=False):
        idx = len(self.items[eng])
        if cc:
            ch = cc if isinstance(cc, str) else "cc0"
            o = self.cc_ord.get(ch, 0)
            self.cc_ord[ch] = o + 1
            ev = ("cc", ch, o)
            self.last_cc_ev[ch] = ev
        elif dma is not None:
            o = self.slot_ord.get(dma, 0)
            self.slot_ord[dma] = o + 1
            ev = ("dma", dma, o)
            self.slot_last_ev[dma] = ev
            if dma in self.once:
                self.open_group.setdefault(dma, []).append(o)
        else:
            ev = ("cmp", eng, idx)
        deps = set()
        bar = self.pending_bar.pop(eng, None)
        if bar:
            deps.update(bar)
        for r in reads:
            lw = self.last_w.get(r)
            if lw is not None:
                deps.add(lw)
            self.readers.setdefault(r, []).append(ev)
        for w in writes:
            lw = self.last_w.get(w)
            if lw is not None:
                deps.add(lw)
            for rd in self.readers.get(w, ()):
                if rd != ev:
                    deps.add(rd)
            self.readers[w] = []
            self.last_w[w] = ev
        if eng == "tensor":
            deps = {d for d in deps if not (d[0] == "cmp" and d[1] == "tensor")}
        for d in deps:
            if d[0] == "dma" and d[1] in self.once:
                grp = self.open_group.get(d[1])
                if grp:
                    end = self.slot_ord[d[1]] - 1
                    for o in grp:
                        self.group_end[(d[1], o)] = end
                    self.open_group[d[1]] = []
        self.items[eng].append(
            dict(fn=fn, deps=deps, ev=ev, dma=dma, cc=cc, inc=False))
        return ev

    def finish(self, events):
        self.final_waits.extend(events)

    def emit(self, es):
        nc = self.nc
        sem_cmp = {e: es.enter_context(nc.semaphore(f"s_{e}")) for e in ENGINES}
        sem_dma = {s: es.enter_context(nc.semaphore(f"d{i}"))
                   for i, s in enumerate(sorted(self.slot_ord))}
        sem_cc = {ch: es.enter_context(nc.semaphore(f"s_cc_{ch}"))
                  for ch in sorted(self.cc_ord)}

        need = set()
        for e in ENGINES:
            for it in self.items[e]:
                for d in it["deps"]:
                    if d[0] == "cmp":
                        need.add((d[1], d[2]))
        for d in self.final_waits:
            if d[0] == "cmp":
                need.add((d[1], d[2]))
        prefix = {}
        for e in ENGINES:
            c = 0
            vals = []
            for i, it in enumerate(self.items[e]):
                if it["dma"] is not None or it["cc"]:
                    vals.append(None)
                    continue
                if (e, i) in need:
                    it["inc"] = True
                    c += 1
                vals.append(c)
            prefix[e] = vals

        def ev_wait(ev):
            kind = ev[0]
            if kind == "cmp":
                return (sem_cmp[ev[1]], prefix[ev[1]][ev[2]])
            if kind == "dma":
                slot, o = ev[1], ev[2]
                if slot in self.once:
                    end = self.group_end.get((slot, o), o)
                    return (sem_dma[slot], 16 * (end + 1))
                return (sem_dma[slot], 16 * (o + 1))
            return (sem_cc[ev[1]], ev[2] + 1)

        def section(eng_name):
            items = self.items[eng_name]
            waited = {}

            def do_waits(eng, deps):
                ws = {}
                for d in deps:
                    sem, val = ev_wait(d)
                    key = id(sem)
                    if val > ws.get(key, (None, -1))[1]:
                        ws[key] = (sem, val)
                for sem, val in ws.values():
                    if waited.get(id(sem), -1) >= val:
                        continue
                    eng.wait_ge(sem, val)
                    waited[id(sem)] = val

            def run(eng):
                for it in items:
                    do_waits(eng, it["deps"])
                    inst = it["fn"](eng)
                    if it["cc"]:
                        ch = it["cc"] if isinstance(it["cc"], str) else "cc0"
                        inst.then_inc(sem_cc[ch], 1)
                    elif it["dma"] is not None:
                        inst.then_inc(sem_dma[it["dma"]], 16)
                    elif it["inc"]:
                        inst.then_inc(sem_cmp[eng_name], 1)
                if eng_name == "sync":
                    do_waits(eng, self.final_waits)
            return run

        with nc.Block() as block:
            block.sync(section("sync"))
            block.scalar(section("scalar"))
            block.vector(section("vector"))
            block.tensor(section("tensor"))
            block.gpsimd(section("gpsimd"))


def emit_ln(b, x_t, xn_t, cst, scr, ps, tag, pre=None):
    nc = b.nc
    xc, xsq = scr["ln_xc"], scr["ln_xsq"]
    rows = scr["rows"]
    A, Bb = scr["ln_A"], scr["ln_B"]
    R0, R1 = ps["R0"], ps["R1"]
    PB = ps["B"]
    PBname = ps["Bname"]

    mean = rows[0:1, 0:1024]
    var = rows[0:1, 1024:2048]
    if pre is None:
        for dc in range(16):
            b.op("scalar", lambda s, dc=dc: s.activation(
                out=xc[:], in_=x_t[dc][:], func=AF.Copy),
                reads=[f"x{dc}"], writes=[f"{tag}xc"])
            for th in range(2):
                b.op("tensor", lambda t, dc=dc, th=th: t.matmul(
                    (R0 if th == 0 else R1)[0:1, :],
                    cst["ones_bf"][:, 0:1], xc[:, th * 512:(th + 1) * 512],
                    start=(dc == 0), stop=(dc == 15)),
                    reads=[f"{tag}xc", "ones_bf"], writes=[f"psR{th}"])
        sx_src = [((R0 if th == 0 else R1)[0:1, :], f"psR{th}") for th in range(2)]
    else:
        sx_src = pre["sx"]
    for th in range(2):
        ap, rg = sx_src[th]
        b.op("vector", lambda v, th=th, ap=ap: v.tensor_scalar(
            mean[0:1, th * 512:(th + 1) * 512], ap,
            1.0 / 2048.0, None, ALU.mult),
            reads=[rg], writes=[f"{tag}mean"])
    if pre is None:
        for dc in range(16):
            b.op("scalar", lambda s, dc=dc: s.activation(
                out=xsq[:], in_=x_t[dc][:], func=AF.Square),
                reads=[f"x{dc}"], writes=[f"{tag}xsq"])
            for th in range(2):
                b.op("tensor", lambda t, dc=dc, th=th: t.matmul(
                    (R0 if th == 0 else R1)[0:1, :],
                    cst["ones_bf"][:, 0:1], xsq[:, th * 512:(th + 1) * 512],
                    start=(dc == 0), stop=(dc == 15)),
                    reads=[f"{tag}xsq", "ones_bf"], writes=[f"psR{th}"])
        sq_src = [((R0 if th == 0 else R1)[0:1, :], f"psR{th}") for th in range(2)]
    else:
        sq_src = pre["sq"]
    b.op("vector", lambda v: v.tensor_tensor(var, mean, mean, ALU.mult),
        reads=[f"{tag}mean"], writes=[f"{tag}var"])
    for th in range(2):
        ap, rg = sq_src[th]
        b.op("vector", lambda v, th=th, ap=ap: v.scalar_tensor_tensor(
            var[0:1, th * 512:(th + 1) * 512], ap,
            1.0 / 2048.0, var[0:1, th * 512:(th + 1) * 512],
            ALU.mult, ALU.subtract),
            reads=[rg, f"{tag}var"], writes=[f"{tag}var"])
    b.op("scalar", lambda s: s.activation(
        out=var, in_=var, func=AF.Ln, bias=cst["ccol_eps_row"]),
        reads=[f"{tag}var", "cst"], writes=[f"{tag}var"])
    b.op("scalar", lambda s: s.activation(
        out=var, in_=var, func=AF.Exp, scale=-0.5),
        reads=[f"{tag}var"], writes=[f"{tag}var"])
    b.op("vector", lambda v: v.scalar_tensor_tensor(
        mean, mean, -1.0, var, ALU.mult, ALU.mult),
        reads=[f"{tag}mean", f"{tag}var"], writes=[f"{tag}mean"])
    for th in range(2):
        b.op("tensor", lambda t, th=th: t.matmul(
            PB[:, th * 512:(th + 1) * 512], cst["ones_row"],
            var[0:1, th * 512:(th + 1) * 512], start=True, stop=True),
            reads=[f"{tag}var", "cst"], writes=[f"{PBname}:{th}"])
        b.op("vector", lambda v, th=th: v.tensor_copy(
            A[:, th * 512:(th + 1) * 512], PB[:, th * 512:(th + 1) * 512]),
            reads=[f"{PBname}:{th}"], writes=[f"{tag}A"])
        b.op("tensor", lambda t, th=th: t.matmul(
            PB[:, th * 512:(th + 1) * 512], cst["ones_row"],
            mean[0:1, th * 512:(th + 1) * 512], start=True, stop=True),
            reads=[f"{tag}mean"], writes=[f"{PBname}:{th}"])
        b.op("vector", lambda v, th=th: v.tensor_copy(
            Bb[:, th * 512:(th + 1) * 512], PB[:, th * 512:(th + 1) * 512]),
            reads=[f"{PBname}:{th}"], writes=[f"{tag}B"])
    for dc in range(16):
        b.op("vector", lambda v, dc=dc: v.tensor_tensor(
            scr["ln_t"][:], x_t[dc][:], A[:], ALU.mult),
            reads=[f"x{dc}", f"{tag}A"], writes=[f"{tag}t"])
        b.op("vector", lambda v, dc=dc: v.tensor_tensor(
            xn_t[dc][:], scr["ln_t"][:], Bb[:], ALU.add),
            reads=[f"{tag}t", f"{tag}B"], writes=[f"xn{dc}"])


def emit_load_x(b, xin, x_t, xrg="xin_d"):
    for dc in range(16):
        b.op("sync", lambda s, dc=dc: s.dma_start(
            out=x_t[dc][:], in_=xin[dc * 128:(dc + 1) * 128, :]),
            reads=[xrg], writes=[f"x{dc}"], dma=f"x{dc}")


def build_proj(layer, core, ctx=None):
    import contextlib
    if ctx is None:
        nc = bass.Bass("TRN2", target_bir_lowering=False, debug=False)
        es = contextlib.ExitStack()
        b = None
        pref = ""
        dram = None
        xrg_in, xrg_out = "xin_d", "xout_d"
    else:
        nc, b, pref, dram = ctx["nc"], ctx["b"], ctx["pref"], ctx["dram"]
        xrg_in, xrg_out = ctx["x_in_rg"], ctx["x_out_rg"]

    def DR(name, shape, dt, kind):
        if dram is not None:
            return dram[name]
        return nc.dram_tensor(name, shape, dt, kind=kind).ap()

    xin = DR("xin", [D, TOWN], F32, "ExternalInput")
    names = ["wq", "wk", "wv"] + (["wg"] if layer == 1 else [])
    wts = {n: DR(n, [4, 16, 128, 512], F32, "ExternalInput") for n in names}
    if layer == 0:
        bcols = DR("bcols", [128, 48], F32, "ExternalInput")
        bvrow = DR("bvrow", [1, D], F32, "ExternalInput")
    if layer == 1:
        cosin = DR("cosin", [128, 2 * TOWN], F32, "ExternalInput")
        protm = DR("protm", [128, 128], F32, "ExternalInput")
    consts = DR("consts", [128, 144], F32, "ExternalInput")

    q_out = DR("q_out", [NH, 128, TOWN], BF16, "ExternalOutput")
    k_out = DR("k_out", [NH, 128, TOWN], BF16, "ExternalOutput")
    v_out = DR("v_out", [TOWN, D], BF16, "ExternalOutput")
    if layer == 1:
        g_out = DR("g_out", [NH, 128, TOWN], BF16, "ExternalOutput")

    eph = contextlib.ExitStack()
    sb = lambda name, shape, dt: eph.enter_context(
        nc.sbuf_tensor(pref + name, shape, dt))
    x_t = [sb(f"x{i}", [128, TOWN], F32) for i in range(16)]
    xn_t = [sb(f"xn{i}", [128, TOWN], BF16) for i in range(16)]
    wbuf = [sb(f"wbuf{i}", [128, 8192], BF16) for i in range(2)]
    scr = {
        "ln_xc": sb("ln_xc", [128, TOWN], BF16),
        "ln_xsq": sb("ln_xsq", [128, TOWN], BF16),
        "rows": sb("rows", [1, 2048], F32),
        "ln_A": sb("ln_A", [128, TOWN], F32),
        "ln_B": sb("ln_B", [128, TOWN], F32),
        "ln_t": sb("ln_t", [128, TOWN], F32),
    }
    cstbuf = sb("cstbuf", [128, 144], F32)
    ones_bf = sb("ones_bf", [128, 128], BF16)
    qk_sb2 = [sb(f"qk_sb{i}", [128, TOWN], BF16) for i in range(2)]
    v_sb = [sb(f"v_sb{i}", [128, 512], BF16) for i in range(2)]
    if layer == 0:
        bcol_sb = sb("bcol_sb", [128, 48], F32)
        bvrow_sb = sb("bvrow_sb", [1, D], BF16)
        bvb_sb = sb("bvb_sb", [128, D], BF16)
    if layer == 1:
        cs_sb = sb("cs_sb", [128, 2 * TOWN], F32)
        pro_sb = sb("pro_sb", [128, 128], BF16)
        qbf_sb2 = [scr["ln_xc"], sb("qbf1", [128, TOWN], BF16)]
        r1_sb2 = [scr["ln_A"], sb("r1b", [128, TOWN], F32)]
        r2_sb2 = [scr["ln_B"], sb("r2b", [128, TOWN], F32)]
        qbn2 = ["lnxc", "qbf1"]
        r1n2 = ["lnA", "r1b"]
        r2n2 = ["lnB", "r2b"]

    if ctx is None:
        PP = [nc.alloc_psum_tensor(f"P{i}", [128, TOWN], F32) for i in range(4)]
        b = Builder(nc)
    else:
        PP = ctx["P"]
        b.barrier()
    psA, psB, psC = PP[0], PP[1], PP[2]
    psR0 = PP[3][0:1, 0:512]
    psR1 = PP[3][0:1, 512:1024]
    ps = {"R0": psR0, "R1": psR1, "B": psC, "Bname": "psC"}
    cst = {"ones_bf": ones_bf, "ones_row": cstbuf[0:1, 8:136],
           "ccol_eps_row": cstbuf[0:1, 1:2], "negC_col": cstbuf[:, 0:1]}

    b.once_slot("cst", "ones_bf", "bcol", "bvrow", "cs", "pro")
    b.op("sync", lambda s: s.dma_start(out=cstbuf[:], in_=consts),
         writes=["cst"], dma="cst")
    b.op("gpsimd", lambda g: g.dma_start(out=ones_bf[:], in_=consts[:, 8:136]),
         writes=["ones_bf"], dma="ones_bf")
    emit_load_x(b, xin, x_t, xrg_in)
    if layer == 0:
        b.op("sync", lambda s: s.dma_start(out=bcol_sb[:], in_=bcols),
             writes=["bcol"], dma="bcol")
        b.op("gpsimd", lambda g: g.dma_start(out=bvrow_sb[:], in_=bvrow),
             writes=["bvrow"], dma="bvrow")
        for j in range(4):
            b.op("tensor", lambda t, j=j: t.matmul(
                psA[:, 0:512], ones_bf[0:1, 0:128],
                bvrow_sb[0:1, j * 512:(j + 1) * 512], start=True, stop=True),
                reads=["bvrow", "ones_bf"], writes=["psA:0"])
            b.op("vector", lambda v, j=j: v.tensor_copy(
                bvb_sb[:, j * 512:(j + 1) * 512], psA[:, 0:512]),
                reads=["psA:0"], writes=["bvb"])
    if layer == 1:
        b.op("sync", lambda s: s.dma_start(out=cs_sb[:], in_=cosin),
             writes=["cs"], dma="cs")
        b.op("gpsimd", lambda g: g.dma_start(out=pro_sb[:], in_=protm),
             writes=["pro"], dma="pro")

    emit_ln(b, x_t, xn_t, cst, scr, ps, "ln",
            pre=(ctx.get("stats") if ctx is not None else None))

    out_evs = []

    def stream_wgroup(name, g, buf_i):
        return b.op("gpsimd", lambda gg, name=name, g=g, buf_i=buf_i: gg.dma_start(
            out=wbuf[buf_i][:].rearrange("p (c f) -> p c f", c=16),
            in_=wts[name][g].rearrange("c p f -> p c f")),
            reads=[], writes=[f"wbuf{buf_i}"], dma=f"wbuf{buf_i}")

    bufi_box = [0]

    def emit_fm_proj(name, out_ap, bias_off, gate=False, post=None):
        for g in range(4):
            buf_i = bufi_box[0]
            stream_wgroup(name, g, buf_i)
            if post is not None and g in post:
                post[g]()
            for hcl in range(4):
                hc = g * 4 + hcl
                pst = psA if (hc % 2 == 0) else psB
                pname = "psA" if (hc % 2 == 0) else "psB"
                for th in range(2):
                    for dc in range(16):
                        b.op("tensor", lambda t, dc=dc, th=th, hcl=hcl, pst=pst, buf_i=buf_i: t.matmul(
                            pst[:, th * 512:(th + 1) * 512],
                            wbuf[buf_i][:, dc * 512 + hcl * 128: dc * 512 + hcl * 128 + 128],
                            xn_t[dc][:, th * 512:(th + 1) * 512],
                            start=(dc == 0), stop=(dc == 15)),
                            reads=[f"wbuf{buf_i}", f"xn{dc}"],
                            writes=[f"{pname}:{th}"])
                qs = qk_sb2[hc % 2]
                qsn = f"qk_sb{hc % 2}"
                if gate:
                    gsb = r2_sb2[hc % 2]
                    gsn = r2n2[hc % 2]
                    b.op("scalar", lambda s, pst=pst, gsb=gsb: s.activation(
                        out=gsb[:], in_=pst[:, :], func=AF.Sigmoid),
                        reads=[f"{pname}:0", f"{pname}:1"], writes=[gsn])
                    b.op("vector", lambda v, pst=pst, qs=qs, gsb=gsb: v.tensor_tensor(
                        qs[:], pst[:, :], gsb[:], ALU.mult),
                        reads=[f"{pname}:0", f"{pname}:1", gsn], writes=[qsn])
                elif layer == 0:
                    for th in range(2):
                        b.op("scalar", lambda s, th=th, pst=pst, hc=hc, bias_off=bias_off, qs=qs: s.activation(
                            out=qs[:, th * 512:(th + 1) * 512],
                            in_=pst[:, th * 512:(th + 1) * 512], func=AF.Identity,
                            bias=bcol_sb[:, bias_off + hc:bias_off + hc + 1]),
                            reads=[f"{pname}:{th}", "bcol"], writes=[qsn])
                else:
                    hp2 = hc % 2
                    qbf_sb, r1_sb, r2_sb = qbf_sb2[hp2], r1_sb2[hp2], r2_sb2[hp2]
                    qbn, r1n, r2n = qbn2[hp2], r1n2[hp2], r2n2[hp2]
                    for th in range(2):
                        b.op("scalar", lambda s, th=th, pst=pst, qbf_sb=qbf_sb: s.activation(
                            out=qbf_sb[:, th * 512:(th + 1) * 512],
                            in_=pst[:, th * 512:(th + 1) * 512], func=AF.Copy),
                            reads=[f"{pname}:{th}"], writes=[qbn])
                    for th in range(2):
                        b.op("tensor", lambda t, th=th, qbf_sb=qbf_sb, hp2=hp2: t.matmul(
                            (psC if hp2 == 0 else psA)[:, th * 512:(th + 1) * 512], pro_sb[:, :],
                            qbf_sb[:, th * 512:(th + 1) * 512], start=True, stop=True),
                            reads=["pro", qbn],
                            writes=[(f"psC:{th}" if hp2 == 0 else f"psA:{th}")])
                    b.op("vector", lambda v, qbf_sb=qbf_sb, r1_sb=r1_sb: v.tensor_tensor(
                        r1_sb[:], qbf_sb[:], cs_sb[:, 0:TOWN], ALU.mult),
                        reads=[qbn, "cs"], writes=[r1n])
                    b.op("vector", lambda v, r2_sb=r2_sb, hp2=hp2: v.tensor_tensor(
                        r2_sb[:], (psC if hp2 == 0 else psA)[:, :], cs_sb[:, TOWN:2 * TOWN], ALU.mult),
                        reads=[("psC:0" if hp2 == 0 else "psA:0"),
                               ("psC:1" if hp2 == 0 else "psA:1"), "cs"],
                        writes=[r2n])
                    b.op("vector", lambda v, qs=qs, r1_sb=r1_sb, r2_sb=r2_sb: v.tensor_tensor(
                        qs[:], r1_sb[:], r2_sb[:], ALU.add),
                        reads=[r1n, r2n], writes=[qsn])
                ev = b.op("sync", lambda s, out_ap=out_ap, hc=hc, qs=qs: s.dma_start(
                    out=out_ap[hc], in_=qs[:]),
                    reads=[qsn],
                    writes=["kv_d" if name == "wk" else ("g_d" if gate else "q_d")],
                    dma=f"st_{qsn}")
                out_evs.append(ev)
            bufi_box[0] ^= 1

    def emit_v_proj():
        for g in range(4):
            buf_i = bufi_box[0]
            stream_wgroup("wv", g, buf_i)
            for tc in range(8):
                pst = psA if (tc % 2 == 0) else psB
                pname = "psA" if (tc % 2 == 0) else "psB"
                vp = tc % 2
                for dc in range(16):
                    b.op("tensor", lambda t, dc=dc, tc=tc, pst=pst, buf_i=buf_i: t.matmul(
                        pst[:, 0:512],
                        xn_t[dc][:, tc * 128:(tc + 1) * 128],
                        wbuf[buf_i][:, dc * 512:(dc + 1) * 512],
                        start=(dc == 0), stop=(dc == 15)),
                        reads=[f"wbuf{buf_i}", f"xn{dc}"], writes=[f"{pname}:0"])
                if layer == 0:
                    b.op("vector", lambda v, g=g, vp=vp, pst=pst: v.tensor_tensor(
                        v_sb[vp][:], pst[:, 0:512],
                        bvb_sb[:, g * 512:(g + 1) * 512], ALU.add),
                        reads=[f"{pname}:0", "bvb"], writes=[f"v_sb{vp}"])
                else:
                    b.op("vector", lambda v, vp=vp, pst=pst: v.tensor_copy(
                        v_sb[vp][:], pst[:, 0:512]),
                        reads=[f"{pname}:0"], writes=[f"v_sb{vp}"])
                ev = b.op("sync", lambda s, g=g, tc=tc, vp=vp: s.dma_start(
                    out=v_out[tc * 128:(tc + 1) * 128, g * 512:(g + 1) * 512],
                    in_=v_sb[vp][:]),
                    reads=[f"v_sb{vp}"], writes=["v_d"], dma=f"st_v{vp}")
                out_evs.append(ev)
            bufi_box[0] ^= 1

    def emit_vg():
        b.op("gpsimd", lambda g: g.collective_compute(
            "AllGather", ALU.bypass,
            replica_groups=[[0, 1], [2, 3], [4, 5], [6, 7]],
            ins=[ctx["v_own"]], outs=[ctx["v_gth"]]),
            reads=["v_d"], writes=["vg_d"], cc="cca")

    def emit_kg():
        b.op("gpsimd", lambda g: g.collective_compute(
            "AllGather", ALU.bypass,
            replica_groups=[[0, 1], [2, 3], [4, 5], [6, 7]],
            ins=[ctx["k_own"]], outs=[ctx["k_gth"]]),
            reads=["kv_d"], writes=["kg_d"], cc="ccb")

    emit_v_proj()
    emit_fm_proj("wk", k_out, 16)
    emit_fm_proj("wq", q_out, 0,
                 post=({0: emit_vg, 1: emit_kg} if ctx is not None else None))
    if layer == 1:
        emit_fm_proj("wg", g_out, 32, gate=True)

    if ctx is None:
        b.finish(out_evs)
        b.emit(es)
        eph.close()
        es.close()
        return nc
    eph.close()
    return out_evs


def build_attn(layer, ctx=None):
    import contextlib
    if ctx is None:
        nc = bass.Bass("TRN2", target_bir_lowering=False, debug=False)
        es = contextlib.ExitStack()
        b = None
        pref = ""
        dram = None
        xrg_in, xrg_out = "xin_d", "xout_d"
        kv_gath = None
    else:
        nc, b, pref, dram = ctx["nc"], ctx["b"], ctx["pref"], ctx["dram"]
        xrg_in, xrg_out = ctx["x_in_rg"], ctx["x_out_rg"]
        kv_gath = ctx["kv_gath"]

    def DR(name, shape, dt, kind):
        if dram is not None:
            return dram.get(name)
        return nc.dram_tensor(name, shape, dt, kind=kind).ap()

    xin = DR("xin", [D, TOWN], F32, "ExternalInput")
    q_in = DR("q_in", [NH, 128, TOWN], BF16, "ExternalInput")
    if kv_gath is None:
        k_in = DR("k_in", [NH, 128, TFULL], BF16, "ExternalInput")
        v_in = DR("v_in", [TFULL, D], BF16, "ExternalInput")
    wo = DR("wo", [4, 16, 128, 512], F32, "ExternalInput")
    consts = DR("consts", [128, 144], F32, "ExternalInput")
    if layer == 0:
        bocols = DR("bocols", [128, 16], F32, "ExternalInput")
    else:
        g_in = DR("g_in", [NH, 128, TOWN], BF16, "ExternalInput")
        normw = DR("normw", [128, 1], F32, "ExternalInput")
        maskA = DR("maskA", [128, 128], F32, "ExternalInput")
        maskB = DR("maskB", [128, 128], F32, "ExternalInput")
    xout = DR("xout", [D, TOWN], F32, "ExternalOutput")

    eph = contextlib.ExitStack()
    sb = lambda name, shape, dt: eph.enter_context(
        nc.sbuf_tensor(pref + name, shape, dt))
    v_t = [sb(f"v{i}", [128, D], BF16) for i in range(16)]
    k_h = [sb(f"k_h{i}", [128, TFULL], BF16) for i in range(2)]
    q_h = [sb(f"q_h{i}", [128, TOWN], BF16) for i in range(2)]
    probs = [sb(f"pr{i}", [128, TOWN], BF16) for i in range(2)]
    ao_t = [sb(f"ao{i}", [128, TOWN], BF16) for i in range(16)]
    wbuf = [sb(f"wbuf{i}", [128, 8192], BF16) for i in range(2)]
    cstbuf = sb("cstbuf", [128, 144], F32)
    ones_bf = sb("ones_bf", [128, 128], BF16)
    invrow = sb("invrow", [1, TOWN], F32)
    invrowb = sb("invrowb", [1, TOWN], BF16)
    osb_sb = sb("osb_sb", [128, TOWN], F32)
    stat_xc = sb("stat_xc", [128, TOWN], BF16)
    stat_sq = sb("stat_sq", [128, TOWN], BF16)
    xc_sb = [sb(f"xc{i}", [128, TOWN], F32) for i in range(2)]
    xo_sb = [sb(f"xo{i}", [128, TOWN], F32) for i in range(2)]
    if layer == 0:
        boc_sb = sb("boc_sb", [128, 16], F32)
    else:
        g_hb = [sb(f"g_h{i}", [128, TOWN], BF16) for i in range(2)]
        nw_sb = sb("nw_sb", [128, 1], F32)
        mA_sb = sb("mA_sb", [128, 128], F32)
        mB_sb = sb("mB_sb", [128, 128], F32)
        onorm = sb("onorm", [128, TOWN], F32)
        sq_sb = sb("sq_sb", [128, TOWN], BF16)
        rrow = sb("rrow", [1, TOWN], F32)
        rrowb = sb("rrowb", [1, TOWN], BF16)

    if ctx is None:
        PP = [nc.alloc_psum_tensor(f"P{i}", [128, TOWN], F32) for i in range(4)]
        b = Builder(nc)
    else:
        PP = ctx["P"]
        b.barrier()
    SC = [PP[0], PP[1]]
    OUT = PP[2]
    R0 = PP[3][0:1, 0:512]
    R1 = PP[3][0:1, 512:1024]
    b.once_slot("cst", "ones_bf", "boc", "nw", "mA", "mB")

    b.op("sync", lambda s: s.dma_start(out=cstbuf[:], in_=consts),
         writes=["cst"], dma="cst")
    b.op("gpsimd", lambda g: g.dma_start(out=ones_bf[:], in_=consts[:, 8:136]),
         writes=["ones_bf"], dma="ones_bf")
    for i in range(16):
        if kv_gath is None:
            b.op("scalar", lambda s, i=i: s.dma_start(
                out=v_t[i][:], in_=v_in[i * 128:(i + 1) * 128, :]),
                writes=[f"v{i}"], dma=f"v{i}")
        else:
            r, j = i // 8, i % 8
            src_ap = kv_gath[1][r * 2048 + j * 256: r * 2048 + (j + 1) * 256, :]
            b.op("scalar", lambda s, i=i, src_ap=src_ap: s.dma_start(
                out=v_t[i][:], in_=src_ap.rearrange("(p a) c -> p (a c)", a=2)),
                reads=["vg_d"], writes=[f"v{i}"], dma=f"v{i}")
    if layer == 0:
        b.op("sync", lambda s: s.dma_start(out=boc_sb[:], in_=bocols),
             writes=["boc"], dma="boc")
    else:
        b.op("sync", lambda s: s.dma_start(out=nw_sb[:], in_=normw),
             writes=["nw"], dma="nw")
        b.op("sync", lambda s: s.dma_start(out=mA_sb[:], in_=maskA),
             writes=["mA"], dma="mA")
        b.op("sync", lambda s: s.dma_start(out=mB_sb[:], in_=maskB),
             writes=["mB"], dma="mB")

    def kc_info(kc):
        if layer == 0:
            return 0, None
        jj = kc % 8
        qf = jj * 128
        return qf, jj

    kcs = list(range(16))

    def pieces(qf):
        if qf < 512:
            return [(qf, 512 - qf), (512, 512)]
        return [(qf, 1024 - qf)]

    touch = {0: [], 1: []}
    for kc in kcs:
        qf, _ = kc_info(kc)
        for (qo, w) in pieces(qf):
            bank = 0 if qo < 512 else 1
            touch[bank].append(kc)
    first_t = {bk: (t[0] if t else None) for bk, t in touch.items()}
    last_t = {bk: (t[-1] if t else None) for bk, t in touch.items()}

    def head_dmas(h):
        b.op("sync", lambda s, h=h: s.dma_start(out=q_h[h % 2][:], in_=q_in[h]),
             reads=["q_d"], writes=[f"q_h{h % 2}"], dma=f"q{h % 2}")
        if kv_gath is None:
            b.op("sync", lambda s, h=h: s.dma_start(out=k_h[h % 2][:], in_=k_in[h]),
                 writes=[f"k_h{h % 2}"], dma=f"k{h % 2}")
        else:
            for r in range(2):
                b.op("sync", lambda s, h=h, r=r: s.dma_start(
                    out=k_h[h % 2][:, r * TOWN:(r + 1) * TOWN],
                    in_=kv_gath[0][r * 2048 + h * 128: r * 2048 + (h + 1) * 128, :]),
                    reads=["kg_d"], writes=[f"k_h{h % 2}"], dma=f"k{h % 2}")
        if layer == 1:
            b.op("sync", lambda s, h=h: s.dma_start(out=g_hb[h % 2][:], in_=g_in[h]),
                 reads=["g_d"], writes=[f"g_h{h % 2}"], dma=f"g{h % 2}")

    def emit_scores(h, kc):
        par = kc % 2
        hb = h % 2
        qf, dq = kc_info(kc)
        for (qo, w) in pieces(qf):
            bank = 0 if qo < 512 else 1
            b.op("tensor", lambda t, kc=kc, hb=hb, qo=qo, w=w, par=par: t.matmul(
                SC[par][:, qo:qo + w], k_h[hb][:, kc * 128:(kc + 1) * 128],
                q_h[hb][:, qo:qo + w], start=True, stop=True),
                reads=[f"k_h{hb}", f"q_h{hb}"], writes=[f"SC{par}:{bank}"])
        if layer == 1 and dq is not None:
            mt = mA_sb if kc < 8 else mB_sb
            mr = "mA" if kc < 8 else "mB"
            bank = 0 if dq * 128 < 512 else 1
            b.op("vector", lambda v, dq=dq, par=par, mt=mt: v.tensor_tensor(
                SC[par][:, dq * 128:(dq + 1) * 128],
                SC[par][:, dq * 128:(dq + 1) * 128], mt[:], ALU.add),
                reads=[f"SC{par}:{bank}", mr], writes=[f"SC{par}:{bank}"])
        rd = [f"SC{par}:0", f"SC{par}:1"] if qf < 512 else [f"SC{par}:1"]
        b.op("scalar", lambda s, par=par, qf=qf: s.activation(
            out=probs[par][:, qf:TOWN], in_=SC[par][:, qf:TOWN], func=AF.Exp,
            scale=SCALE_ATT, bias=cstbuf[:, 0:1]),
            reads=rd + ["cst"], writes=[f"pr{par}"])

    def emit_sumout(h, kc):
        par = kc % 2
        qf, _ = kc_info(kc)
        for (qo, w) in pieces(qf):
            bank = 0 if qo < 512 else 1
            Rp = R0 if bank == 0 else R1
            st = (kc == first_t[bank])
            sp = (kc == last_t[bank])
            b.op("tensor", lambda t, qo=qo, w=w, par=par, Rp=Rp, st=st, sp=sp, bank=bank: t.matmul(
                Rp[0:1, qo - bank * 512:qo - bank * 512 + w],
                ones_bf[:, 0:1], probs[par][:, qo:qo + w],
                start=st, stop=sp, skip_group_check=True),
                reads=[f"pr{par}", "ones_bf"], writes=[f"R{bank}"])
            b.op("tensor", lambda t, kc=kc, h=h, qo=qo, w=w, par=par, st=st, sp=sp: t.matmul(
                OUT[:, qo:qo + w], v_t[kc][:, h * 128:(h + 1) * 128],
                probs[par][:, qo:qo + w],
                start=st, stop=sp, skip_group_check=True),
                reads=[f"pr{par}", f"v{kc}"], writes=[f"OUT:{bank}"])

    head_dmas(0)
    for h in range(NH):
        if h + 1 < NH:
            head_dmas(h + 1)
        prev = None
        for kc in kcs:
            emit_scores(h, kc)
            if prev is not None:
                emit_sumout(h, prev)
            prev = kc
        emit_sumout(h, prev)
        b.op("vector", lambda v: v.tensor_copy(osb_sb[:], OUT[:, :]),
             reads=["OUT:0", "OUT:1"], writes=["osb"])
        b.op("scalar", lambda s: s.activation(
            out=invrow[0:1, 0:512], in_=R0[0:1, :], func=AF.Ln),
            reads=["R0"], writes=["invrow"])
        b.op("scalar", lambda s: s.activation(
            out=invrow[0:1, 512:1024], in_=R1[0:1, :], func=AF.Ln),
            reads=["R1"], writes=["invrow"])
        b.op("scalar", lambda s: s.activation(
            out=invrowb[0:1, :], in_=invrow[0:1, :], func=AF.Exp, scale=-1.0),
            reads=["invrow"], writes=["invrowb"])
        for th in range(2):
            b.op("tensor", lambda t, th=th: t.matmul(
                OUT[:, th * 512:(th + 1) * 512], ones_bf[0:1, 0:128],
                invrowb[0:1, th * 512:(th + 1) * 512], start=True, stop=True),
                reads=["invrowb", "ones_bf"], writes=[f"OUT:{th}"])
        if layer == 0:
            b.op("vector", lambda v, h=h: v.tensor_tensor(
                ao_t[h][:], osb_sb[:], OUT[:, :], ALU.mult),
                reads=["osb", "OUT:0", "OUT:1"], writes=[f"ao{h}"])
        else:
            hb = h % 2
            b.op("vector", lambda v: v.tensor_tensor(
                onorm[:], osb_sb[:], OUT[:, :], ALU.mult),
                reads=["osb", "OUT:0", "OUT:1"], writes=["onorm"])
            b.op("scalar", lambda s: s.activation(
                out=sq_sb[:], in_=onorm[:], func=AF.Square),
                reads=["onorm"], writes=["sq"])
            for th in range(2):
                Rp = R0 if th == 0 else R1
                b.op("tensor", lambda t, th=th, Rp=Rp: t.matmul(
                    Rp[0:1, :], ones_bf[:, 0:1], sq_sb[:, th * 512:(th + 1) * 512],
                    start=True, stop=True),
                    reads=["sq", "ones_bf"], writes=[f"R{th}"])
                b.op("scalar", lambda s, th=th, Rp=Rp: s.activation(
                    out=rrow[0:1, th * 512:(th + 1) * 512], in_=Rp[0:1, :],
                    func=AF.Ln, scale=1.0 / 128.0, bias=cstbuf[0:1, 2:3]),
                    reads=[f"R{th}", "cst"], writes=["rrow"])
            b.op("scalar", lambda s: s.activation(
                out=rrowb[0:1, :], in_=rrow[0:1, :], func=AF.Exp, scale=-0.5),
                reads=["rrow"], writes=["rrowb"])
            for th in range(2):
                b.op("tensor", lambda t, th=th: t.matmul(
                    SC[1][:, th * 512:(th + 1) * 512], ones_bf[0:1, 0:128],
                    rrowb[0:1, th * 512:(th + 1) * 512], start=True, stop=True),
                    reads=["rrowb", "ones_bf"], writes=[f"SC1:{th}"])
            b.op("vector", lambda v: v.tensor_tensor(
                onorm[:], onorm[:], SC[1][:, :], ALU.mult),
                reads=["onorm", "SC1:0", "SC1:1"], writes=["onorm"])
            b.op("vector", lambda v, h=h, hb=hb: v.scalar_tensor_tensor(
                ao_t[h][:], onorm[:], nw_sb[:, 0:1], g_hb[hb][:],
                ALU.mult, ALU.mult),
                reads=["onorm", "nw", f"g_h{hb}"], writes=[f"ao{h}"])

    out_evs = []
    for g in range(4):
        b.op("gpsimd", lambda gg, g=g: gg.dma_start(
            out=wbuf[g % 2][:].rearrange("p (c f) -> p c f", c=16),
            in_=wo[g].rearrange("c p f -> p c f")),
            writes=[f"wbuf{g % 2}"], dma=f"wbuf{g % 2}")
        for hcl in range(4):
            dcog = g * 4 + hcl
            par = dcog % 2
            pp = SC[0] if par == 0 else OUT
            ppn = "SC0" if par == 0 else "OUT"
            b.op("sync", lambda s, dcog=dcog, par=par: s.dma_start(
                out=xc_sb[par][:], in_=xin[dcog * 128:(dcog + 1) * 128, :]),
                reads=[xrg_in], writes=[f"xc{par}"], dma=f"xi{par}")
            for th in range(2):
                for hc in range(16):
                    b.op("tensor", lambda t, g=g, hcl=hcl, th=th, hc=hc, pp=pp: t.matmul(
                        pp[:, th * 512:(th + 1) * 512],
                        wbuf[g % 2][:, hc * 512 + hcl * 128: hc * 512 + hcl * 128 + 128],
                        ao_t[hc][:, th * 512:(th + 1) * 512],
                        start=(hc == 0), stop=(hc == 15)),
                        reads=[f"wbuf{g % 2}", f"ao{hc}"], writes=[f"{ppn}:{th}"])
            if layer == 0:
                b.op("vector", lambda v, dcog=dcog, par=par, pp=pp, ppn=ppn: v.scalar_tensor_tensor(
                    xo_sb[par][:], pp[:, :], boc_sb[:, dcog:dcog + 1],
                    xc_sb[par][:], ALU.add, ALU.add),
                    reads=[f"{ppn}:0", f"{ppn}:1", "boc", f"xc{par}"],
                    writes=[f"xo{par}"])
            else:
                b.op("vector", lambda v, par=par, pp=pp, ppn=ppn: v.tensor_tensor(
                    xo_sb[par][:], pp[:, :], xc_sb[par][:], ALU.add),
                    reads=[f"{ppn}:0", f"{ppn}:1", f"xc{par}"],
                    writes=[f"xo{par}"])
            if ctx is not None:
                b.op("scalar", lambda s, par=par: s.activation(
                    out=stat_xc[:], in_=xo_sb[par][:], func=AF.Copy),
                    reads=[f"xo{par}"], writes=["stat_xc"])
                b.op("scalar", lambda s, par=par: s.activation(
                    out=stat_sq[:], in_=xo_sb[par][:], func=AF.Square),
                    reads=[f"xo{par}"], writes=["stat_sq"])
                for th in range(2):
                    b.op("tensor", lambda t, th=th, dcog=dcog: t.matmul(
                        (R0 if th == 0 else R1)[0:1, :],
                        ones_bf[:, 0:1], stat_xc[:, th * 512:(th + 1) * 512],
                        start=(dcog == 0), stop=(dcog == 15), skip_group_check=True),
                        reads=["stat_xc", "ones_bf"], writes=[f"R{th}"])
                    b.op("tensor", lambda t, th=th, dcog=dcog: t.matmul(
                        SC[1][0:1, th * 512:(th + 1) * 512],
                        ones_bf[:, 0:1], stat_sq[:, th * 512:(th + 1) * 512],
                        start=(dcog == 0), stop=(dcog == 15), skip_group_check=True),
                        reads=["stat_sq", "ones_bf"], writes=[f"SC1:{th}"])
            ev = b.op("sync", lambda s, dcog=dcog, par=par: s.dma_start(
                out=xout[dcog * 128:(dcog + 1) * 128, :], in_=xo_sb[par][:]),
                reads=[f"xo{par}"], writes=[xrg_out], dma=f"st_xo{par}")
            out_evs.append(ev)

    if ctx is None:
        b.finish(out_evs)
        b.emit(es)
        eph.close()
        es.close()
        return nc
    eph.close()
    return out_evs


def build_moe(ctx=None):
    import contextlib
    if ctx is None:
        nc = bass.Bass("TRN2", target_bir_lowering=False, debug=False)
        es = contextlib.ExitStack()
        b = None
        pref = ""
        dram = None
        xrg_in, xrg_out = "xin_d", "xout_d"
    else:
        nc, b, pref, dram = ctx["nc"], ctx["b"], ctx["pref"], ctx["dram"]
        xrg_in, xrg_out = ctx["x_in_rg"], ctx["x_out_rg"]

    def DR(name, shape, dt, kind):
        if dram is not None:
            return dram[name]
        return nc.dram_tensor(name, shape, dt, kind=kind).ap()

    xin = DR("xin", [D, TOWN], F32, "ExternalInput")
    wg = DR("wg", [16, 128, 8], F32, "ExternalInput")
    mbias = DR("mbias", [8, 1], F32, "ExternalInput")
    w1 = DR("w1", [8, 4, 16, 128, 256], F32, "ExternalInput")
    w3 = DR("w3", [8, 4, 16, 128, 256], F32, "ExternalInput")
    w2 = DR("w2", [8, 4, 8, 128, 512], F32, "ExternalInput")
    ident = DR("ident", [128, 128], F32, "ExternalInput")
    consts = DR("consts", [128, 144], F32, "ExternalInput")
    xout = DR("xout", [D, TOWN], F32, "ExternalOutput")

    eph = contextlib.ExitStack()
    sb = lambda name, shape, dt: eph.enter_context(
        nc.sbuf_tensor(pref + name, shape, dt))
    xn_t = [sb(f"xn{i}", [128, TOWN], BF16) for i in range(16)]
    xb = [sb(f"xb{i}", [128, TOWN], F32) for i in range(2)]
    xc = sb("lxc", [128, TOWN], BF16)
    xsq = sb("lxsq", [128, TOWN], BF16)
    rows = sb("rows", [1, 2048], F32)
    A_sb = sb("lnA", [128, TOWN], F32)
    B_sb = sb("lnB", [128, TOWN], F32)
    T_sb = sb("lnT", [128, TOWN], F32)
    rw_sb = sb("rw", [8, TOWN], F32)
    cstbuf = sb("cstbuf", [128, 144], F32)
    ones_bf = sb("ones_bf", [128, 128], BF16)
    id_sb = sb("id_sb", [128, 128], F32)
    mb_sb = sb("mb_sb", [8, 1], F32)
    wgt_sb = sb("wgt_sb", [128, 16 * 8], F32)
    sct = sb("sct", [128, 64], F32)
    rwt = sb("rwt", [128, 64], F32)
    ewt = sb("ewt", [128, 64], F32)
    s2_sb = sb("s2t", [128, 8], F32)
    col1 = sb("col1", [128, 1], F32)
    col2 = sb("col2", [128, 1], F32)
    Bf = sb("Bf", [8, 1], F32)
    ew_fm = sb("ew_fm", [8, TOWN], F32)
    ewrow = sb("ewrow", [1, TOWN], F32)
    ew_bc = [sb(f"ewb{e}", [128, TOWN], BF16) for e in range(8)]
    w1buf = [sb(f"w1b{i}", [128, 4096], BF16) for i in range(2)]
    w3buf = [sb(f"w3b{i}", [128, 4096], BF16) for i in range(2)]
    w2buf = [sb(f"w2b{i}", [128, 4096], BF16) for i in range(2)]
    s1_sb = sb("s1", [128, TOWN], BF16)
    t3_sb = sb("t3", [128, TOWN], BF16)
    g_t = [sb(f"g{i}", [128, TOWN], BF16) for i in range(8)]
    op_t = [sb(f"op{i}", [128, TOWN], BF16) for i in range(16)]

    if ctx is None:
        P = [nc.alloc_psum_tensor(f"P{i}", [128, TOWN], F32) for i in range(4)]
        b = Builder(nc)
    else:
        P = ctx["P"]
        b.barrier()
    b.once_slot("cst", "ones_bf", "id", "mb", "wgt")
    b.op("sync", lambda s: s.dma_start(out=cstbuf[:], in_=consts),
         writes=["cst"], dma="cst")
    b.op("gpsimd", lambda g: g.dma_start(out=ones_bf[:], in_=consts[:, 8:136]),
         writes=["ones_bf"], dma="ones_bf")
    b.op("sync", lambda s: s.dma_start(out=id_sb[:], in_=ident),
         writes=["id"], dma="id")
    b.op("sync", lambda s: s.dma_start(out=mb_sb[:], in_=mbias),
         writes=["mb"], dma="mb")
    b.op("sync", lambda s: s.dma_start(
        out=wgt_sb[:].rearrange("p (c e) -> p c e", c=16),
        in_=wg.rearrange("c p e -> p c e")), writes=["wgt"], dma="wgt")

    stats = ctx.get("stats") if ctx is not None else None
    mean = rows[0:1, 0:1024]
    var = rows[0:1, 1024:2048]
    if stats is not None:
        for th in range(2):
            ap, rg = stats["sx"][th]
            b.op("vector", lambda v, th=th, ap=ap: v.tensor_scalar(
                mean[0:1, th * 512:(th + 1) * 512], ap,
                1.0 / 2048.0, None, ALU.mult),
                reads=[rg], writes=["mean"])
        b.op("vector", lambda v: v.tensor_tensor(var, mean, mean, ALU.mult),
             reads=["mean"], writes=["var"])
        for th in range(2):
            ap, rg = stats["sq"][th]
            b.op("vector", lambda v, th=th, ap=ap: v.scalar_tensor_tensor(
                var[0:1, th * 512:(th + 1) * 512], ap,
                1.0 / 2048.0, var[0:1, th * 512:(th + 1) * 512], ALU.mult, ALU.subtract),
                reads=[rg, "var"], writes=["var"])
    for dc in range(16):
        par = dc % 2
        b.op("sync", lambda s, dc=dc, par=par: s.dma_start(
            out=xb[par][:], in_=xin[dc * 128:(dc + 1) * 128, :]),
            reads=[xrg_in], writes=[f"xb{par}"], dma=f"x{par}")
        if stats is None:
            b.op("scalar", lambda s, par=par: s.activation(
                out=xc[:], in_=xb[par][:], func=AF.Copy),
                reads=[f"xb{par}"], writes=["xc"])
            b.op("scalar", lambda s, par=par: s.activation(
                out=xsq[:], in_=xb[par][:], func=AF.Square),
                reads=[f"xb{par}"], writes=["xsq"])
            for th in range(2):
                b.op("tensor", lambda t, dc=dc, th=th: t.matmul(
                    P[0][0:1, th * 512:(th + 1) * 512],
                    ones_bf[:, 0:1], xc[:, th * 512:(th + 1) * 512],
                    start=(dc == 0), stop=(dc == 15), skip_group_check=True),
                    reads=["xc", "ones_bf"], writes=[f"P0:{th}"])
                b.op("tensor", lambda t, dc=dc, th=th: t.matmul(
                    P[1][0:1, th * 512:(th + 1) * 512],
                    ones_bf[:, 0:1], xsq[:, th * 512:(th + 1) * 512],
                    start=(dc == 0), stop=(dc == 15), skip_group_check=True),
                    reads=["xsq", "ones_bf"], writes=[f"P1:{th}"])
        for th in range(2):
            b.op("tensor", lambda t, dc=dc, th=th, par=par: t.matmul(
                P[2][0:8, th * 512:(th + 1) * 512],
                wgt_sb[:, dc * 8:(dc + 1) * 8], xb[par][:, th * 512:(th + 1) * 512],
                start=(dc == 0), stop=(dc == 15), skip_group_check=True),
                reads=["wgt", f"xb{par}"], writes=[f"P2:{th}"])
        b.op("tensor", lambda t, dc=dc: t.matmul(
            P[3][0:8, 0:1], wgt_sb[:, dc * 8:(dc + 1) * 8], cstbuf[:, 8:9],
            start=(dc == 0), stop=(dc == 15), skip_group_check=True),
            reads=(["wgt", "cst"] + ([stats["sx"][0][1], stats["sx"][1][1]]
                                     if stats is not None else [])),
            writes=["P3:0"])

    if stats is None:
        for th in range(2):
            b.op("vector", lambda v, th=th: v.tensor_scalar(
                mean[0:1, th * 512:(th + 1) * 512], P[0][0:1, th * 512:(th + 1) * 512],
                1.0 / 2048.0, None, ALU.mult),
                reads=[f"P0:{th}"], writes=["mean"])
        b.op("vector", lambda v: v.tensor_tensor(var, mean, mean, ALU.mult),
             reads=["mean"], writes=["var"])
        for th in range(2):
            b.op("vector", lambda v, th=th: v.scalar_tensor_tensor(
                var[0:1, th * 512:(th + 1) * 512], P[1][0:1, th * 512:(th + 1) * 512],
                1.0 / 2048.0, var[0:1, th * 512:(th + 1) * 512], ALU.mult, ALU.subtract),
                reads=[f"P1:{th}", "var"], writes=["var"])
    b.op("scalar", lambda s: s.activation(
        out=var, in_=var, func=AF.Ln, bias=cstbuf[0:1, 1:2]),
        reads=["var", "cst"], writes=["var"])
    b.op("scalar", lambda s: s.activation(
        out=var, in_=var, func=AF.Exp, scale=-0.5),
        reads=["var"], writes=["var"])
    b.op("vector", lambda v: v.tensor_copy(Bf[:], P[3][0:8, 0:1]),
         reads=["P3:0"], writes=["Bf"])
    b.op("vector", lambda v: v.tensor_tensor(mean, mean, var, ALU.mult),
         reads=["mean", "var"], writes=["mean"])
    b.op("vector", lambda v: v.tensor_copy(T_sb[0:8, :], P[2][0:8, :]),
         reads=["P2:0", "P2:1"], writes=["lnT"])
    for th in range(2):
        b.op("tensor", lambda t, th=th: t.matmul(
            P[3][0:8, th * 512:(th + 1) * 512], cstbuf[0:1, 8:16],
            var[0:1, th * 512:(th + 1) * 512], start=True, stop=True),
            reads=["var", "cst"], writes=[f"P3:{th}"])
    b.op("vector", lambda v: v.tensor_tensor(
        A_sb[0:8, :], T_sb[0:8, :], P[3][0:8, :], ALU.mult),
        reads=["lnT", "P3:0", "P3:1"], writes=["lnA"])
    for th in range(2):
        b.op("tensor", lambda t, th=th: t.matmul(
            P[3][0:8, th * 512:(th + 1) * 512], cstbuf[0:1, 8:16],
            mean[0:1, th * 512:(th + 1) * 512], start=True, stop=True),
            reads=["mean", "cst"], writes=[f"P3:{th}"])
    b.op("vector", lambda v: v.tensor_scalar(
        B_sb[0:8, :], P[3][0:8, :], Bf[:, 0:1], None, ALU.mult),
        reads=["P3:0", "P3:1", "Bf"], writes=["lnB"])
    b.op("vector", lambda v: v.tensor_tensor(
        A_sb[0:8, :], A_sb[0:8, :], B_sb[0:8, :], ALU.subtract),
        reads=["lnA", "lnB"], writes=["lnA"])
    b.op("scalar", lambda s: s.activation(
        out=rw_sb[:], in_=A_sb[0:8, :], func=AF.Exp, scale=-1.0),
        reads=["lnA"], writes=["rw"])
    b.op("vector", lambda v: v.tensor_scalar(
        rw_sb[:], rw_sb[:], 1.0, None, ALU.add),
        reads=["rw"], writes=["rw"])
    b.op("vector", lambda v: v.reciprocal(rw_sb[:], rw_sb[:]),
         reads=["rw"], writes=["rw"])
    b.op("vector", lambda v: v.tensor_scalar(
        T_sb[0:8, :], rw_sb[:], mb_sb[0:8, 0:1], None, ALU.add),
        reads=["rw", "mb"], writes=["lnT"])

    for tc in range(8):
        b.op("tensor", lambda t, tc=tc: t.transpose(
            P[3][:, 0:8], T_sb[0:8, tc * 128:(tc + 1) * 128], id_sb[0:8, 0:8]),
            reads=["lnT", "id"], writes=["P3:0"])
        b.op("vector", lambda v, tc=tc: v.tensor_copy(
            sct[:, tc * 8:(tc + 1) * 8], P[3][:, 0:8]),
            reads=["P3:0"], writes=["sct"])
        b.op("tensor", lambda t, tc=tc: t.transpose(
            P[3][:, 0:8], rw_sb[0:8, tc * 128:(tc + 1) * 128], id_sb[0:8, 0:8]),
            reads=["rw", "id"], writes=["P3:0"])
        b.op("vector", lambda v, tc=tc: v.tensor_copy(
            rwt[:, tc * 8:(tc + 1) * 8], P[3][:, 0:8]),
            reads=["P3:0"], writes=["rwt"])
    for tc in range(8):
        s_ = sct[:, tc * 8:(tc + 1) * 8]
        r_ = rwt[:, tc * 8:(tc + 1) * 8]
        e_ = ewt[:, tc * 8:(tc + 1) * 8]
        b.op("vector", lambda v, s_=s_: v.reduce_max(
            col1[:, 0:1], s_, axis=mybir.AxisListType.X),
            reads=["sct"], writes=["col1"])
        b.op("vector", lambda v, s_=s_, e_=e_: v.tensor_scalar(
            e_, s_, col1[:, 0:1], None, ALU.is_ge),
            reads=["sct", "col1"], writes=["ewt"])
        b.op("vector", lambda v, s_=s_, e_=e_: v.scalar_tensor_tensor(
            s2_sb[:], e_, -1e30, s_, ALU.mult, ALU.add),
            reads=["ewt", "sct"], writes=["s2t"])
        b.op("vector", lambda v: v.reduce_max(
            col2[:, 0:1], s2_sb[:], axis=mybir.AxisListType.X),
            reads=["s2t"], writes=["col2"])
        b.op("vector", lambda v: v.tensor_scalar(
            s2_sb[:], s2_sb[:], col2[:, 0:1], None, ALU.is_ge),
            reads=["s2t", "col2"], writes=["s2t"])
        b.op("vector", lambda v, e_=e_: v.tensor_tensor(
            e_, e_, s2_sb[:], ALU.add),
            reads=["ewt", "s2t"], writes=["ewt"])
        b.op("vector", lambda v, r_=r_, e_=e_: v.tensor_tensor(
            e_, e_, r_, ALU.mult),
            reads=["ewt", "rwt"], writes=["ewt"])
        b.op("vector", lambda v, e_=e_: v.reduce_sum(
            col1[:, 0:1], e_, axis=mybir.AxisListType.X),
            reads=["ewt"], writes=["col1"])
        b.op("vector", lambda v: v.reciprocal(col1[:, 0:1], col1[:, 0:1]),
             reads=["col1"], writes=["col1"])
        b.op("vector", lambda v, e_=e_: v.tensor_scalar(
            e_, e_, col1[:, 0:1], None, ALU.mult),
            reads=["ewt", "col1"], writes=["ewt"])
    for tc in range(8):
        b.op("tensor", lambda t, tc=tc: t.transpose(
            P[3][0:8, 0:128], ewt[:, tc * 8:(tc + 1) * 8], id_sb[:, 0:128]),
            reads=["ewt", "id"], writes=["P3:0"])
        b.op("vector", lambda v, tc=tc: v.tensor_copy(
            ew_fm[0:8, tc * 128:(tc + 1) * 128], P[3][0:8, 0:128]),
            reads=["P3:0"], writes=["ew_fm"])
    for e in range(8):
        b.op("gpsimd", lambda g, e=e: g.dma_start(
            out=ewrow[0:1, :], in_=ew_fm[e:e + 1, :]),
            reads=["ew_fm"], writes=["ewrow"], dma="ewrow")
        for th in range(2):
            b.op("tensor", lambda t, th=th: t.matmul(
                P[3][:, th * 512:(th + 1) * 512], cstbuf[0:1, 8:136],
                ewrow[0:1, th * 512:(th + 1) * 512], start=True, stop=True),
                reads=["ewrow", "cst"], writes=[f"P3:{th}"])
        b.op("vector", lambda v, e=e: v.tensor_copy(
            ew_bc[e][:], P[3][:, :]),
            reads=["P3:0", "P3:1"], writes=[f"ewb{e}"])

    for th in range(2):
        b.op("tensor", lambda t, th=th: t.matmul(
            P[0][:, th * 512:(th + 1) * 512], cstbuf[0:1, 8:136],
            var[0:1, th * 512:(th + 1) * 512], start=True, stop=True),
            reads=["var", "cst"], writes=[f"P0:{th}"])
        b.op("vector", lambda v, th=th: v.tensor_copy(
            A_sb[:, th * 512:(th + 1) * 512], P[0][:, th * 512:(th + 1) * 512]),
            reads=[f"P0:{th}"], writes=["lnA"])
        b.op("tensor", lambda t, th=th: t.matmul(
            P[1][:, th * 512:(th + 1) * 512], cstbuf[0:1, 8:136],
            mean[0:1, th * 512:(th + 1) * 512], start=True, stop=True),
            reads=["mean", "cst"], writes=[f"P1:{th}"])
        b.op("vector", lambda v, th=th: v.tensor_scalar(
            B_sb[:, th * 512:(th + 1) * 512], P[1][:, th * 512:(th + 1) * 512],
            -1.0, None, ALU.mult),
            reads=[f"P1:{th}"], writes=["lnB"])
    for dc in range(16):
        par = dc % 2
        b.op("sync", lambda s, dc=dc, par=par: s.dma_start(
            out=xb[par][:], in_=xin[dc * 128:(dc + 1) * 128, :]),
            reads=[xrg_in], writes=[f"xb{par}"], dma=f"x{par}")
        b.op("vector", lambda v, par=par: v.tensor_tensor(
            T_sb[:], xb[par][:], A_sb[:], ALU.mult),
            reads=[f"xb{par}", "lnA"], writes=["lnT"])
        b.op("vector", lambda v, dc=dc: v.tensor_tensor(
            xn_t[dc][:], T_sb[:], B_sb[:], ALU.add),
            reads=["lnT", "lnB"], writes=[f"xn{dc}"])

    for e in range(8):
        for gq in range(4):
            wpar = gq % 2
            b.op("gpsimd", lambda g, e=e, gq=gq, wpar=wpar: g.dma_start(
                out=w1buf[wpar][:].rearrange("p (c f) -> p c f", c=16),
                in_=w1[e, gq].rearrange("c p f -> p c f")),
                writes=[f"w1b{wpar}"], dma=f"w1b{wpar}")
            b.op("gpsimd", lambda g, e=e, gq=gq, wpar=wpar: g.dma_start(
                out=w3buf[wpar][:].rearrange("p (c f) -> p c f", c=16),
                in_=w3[e, gq].rearrange("c p f -> p c f")),
                writes=[f"w3b{wpar}"], dma=f"w3b{wpar}")
            for fcl in range(2):
                fc = gq * 2 + fcl
                for th in range(2):
                    for dc in range(16):
                        b.op("tensor", lambda t, dc=dc, th=th, fcl=fcl, wpar=wpar: t.matmul(
                            P[0][:, th * 512:(th + 1) * 512],
                            w1buf[wpar][:, dc * 256 + fcl * 128: dc * 256 + fcl * 128 + 128],
                            xn_t[dc][:, th * 512:(th + 1) * 512],
                            start=(dc == 0), stop=(dc == 15)),
                            reads=[f"w1b{wpar}", f"xn{dc}"], writes=[f"P0:{th}"])
                    for dc in range(16):
                        b.op("tensor", lambda t, dc=dc, th=th, fcl=fcl, wpar=wpar: t.matmul(
                            P[1][:, th * 512:(th + 1) * 512],
                            w3buf[wpar][:, dc * 256 + fcl * 128: dc * 256 + fcl * 128 + 128],
                            xn_t[dc][:, th * 512:(th + 1) * 512],
                            start=(dc == 0), stop=(dc == 15)),
                            reads=[f"w3b{wpar}", f"xn{dc}"], writes=[f"P1:{th}"])
                b.op("scalar", lambda s: s.activation(
                    out=s1_sb[:], in_=P[0][:, :], func=AF.Silu),
                    reads=["P0:0", "P0:1"], writes=["s1"])
                b.op("vector", lambda v, e=e: v.tensor_tensor(
                    t3_sb[:], P[1][:, :], ew_bc[e][:], ALU.mult),
                    reads=["P1:0", "P1:1", f"ewb{e}"], writes=["t3"])
                b.op("vector", lambda v, fc=fc: v.tensor_tensor(
                    g_t[fc][:], s1_sb[:], t3_sb[:], ALU.mult),
                    reads=["s1", "t3"], writes=[f"g{fc}"])
        for g4 in range(4):
            wpar = g4 % 2
            b.op("gpsimd", lambda g, e=e, g4=g4, wpar=wpar: g.dma_start(
                out=w2buf[wpar][:].rearrange("p (c f) -> p c f", c=8),
                in_=w2[e, g4].rearrange("c p f -> p c f")),
                writes=[f"w2b{wpar}"], dma=f"w2b{wpar}")
            for dco in range(4):
                dcog = g4 * 4 + dco
                pp = 2 + (dco % 2)
                for th in range(2):
                    for fc in range(8):
                        b.op("tensor", lambda t, fc=fc, th=th, dco=dco, wpar=wpar, pp=pp: t.matmul(
                            P[pp][:, th * 512:(th + 1) * 512],
                            w2buf[wpar][:, fc * 512 + dco * 128: fc * 512 + dco * 128 + 128],
                            g_t[fc][:, th * 512:(th + 1) * 512],
                            start=(fc == 0), stop=(fc == 7)),
                            reads=[f"w2b{wpar}", f"g{fc}"], writes=[f"P{pp}:{th}"])
                if e == 0:
                    b.op("vector", lambda v, dcog=dcog, pp=pp: v.tensor_copy(
                        op_t[dcog][:], P[pp][:, :]),
                        reads=[f"P{pp}:0", f"P{pp}:1"], writes=[f"op{dcog}"])
                else:
                    b.op("vector", lambda v, dcog=dcog, pp=pp: v.tensor_tensor(
                        op_t[dcog][:], P[pp][:, :], op_t[dcog][:], ALU.add),
                        reads=[f"P{pp}:0", f"P{pp}:1", f"op{dcog}"],
                        writes=[f"op{dcog}"])

    out_evs = []
    for dcog in range(16):
        par = dcog % 2
        b.op("sync", lambda s, dcog=dcog, par=par: s.dma_start(
            out=xb[par][:], in_=xin[dcog * 128:(dcog + 1) * 128, :]),
            reads=[xrg_in], writes=[f"xb{par}"], dma=f"x{par}")
        b.op("vector", lambda v, dcog=dcog, par=par: v.tensor_tensor(
            xb[par][:], xb[par][:], op_t[dcog][:], ALU.add),
            reads=[f"xb{par}", f"op{dcog}"], writes=[f"xb{par}"])
        if ctx is not None:
            b.op("scalar", lambda s, par=par: s.activation(
                out=xc[:], in_=xb[par][:], func=AF.Copy),
                reads=[f"xb{par}"], writes=["xc"])
            b.op("scalar", lambda s, par=par: s.activation(
                out=xsq[:], in_=xb[par][:], func=AF.Square),
                reads=[f"xb{par}"], writes=["xsq"])
            for th in range(2):
                b.op("tensor", lambda t, th=th, dcog=dcog: t.matmul(
                    P[3][0:1, th * 512:(th + 1) * 512],
                    ones_bf[:, 0:1], xc[:, th * 512:(th + 1) * 512],
                    start=(dcog == 0), stop=(dcog == 15), skip_group_check=True),
                    reads=["xc", "ones_bf"], writes=[f"P3:{th}"])
                b.op("tensor", lambda t, th=th, dcog=dcog: t.matmul(
                    P[1][0:1, th * 512:(th + 1) * 512],
                    ones_bf[:, 0:1], xsq[:, th * 512:(th + 1) * 512],
                    start=(dcog == 0), stop=(dcog == 15), skip_group_check=True),
                    reads=["xsq", "ones_bf"], writes=[f"P1:{th}"])
        ev = b.op("sync", lambda s, dcog=dcog, par=par: s.dma_start(
            out=xout[dcog * 128:(dcog + 1) * 128, :], in_=xb[par][:]),
            reads=[f"xb{par}"], writes=[xrg_out], dma=f"st_xo{par}")
        out_evs.append(ev)

    if ctx is None:
        b.finish(out_evs)
        b.emit(es)
        eph.close()
        es.close()
        return nc
    eph.close()
    return out_evs


def emit_ffn_scan_early(b, w1pa, w2pa, cc_in, cc_out, scanb, acc1, acc2):
    nb = len(scanb)
    for m, (pa, acc, accn) in enumerate([(w1pa, acc1, "facc1"), (w2pa, acc2, "facc2")]):
        for i in range(16):
            par = i % nb
            b.op("gpsimd", lambda s, pa=pa, i=i, par=par: s.dma_start(
                out=scanb[par][:, 0:1024], in_=pa[:, i * 1024:(i + 1) * 1024]),
                writes=[f"fscan{par}"], dma=f"fscan{par}")
            b.op("vector", lambda v, par=par, acc=acc, i=i: v.tensor_reduce(
                acc[:, i:i + 1], scanb[par][:, 0:1024], mybir.AxisListType.X, ALU.add,
                apply_absolute_value=True),
                reads=[f"fscan{par}"], writes=[accn])
        b.op("vector", lambda v, acc=acc: v.tensor_reduce(
            acc[:, 16:17], acc[:, 0:16], mybir.AxisListType.X, ALU.add),
            reads=[accn], writes=[accn])
    for m, acc, accn in [(0, acc1, "facc1"), (1, acc2, "facc2")]:
        b.op("gpsimd", lambda g, m=m, acc=acc: g.dma_start(
            out=cc_in.ap()[m:m + 1, :], in_=acc[:, 16:17]),
            reads=[accn], writes=["ccin_d"], dma="ccin")
    b.op("gpsimd", lambda g: g.collective_compute(
        "AllReduce", ALU.add, replica_groups=[list(range(8))],
        ins=[cc_in.ap()], outs=[cc_out.ap()]),
        reads=["ccin_d"], writes=["ccout_d"], cc="ccr")


def emit_ffn_scales(b, cc_out, cstbuf, srow, allred, inv1c, inv2c, P,
                    extra_reads=()):
    b.op("sync", lambda s: s.dma_start(out=allred[0:1, 0:256], in_=cc_out.ap()),
         reads=["ccout_d"], writes=["fallred"], dma="fallred")
    for m in range(2):
        b.op("vector", lambda v, m=m: v.reduce_sum(
            srow[0:1, m:m + 1], allred[0:1, m * 128:(m + 1) * 128],
            axis=mybir.AxisListType.X),
            reads=["fallred"], writes=["fsrow"])
    b.op("vector", lambda v: v.tensor_scalar(
        srow[0:1, 2:4], srow[0:1, 0:2], 1.0 / 16777216.0, 1e-8, ALU.mult, ALU.max),
        reads=["fsrow"], writes=["fsrow2"])
    b.op("vector", lambda v: v.reciprocal(srow[0:1, 2:4], srow[0:1, 2:4]),
         reads=["fsrow2"], writes=["fsrow2"])
    for m, col in [(0, inv1c), (1, inv2c)]:
        b.op("tensor", lambda t, m=m: t.matmul(
            P[3][:, 0:1], cstbuf[0:1, 8:136], srow[0:1, 2 + m:3 + m],
            start=True, stop=True),
            reads=["fsrow2", "cst"] + list(extra_reads), writes=["P3:0"])
        b.op("vector", lambda v, col=col, m=m: v.tensor_copy(col[:, 0:1], P[3][:, 0:1]),
             reads=["P3:0"], writes=[f"finv{m + 1}c"])


def build_ffn(ctx=None):
    import contextlib
    if ctx is None:
        nc = bass.Bass("TRN2", target_bir_lowering=False, debug=False)
        es = contextlib.ExitStack()
        b = None
        pref = ""
        dram = None
        xrg_in, xrg_out = "xin_d", "xout_d"
    else:
        nc, b, pref, dram = ctx["nc"], ctx["b"], ctx["pref"], ctx["dram"]
        xrg_in, xrg_out = ctx["x_in_rg"], ctx["x_out_rg"]

    def DR(name, shape, dt, kind):
        if dram is not None:
            return dram[name]
        return nc.dram_tensor(name, shape, dt, kind=kind).ap()

    xin = DR("xin", [D, TOWN], F32, "ExternalInput")
    w1 = DR("w1", [16, 16, 128, 512], F32, "ExternalInput")
    w2 = DR("w2", [4, 64, 128, 512], F32, "ExternalInput")
    w1pa = DR("w1pa", [128, 16384], F32, "ExternalInput")
    w2pa = DR("w2pa", [128, 16384], F32, "ExternalInput")
    b1c = DR("b1c", [128, 64], F32, "ExternalInput")
    b2c = DR("b2c", [128, 16], F32, "ExternalInput")
    consts = DR("consts", [128, 144], F32, "ExternalInput")
    xout = DR("xout", [D, TOWN], F32, "ExternalOutput")
    if dram is not None:
        h_dram, cc_in, cc_out = dram["h_dram"], dram["cc_in"], dram["cc_out"]
    else:
        h_dram = nc.dram_tensor("h_dram", [64, 128, TOWN], BF16)
        cc_in = nc.dram_tensor("cc_in", [2, 128], F32)
        cc_out = nc.dram_tensor("cc_out", [2, 128], F32)

    eph = contextlib.ExitStack()
    sb = lambda name, shape, dt: eph.enter_context(
        nc.sbuf_tensor(pref + name, shape, dt))
    xn_t = [sb(f"xn{i}", [128, TOWN], BF16) for i in range(16)]
    xb = [sb(f"xb{i}", [128, TOWN], F32) for i in range(2)]
    xo_sb = [sb(f"xo{i}", [128, TOWN], F32) for i in range(2)]
    xc = sb("lxc", [128, TOWN], BF16)
    xsq = sb("lxsq", [128, TOWN], BF16)
    rows = sb("rows", [1, 2048], F32)
    A_sb = sb("lnA", [128, TOWN], F32)
    B_sb = sb("lnB", [128, TOWN], F32)
    T_sb = sb("lnT", [128, TOWN], F32)
    cstbuf = sb("cstbuf", [128, 144], F32)
    ones_bf = sb("ones_bf", [128, 128], BF16)
    acc1 = sb("acc1", [128, 17], F32)
    acc2 = sb("acc2", [128, 17], F32)
    srow = sb("srow", [1, 4], F32)
    allred = sb("allred", [1, 256], F32)
    inv1c = sb("inv1c", [128, 1], F32)
    inv2c = sb("inv2c", [128, 1], F32)
    w1f = [sb(f"w1f{i}", [128, 4096], F32) for i in range(2)]
    q2h = [sb(f"q2h{i}", [128, 4096], BF16) for i in range(4)]
    scanb = w1f
    w2f = w1f
    q2b = q2h
    btmp = sb("btmp", [128, 4096], BF16)
    h2_sb = [sb(f"h2{i}", [128, TOWN], BF16) for i in range(2)]
    h_in = [sb(f"hin{i}", [128, TOWN], BF16) for i in range(8)]
    b1_sb = sb("b1_sb", [128, 64], F32)
    b2_sb = sb("b2_sb", [128, 16], F32)

    if ctx is None:
        P = [nc.alloc_psum_tensor(f"P{i}", [128, TOWN], F32) for i in range(4)]
        b = Builder(nc)
    else:
        P = ctx["P"]
        b.barrier()
    b.once_slot("cst", "ones_bf", "b1", "b2")
    b.op("sync", lambda s: s.dma_start(out=cstbuf[:], in_=consts),
         writes=["cst"], dma="cst")
    b.op("gpsimd", lambda g: g.dma_start(out=ones_bf[:], in_=consts[:, 8:136]),
         writes=["ones_bf"], dma="ones_bf")
    b.op("sync", lambda s: s.dma_start(out=b1_sb[:], in_=b1c),
         writes=["b1"], dma="b1")
    b.op("sync", lambda s: s.dma_start(out=b2_sb[:], in_=b2c),
         writes=["b2"], dma="b2")

    stats = ctx.get("stats") if ctx is not None else None
    mean = rows[0:1, 0:1024]
    var = rows[0:1, 1024:2048]
    if ctx is None or "scales_done" not in ctx:
        emit_ffn_scan_early(b, w1pa, w2pa, cc_in, cc_out, scanb, acc1, acc2)
    if stats is not None:
        for th in range(2):
            ap, rg = stats["sx"][th]
            b.op("vector", lambda v, th=th, ap=ap: v.tensor_scalar(
                mean[0:1, th * 512:(th + 1) * 512], ap,
                1.0 / 2048.0, None, ALU.mult),
                reads=[rg], writes=["mean"])
        b.op("vector", lambda v: v.tensor_tensor(var, mean, mean, ALU.mult),
             reads=["mean"], writes=["var"])
        for th in range(2):
            ap, rg = stats["sq"][th]
            b.op("vector", lambda v, th=th, ap=ap: v.scalar_tensor_tensor(
                var[0:1, th * 512:(th + 1) * 512], ap,
                1.0 / 2048.0, var[0:1, th * 512:(th + 1) * 512], ALU.mult, ALU.subtract),
                reads=[rg, "var"], writes=["var"])
    emit_ffn_scales(b, cc_out, cstbuf, srow, allred, inv1c, inv2c, P,
                    extra_reads=([stats["sx"][0][1], stats["sx"][1][1]]
                                 if stats is not None else ()))

    if stats is not None:
        pass
    else:
        for dc in range(16):
            par = dc % 2
            b.op("sync", lambda s, dc=dc, par=par: s.dma_start(
                out=xb[par][:], in_=xin[dc * 128:(dc + 1) * 128, :]),
                reads=[xrg_in], writes=[f"xb{par}"], dma=f"x{par}")
            b.op("scalar", lambda s, par=par: s.activation(
                out=xc[:], in_=xb[par][:], func=AF.Copy),
                reads=[f"xb{par}"], writes=["xc"])
            b.op("scalar", lambda s, par=par: s.activation(
                out=xsq[:], in_=xb[par][:], func=AF.Square),
                reads=[f"xb{par}"], writes=["xsq"])
            for th in range(2):
                b.op("tensor", lambda t, dc=dc, th=th: t.matmul(
                    P[0][0:1, th * 512:(th + 1) * 512],
                    ones_bf[:, 0:1], xc[:, th * 512:(th + 1) * 512],
                    start=(dc == 0), stop=(dc == 15), skip_group_check=True),
                    reads=["xc", "ones_bf"], writes=[f"P0:{th}"])
                b.op("tensor", lambda t, dc=dc, th=th: t.matmul(
                    P[1][0:1, th * 512:(th + 1) * 512],
                    ones_bf[:, 0:1], xsq[:, th * 512:(th + 1) * 512],
                    start=(dc == 0), stop=(dc == 15), skip_group_check=True),
                    reads=["xsq", "ones_bf"], writes=[f"P1:{th}"])
        for th in range(2):
            b.op("vector", lambda v, th=th: v.tensor_scalar(
                mean[0:1, th * 512:(th + 1) * 512], P[0][0:1, th * 512:(th + 1) * 512],
                1.0 / 2048.0, None, ALU.mult),
                reads=[f"P0:{th}"], writes=["mean"])
        b.op("vector", lambda v: v.tensor_tensor(var, mean, mean, ALU.mult),
             reads=["mean"], writes=["var"])
        for th in range(2):
            b.op("vector", lambda v, th=th: v.scalar_tensor_tensor(
                var[0:1, th * 512:(th + 1) * 512], P[1][0:1, th * 512:(th + 1) * 512],
                1.0 / 2048.0, var[0:1, th * 512:(th + 1) * 512], ALU.mult, ALU.subtract),
                reads=[f"P1:{th}", "var"], writes=["var"])
    b.op("scalar", lambda s: s.activation(
        out=var, in_=var, func=AF.Ln, bias=cstbuf[0:1, 1:2]),
        reads=["var", "cst"], writes=["var"])
    b.op("scalar", lambda s: s.activation(
        out=var, in_=var, func=AF.Exp, scale=-0.5),
        reads=["var"], writes=["var"])
    for th in range(2):
        b.op("tensor", lambda t, th=th: t.matmul(
            P[0][:, th * 512:(th + 1) * 512], cstbuf[0:1, 8:136],
            var[0:1, th * 512:(th + 1) * 512], start=True, stop=True),
            reads=["var", "cst"], writes=[f"P0:{th}"])
        b.op("vector", lambda v, th=th: v.tensor_copy(
            A_sb[:, th * 512:(th + 1) * 512], P[0][:, th * 512:(th + 1) * 512]),
            reads=[f"P0:{th}"], writes=["lnA"])
        b.op("tensor", lambda t, th=th: t.matmul(
            P[1][:, th * 512:(th + 1) * 512], cstbuf[0:1, 8:136],
            mean[0:1, th * 512:(th + 1) * 512], start=True, stop=True),
            reads=["mean", "cst"], writes=[f"P1:{th}"])
        b.op("vector", lambda v, th=th: v.tensor_scalar(
            B_sb[:, th * 512:(th + 1) * 512], P[1][:, th * 512:(th + 1) * 512],
            -1.0, None, ALU.mult),
            reads=[f"P1:{th}"], writes=["lnB"])
    for dc in range(16):
        par = dc % 2
        b.op("sync", lambda s, dc=dc, par=par: s.dma_start(
            out=xb[par][:], in_=xin[dc * 128:(dc + 1) * 128, :]),
            reads=[xrg_in], writes=[f"xb{par}"], dma=f"x{par}")
        b.op("vector", lambda v, par=par: v.tensor_tensor(
            T_sb[:], xb[par][:], A_sb[:], ALU.mult),
            reads=[f"xb{par}", "lnA"], writes=["lnT"])
        b.op("vector", lambda v, dc=dc: v.tensor_tensor(
            xn_t[dc][:], T_sb[:], B_sb[:], ALU.add),
            reads=["lnT", "lnB"], writes=[f"xn{dc}"])

    def quant(src_buf, src_rg, dst_buf, dst_rg, invcol, invrg):
        b.op("scalar", lambda s: s.activation(
            out=dst_buf[:], in_=src_buf[:], func=AF.Sign,
            scale=invcol[:, 0:1], bias=cstbuf[:, 4:5]),
            reads=[src_rg, invrg, "cst"], writes=[dst_rg])
        b.op("scalar", lambda s: s.activation(
            out=btmp[:], in_=src_buf[:], func=AF.Sign,
            scale=invcol[:, 0:1], bias=cstbuf[:, 3:4]),
            reads=[src_rg, invrg, "cst"], writes=["btmp"])
        b.op("vector", lambda v: v.tensor_tensor(
            dst_buf[:], dst_buf[:], btmp[:], ALU.add),
            reads=[dst_rg, "btmp"], writes=[dst_rg])

    for fg in range(16):
        qb = (fg % 2) * 2
        for half in range(2):
            par = half
            b.op("gpsimd", lambda s, fg=fg, half=half, par=par: s.dma_start(
                out=w1f[par][:].rearrange("p (c f) -> p c f", c=8),
                in_=w1[fg, half * 8:(half + 1) * 8].rearrange("c p f -> p c f")),
                writes=[f"w1f{par}"], dma=f"w1f{par}")
            quant(w1f[par], f"w1f{par}", q2h[qb + par], f"q2h{qb + par}", inv1c, "finv1c")
        for fsub in range(4):
            fc = fg * 4 + fsub
            pp = fsub % 2
            for th in range(2):
                for dc in range(16):
                    b.op("tensor", lambda t, dc=dc, th=th, fsub=fsub, pp=pp, qb=qb: t.matmul(
                        P[pp][:, th * 512:(th + 1) * 512],
                        q2h[qb + dc // 8][:, (dc % 8) * 512 + fsub * 128: (dc % 8) * 512 + fsub * 128 + 128],
                        xn_t[dc][:, th * 512:(th + 1) * 512],
                        start=(dc == 0), stop=(dc == 15)),
                        reads=[f"q2h{qb + dc // 8}", f"xn{dc}"], writes=[f"P{pp}:{th}"])
            hp = fc % 2
            b.op("scalar", lambda s, fc=fc, pp=pp, hp=hp: s.activation(
                out=h2_sb[hp][:], in_=P[pp][:, :], func=AF.Gelu,
                scale=0.5, bias=b1_sb[:, fc:fc + 1]),
                reads=[f"P{pp}:0", f"P{pp}:1", "b1"], writes=[f"h2{hp}"])
            b.op("sync", lambda s, fc=fc, hp=hp: s.dma_start(
                out=h_dram[fc], in_=h2_sb[hp][:]),
                reads=[f"h2{hp}"], writes=[f"hd{fc}"], dma=f"st_h{hp}")

    out_evs = []
    for S in range(4):
        for f8 in range(8):
            par = f8 % 2
            qp = f8 % 4
            b.op("gpsimd", lambda s, S=S, f8=f8, par=par: s.dma_start(
                out=w2f[par][:].rearrange("p (c f) -> p c f", c=8),
                in_=w2[S, f8 * 8:(f8 + 1) * 8].rearrange("c p f -> p c f")),
                writes=[f"w1f{par}"], dma=f"w1f{par}")
            quant(w2f[par], f"w1f{par}", q2b[qp], f"q2h{qp}", inv2c, "finv2c")
            if f8 < 7:
                for fc_l in range(8):
                    fcg = f8 * 8 + fc_l
                    hp = fcg % 8
                    b.op("sync", lambda s, fcg=fcg, hp=hp: s.dma_start(
                        out=h_in[hp][:], in_=h_dram[fcg]),
                        reads=[f"hd{fcg}"], writes=[f"hin{hp}"], dma=f"hi{hp}")
                    for dco in range(4):
                        for th in range(2):
                            b.op("tensor", lambda t, fc_l=fc_l, dco=dco, th=th, qp=qp, hp=hp, fcg=fcg: t.matmul(
                                P[dco][:, th * 512:(th + 1) * 512],
                                q2b[qp][:, fc_l * 512 + dco * 128: fc_l * 512 + dco * 128 + 128],
                                h_in[hp][:, th * 512:(th + 1) * 512],
                                start=(fcg == 0), stop=False),
                                reads=[f"q2h{qp}", f"hin{hp}"], writes=[f"P{dco}:{th}"])
            else:
                for fc_l in range(8):
                    fcg = f8 * 8 + fc_l
                    hp = fcg % 8
                    b.op("sync", lambda s, fcg=fcg, hp=hp: s.dma_start(
                        out=h_in[hp][:], in_=h_dram[fcg]),
                        reads=[f"hd{fcg}"], writes=[f"hin{hp}"], dma=f"hi{hp}")
                for dco in range(4):
                    for fc_l in range(8):
                        fcg = f8 * 8 + fc_l
                        hp = fcg % 8
                        for th in range(2):
                            b.op("tensor", lambda t, fc_l=fc_l, dco=dco, th=th, qp=qp, hp=hp: t.matmul(
                                P[dco][:, th * 512:(th + 1) * 512],
                                q2b[qp][:, fc_l * 512 + dco * 128: fc_l * 512 + dco * 128 + 128],
                                h_in[hp][:, th * 512:(th + 1) * 512],
                                start=False, stop=(fc_l == 7)),
                                reads=[f"q2h{qp}", f"hin{hp}"], writes=[f"P{dco}:{th}"])
        for dco in range(4):
            dcog = S * 4 + dco
            par = dco % 2
            b.op("sync", lambda s, dcog=dcog, par=par: s.dma_start(
                out=xb[par][:], in_=xin[dcog * 128:(dcog + 1) * 128, :]),
                reads=[xrg_in], writes=[f"xb{par}"], dma=f"x{par}")
            b.op("vector", lambda v, dcog=dcog, par=par: v.tensor_scalar(
                xb[par][:], xb[par][:], b2_sb[:, dcog:dcog + 1], None, ALU.add),
                reads=[f"xb{par}", "b2"], writes=[f"xb{par}"])
            b.op("vector", lambda v, dco=dco, par=par: v.scalar_tensor_tensor(
                xo_sb[par][:], P[dco][:, :], 0.5, xb[par][:], ALU.mult, ALU.add),
                reads=[f"P{dco}:0", f"P{dco}:1", f"xb{par}"], writes=[f"xo{par}"])
            ev = b.op("sync", lambda s, dcog=dcog, par=par: s.dma_start(
                out=xout[dcog * 128:(dcog + 1) * 128, :], in_=xo_sb[par][:]),
                reads=[f"xo{par}"], writes=[xrg_out], dma=f"st_xo{par}")
            out_evs.append(ev)

    if ctx is None:
        b.finish(out_evs)
        b.emit(es)
        eph.close()
        es.close()
        return nc
    eph.close()
    return out_evs


def build_mega():
    import contextlib
    nc = bass.Bass("TRN2", target_bir_lowering=False, debug=False)
    es = contextlib.ExitStack()

    def EI(name, shape, dt):
        return nc.dram_tensor(name, shape, dt, kind="ExternalInput").ap()

    xin0 = EI("xin0", [D, TOWN], F32)
    consts = EI("consts", [128, 144], F32)
    ident = EI("ident", [128, 128], F32)
    l0 = {n: EI(f"l0_{n}", [4, 16, 128, 512], F32) for n in ("wq", "wk", "wv", "wo")}
    bcols = EI("bcols", [128, 48], F32)
    bvrow = EI("bvrow", [1, D], F32)
    bocols = EI("bocols", [128, 16], F32)
    wg_m = EI("wg_m", [16, 128, 8], F32)
    mbias = EI("mbias", [8, 1], F32)
    w1m = EI("w1m", [8, 4, 16, 128, 256], F32)
    w3m = EI("w3m", [8, 4, 16, 128, 256], F32)
    w2m = EI("w2m", [8, 4, 8, 128, 512], F32)
    l1 = {n: EI(f"l1_{n}", [4, 16, 128, 512], F32) for n in ("wq", "wk", "wv", "wg", "wo")}
    cosin = EI("cosin", [128, 2 * TOWN], F32)
    protm = EI("protm", [128, 128], F32)
    normw = EI("normw", [128, 1], F32)
    maskA = EI("maskA", [128, 128], F32)
    maskB = EI("maskB", [128, 128], F32)
    w1f_ = EI("w1f_", [16, 16, 128, 512], F32)
    w2f_ = EI("w2f_", [4, 64, 128, 512], F32)
    w1pa = EI("w1pa", [128, 16384], F32)
    w2pa = EI("w2pa", [128, 16384], F32)
    b1c = EI("b1c", [128, 64], F32)
    b2c = EI("b2c", [128, 16], F32)

    xout_ext = nc.dram_tensor("xout_ext", [D, TOWN], F32, kind="ExternalOutput").ap()

    xa = nc.dram_tensor("xa", [D, TOWN], F32).ap()
    xbd = nc.dram_tensor("xbd", [D, TOWN], F32).ap()
    q_st = nc.dram_tensor("q_st", [NH, 128, TOWN], BF16).ap()
    g_st = nc.dram_tensor("g_st", [NH, 128, TOWN], BF16).ap()
    k_own = nc.dram_tensor("k_own", [2048, TOWN], BF16)
    v_own = nc.dram_tensor("v_own", [2048, TOWN], BF16)
    k_gth = nc.dram_tensor("k_gth", [4096, TOWN], BF16)
    v_gth = nc.dram_tensor("v_gth", [4096, TOWN], BF16)
    h_dram = nc.dram_tensor("h_dram", [64, 128, TOWN], BF16)
    cc_in = nc.dram_tensor("cc_in", [2, 128], F32)
    cc_out = nc.dram_tensor("cc_out", [2, 128], F32)

    kv_k = k_own.ap().rearrange("(h p) t -> h p t", h=16)
    kv_v = v_own.ap().rearrange("(a b) c -> a (b c)", b=2)

    P = [nc.alloc_psum_tensor(f"P{i}", [128, TOWN], F32) for i in range(4)]
    b = Builder(nc)

    m_scan = [nc.alloc_sbuf_tensor(f"m_scan{i}", [128, 1024], F32) for i in range(2)]
    m_acc1 = nc.alloc_sbuf_tensor("m_acc1", [128, 17], F32)
    m_acc2 = nc.alloc_sbuf_tensor("m_acc2", [128, 17], F32)

    out_evs = None
    xseq = {0: (xin0, "x_ext", xa, "xa_d"),
            1: (xbd, "xbd_d", xa, "xa_d")}
    attn_stats = {"sx": [(P[3][0:1, 0:512], "R0"), (P[3][0:1, 512:1024], "R1")],
                  "sq": [(P[1][0:1, 0:512], "SC1:0"), (P[1][0:1, 512:1024], "SC1:1")]}
    moe_stats = {"sx": [(P[3][0:1, 0:512], "P3:0"), (P[3][0:1, 512:1024], "P3:1")],
                 "sq": [(P[1][0:1, 0:512], "P1:0"), (P[1][0:1, 512:1024], "P1:1")]}
    for layer in (0, 1):
        x_in, x_in_rg, x_mid, x_mid_rg = xseq[layer]
        dram = {"xin": x_in, "consts": consts,
                "q_out": q_st, "k_out": kv_k, "v_out": kv_v}
        if layer == 0:
            dram.update(wq=l0["wq"], wk=l0["wk"], wv=l0["wv"],
                        bcols=bcols, bvrow=bvrow)
        else:
            dram.update(wq=l1["wq"], wk=l1["wk"], wv=l1["wv"], wg=l1["wg"],
                        cosin=cosin, protm=protm, g_out=g_st)
        pctx = dict(nc=nc, b=b, P=P, pref=f"p{layer}_",
                    dram=dram, x_in_rg=x_in_rg,
                    x_out_rg="unused", kv_gath=None,
                    k_own=k_own.ap(), v_own=v_own.ap(),
                    k_gth=k_gth.ap(), v_gth=v_gth.ap())
        if layer == 1:
            pctx["stats"] = moe_stats
        build_proj(layer, 0, ctx=pctx)

        dram = {"xin": x_in, "q_in": q_st, "consts": consts,
                "wo": l0["wo"] if layer == 0 else l1["wo"],
                "xout": x_mid}
        if layer == 0:
            dram["bocols"] = bocols
        else:
            dram.update(g_in=g_st, normw=normw, maskA=maskA, maskB=maskB)
        if layer == 0:
            emit_ffn_scan_early(b, w1pa, w2pa, cc_in, cc_out, m_scan,
                                m_acc1, m_acc2)
        build_attn(layer, ctx=dict(nc=nc, b=b, P=P, pref=f"a{layer}_",
                                   dram=dram, x_in_rg=x_in_rg,
                                   x_out_rg=x_mid_rg,
                                   kv_gath=(k_gth.ap(), v_gth.ap())))
        if layer == 0:
            dram = {"xin": x_mid, "wg": wg_m, "mbias": mbias, "w1": w1m,
                    "w3": w3m, "w2": w2m, "ident": ident, "consts": consts,
                    "xout": xbd}
            build_moe(ctx=dict(nc=nc, b=b, P=P, pref="m_", dram=dram,
                               x_in_rg=x_mid_rg, x_out_rg="xbd_d",
                               stats=attn_stats))
        else:
            dram = {"xin": x_mid, "w1": w1f_, "w2": w2f_, "w1pa": w1pa,
                    "w2pa": w2pa, "b1c": b1c, "b2c": b2c, "consts": consts,
                    "xout": xout_ext, "h_dram": h_dram, "cc_in": cc_in,
                    "cc_out": cc_out}
            out_evs = build_ffn(ctx=dict(nc=nc, b=b, P=P, pref="f_", dram=dram,
                                         x_in_rg=x_mid_rg, x_out_rg="xfin_d",
                                         scales_done=True, stats=attn_stats))

    b.finish(out_evs)
    b.emit(es)
    es.close()
    return nc


def _own_token_idx(core):
    blocks = EVEN_BLOCKS if core % 2 == 0 else ODD_BLOCKS
    return np.concatenate([np.arange(blk * 128, blk * 128 + 128) for blk in blocks])


def _consts_array():
    c = np.zeros((128, 144), dtype=np.float32)
    c[:, 0] = -CSHIFT
    c[:, 1] = 1e-5
    c[:, 2] = 1e-5
    c[:, 3] = 0.5
    c[:, 4] = -0.5
    c[:, 8:136] = 1.0
    return c


def _block_w(wT):
    return np.ascontiguousarray(
        wT.reshape(16, 128, 4, 512).transpose(2, 0, 1, 3))


def _rope_tables(core):
    idx = _own_token_idx(core).astype(np.float32)
    inv_freq = 1.0 / (10000.0 ** (np.arange(0, HD, 2, dtype=np.float32) / HD))
    freqs = np.outer(idx, inv_freq)
    emb = np.concatenate([freqs, freqs], axis=1)
    cos = np.cos(emb).T.astype(np.float32)
    sin = np.sin(emb).T.astype(np.float32)
    return np.ascontiguousarray(np.concatenate([cos, sin], axis=1))


def _rot_matrix():
    P = np.zeros((128, 128), dtype=np.float32)
    for m in range(64):
        P[m + 64, m] = -1.0
    for m in range(64, 128):
        P[m - 64, m] = 1.0
    return P


def _prep_shared(inp):
    d = {}
    d["l0_wq"] = _block_w(np.ascontiguousarray(inp["L0_attn_Wq"].T))
    d["l0_wk"] = _block_w(np.ascontiguousarray(inp["L0_attn_Wk"].T))
    d["l0_wv"] = _block_w(np.ascontiguousarray(inp["L0_attn_Wv"].T))
    d["l0_wo"] = _block_w(np.ascontiguousarray(inp["L0_attn_Wo"].T))
    bcols = np.zeros((128, 48), np.float32)
    bcols[:, 0:16] = inp["L0_attn_bq"].reshape(16, 128).T
    bcols[:, 16:32] = inp["L0_attn_bk"].reshape(16, 128).T
    d["l0_bcols"] = bcols
    d["l0_bvrow"] = np.ascontiguousarray(inp["L0_attn_bv"].reshape(1, -1))
    d["l0_bocols"] = np.ascontiguousarray(inp["L0_attn_bo"].reshape(16, 128).T)
    d["wg"] = np.ascontiguousarray(inp["L0_ffn_Wg"].T.reshape(16, 128, 8))
    d["mbias"] = np.ascontiguousarray(inp["L0_ffn_bias"].reshape(8, 1))
    d["w1"] = np.ascontiguousarray(inp["L0_ffn_W1"].transpose(0, 2, 1).reshape(8, 16, 128, 4, 256).transpose(0, 3, 1, 2, 4))
    d["w3"] = np.ascontiguousarray(inp["L0_ffn_W3"].transpose(0, 2, 1).reshape(8, 16, 128, 4, 256).transpose(0, 3, 1, 2, 4))
    d["w2"] = np.ascontiguousarray(inp["L0_ffn_W2"].transpose(0, 2, 1).reshape(8, 8, 128, 4, 512).transpose(0, 3, 1, 2, 4))
    d["l1_wq"] = _block_w(np.ascontiguousarray(inp["L1_attn_wq"].T))
    d["l1_wk"] = _block_w(np.ascontiguousarray(inp["L1_attn_wk"].T))
    d["l1_wv"] = _block_w(np.ascontiguousarray(inp["L1_attn_wv"].T))
    d["l1_wg"] = _block_w(np.ascontiguousarray(inp["L1_attn_wg"].T))
    d["l1_wo"] = _block_w(np.ascontiguousarray(inp["L1_attn_wo"].T))
    d["normw"] = np.ascontiguousarray(inp["L1_attn_norm_w"].reshape(128, 1))
    fw1 = inp["L1_ffn_w1"]
    fw2 = inp["L1_ffn_w2"]
    d["f_w1"] = np.ascontiguousarray(fw1.T.reshape(16, 128, 16, 512).transpose(2, 0, 1, 3))
    d["f_w2"] = np.ascontiguousarray(fw2.T.reshape(64, 128, 4, 512).transpose(2, 0, 1, 3))
    n = 16777216 // 8
    d["f_w1pa"] = [np.ascontiguousarray(fw1.reshape(-1)[c * n:(c + 1) * n].reshape(128, -1)) for c in range(8)]
    d["f_w2pa"] = [np.ascontiguousarray(fw2.reshape(-1)[c * n:(c + 1) * n].reshape(128, -1)) for c in range(8)]
    d["f_b1c"] = np.ascontiguousarray(inp["L1_ffn_b1"].reshape(64, 128).T)
    d["f_b2c"] = np.ascontiguousarray(inp["L1_ffn_b2"].reshape(16, 128).T)
    d["consts"] = _consts_array()
    d["ident"] = np.eye(128, dtype=np.float32)
    tri = np.where(np.arange(128)[:, None] > np.arange(128)[None, :],
                   np.float32(-1e30), np.float32(0.0))
    d["tri"] = tri
    d["fullmask"] = np.full((128, 128), -1e30, np.float32)
    d["zeromask"] = np.zeros((128, 128), np.float32)
    return d


_NC_CACHE = {}


def _get_nc(name, fn):
    if name not in _NC_CACHE:
        _NC_CACHE[name] = fn()
    return _NC_CACHE[name]


FUSED = True


def kernel(**inputs):
    if FUSED:
        return kernel_fused(**inputs)
    return kernel_unfused(**inputs)


def kernel_fused(**inputs):
    P = _prep_shared(inputs)
    nc = _get_nc("mega", build_mega)
    in_maps = []
    for c in range(8):
        im = {
            "xin0": np.ascontiguousarray(
                inputs["L0_attn_x"][c // 2][_own_token_idx(c)].T),
            "consts": P["consts"], "ident": P["ident"],
            "l0_wq": P["l0_wq"], "l0_wk": P["l0_wk"], "l0_wv": P["l0_wv"],
            "l0_wo": P["l0_wo"], "bcols": P["l0_bcols"], "bvrow": P["l0_bvrow"],
            "bocols": P["l0_bocols"],
            "wg_m": P["wg"], "mbias": P["mbias"], "w1m": P["w1"],
            "w3m": P["w3"], "w2m": P["w2"],
            "l1_wq": P["l1_wq"], "l1_wk": P["l1_wk"], "l1_wv": P["l1_wv"],
            "l1_wg": P["l1_wg"], "l1_wo": P["l1_wo"],
            "cosin": _rope_tables(c), "protm": _rot_matrix(),
            "normw": P["normw"],
            "maskA": P["tri"] if c % 2 == 0 else P["zeromask"],
            "maskB": P["fullmask"] if c % 2 == 0 else P["tri"],
            "w1f_": P["f_w1"], "w2f_": P["f_w2"],
            "w1pa": P["f_w1pa"][c], "w2pa": P["f_w2pa"][c],
            "b1c": P["f_b1c"], "b2c": P["f_b2c"],
        }
        in_maps.append(im)
    res = run_bass_kernel_spmd(nc, in_maps, core_ids=list(range(8)))
    out = np.empty((4, TFULL, D), dtype=np.float32)
    for c in range(8):
        out[c // 2, _own_token_idx(c), :] = np.asarray(res.results[c]["xout_ext"]).T
    return out


def kernel_unfused(**inputs):
    P = _prep_shared(inputs)
    x_fm = [np.ascontiguousarray(
        inputs["L0_attn_x"][c // 2][_own_token_idx(c)].T) for c in range(8)]

    def run(nc, in_maps):
        res = run_bass_kernel_spmd(nc, in_maps, core_ids=list(range(8)))
        return res.results

    for layer in (0, 1):
        nc = _get_nc(f"proj{layer}", lambda layer=layer: build_proj(layer, 0))
        ims = []
        for c in range(8):
            im = {"xin": x_fm[c], "consts": P["consts"]}
            if layer == 0:
                im.update(wq=P["l0_wq"], wk=P["l0_wk"], wv=P["l0_wv"],
                          bcols=P["l0_bcols"], bvrow=P["l0_bvrow"])
            else:
                im.update(wq=P["l1_wq"], wk=P["l1_wk"], wv=P["l1_wv"],
                          wg=P["l1_wg"], cosin=_rope_tables(c), protm=_rot_matrix())
            ims.append(im)
        r = run(nc, ims)
        kf, vf, qf, gf = {}, {}, {}, {}
        for p in range(4):
            e, o = 2 * p, 2 * p + 1
            kf[p] = np.concatenate([r[e]["k_out"], r[o]["k_out"]], axis=2)
            vf[p] = np.concatenate([r[e]["v_out"], r[o]["v_out"]], axis=0)
        for c in range(8):
            qf[c] = r[c]["q_out"]
            if layer == 1:
                gf[c] = r[c]["g_out"]
        nc = _get_nc(f"attn{layer}", lambda layer=layer: build_attn(layer))
        ims = []
        for c in range(8):
            im = {"xin": x_fm[c], "q_in": qf[c], "k_in": kf[c // 2],
                  "v_in": vf[c // 2], "consts": P["consts"],
                  "wo": P["l0_wo"] if layer == 0 else P["l1_wo"]}
            if layer == 0:
                im["bocols"] = P["l0_bocols"]
            else:
                im.update(g_in=gf[c], normw=P["normw"],
                          maskA=P["tri"] if c % 2 == 0 else P["zeromask"],
                          maskB=P["fullmask"] if c % 2 == 0 else P["tri"])
            ims.append(im)
        r = run(nc, ims)
        x_fm = [np.asarray(r[c]["xout"]) for c in range(8)]
        if layer == 0:
            nc = _get_nc("moe", build_moe)
            ims = [{"xin": x_fm[c], "wg": P["wg"], "mbias": P["mbias"],
                    "w1": P["w1"], "w3": P["w3"], "w2": P["w2"],
                    "ident": P["ident"], "consts": P["consts"]}
                   for c in range(8)]
        else:
            nc = _get_nc("ffn", build_ffn)
            ims = [{"xin": x_fm[c], "w1": P["f_w1"], "w2": P["f_w2"],
                    "w1pa": P["f_w1pa"][c], "w2pa": P["f_w2pa"][c],
                    "b1c": P["f_b1c"], "b2c": P["f_b2c"],
                    "consts": P["consts"]} for c in range(8)]
        r = run(nc, ims)
        x_fm = [np.asarray(r[c]["xout"]) for c in range(8)]

    out = np.empty((4, TFULL, D), dtype=np.float32)
    for c in range(8):
        out[c // 2, _own_token_idx(c), :] = x_fm[c].T
    return out


# version 13
# speedup vs baseline: 1.0107x; 1.0074x over previous
import math
import numpy as np
import concourse.bass as bass
import concourse.mybir as mybir
from concourse.bass_utils import run_bass_kernel_spmd

F32 = mybir.dt.float32
BF16 = mybir.dt.bfloat16
AF = mybir.ActivationFunctionType
ALU = mybir.AluOpType

D = 2048
NH = 16
HD = 128
TOWN = 1024
TFULL = 2048
NB = 8
EVEN_BLOCKS = [0, 2, 4, 6, 8, 10, 12, 14]
ODD_BLOCKS = [1, 3, 5, 7, 9, 11, 13, 15]
CSHIFT = 20.0
SCALE_ATT = 1.0 / math.sqrt(HD)

ENGINES = ("tensor", "vector", "scalar", "gpsimd", "sync")
DMA_QUEUES = ("sync", "scalar", "gpsimd")


class Builder:

    def __init__(self, nc):
        self.nc = nc
        self.items = {e: [] for e in ENGINES}
        self.last_w = {}
        self.readers = {}
        self.slot_ord = {}
        self.once = set()
        self.cc_ord = {}
        self.final_waits = []
        self.group_end = {}
        self.open_group = {}
        self.slot_last_ev = {}
        self.last_cc_ev = {}
        self.pending_bar = {}

    def once_slot(self, *names):
        self.once.update(names)

    def barrier(self):
        evs = set()
        for e in ENGINES:
            for it in reversed(self.items[e]):
                if not it["dma"] and not it["cc"]:
                    evs.add(it["ev"])
                    break
        evs.update(self.slot_last_ev.values())
        evs.update(self.last_cc_ev.values())
        for e in ENGINES:
            self.pending_bar[e] = set(evs)

    def op(self, eng, fn, reads=(), writes=(), dma=None, cc=False):
        idx = len(self.items[eng])
        if cc:
            ch = cc if isinstance(cc, str) else "cc0"
            o = self.cc_ord.get(ch, 0)
            self.cc_ord[ch] = o + 1
            ev = ("cc", ch, o)
            self.last_cc_ev[ch] = ev
        elif dma is not None:
            o = self.slot_ord.get(dma, 0)
            self.slot_ord[dma] = o + 1
            ev = ("dma", dma, o)
            self.slot_last_ev[dma] = ev
            if dma in self.once:
                self.open_group.setdefault(dma, []).append(o)
        else:
            ev = ("cmp", eng, idx)
        deps = set()
        bar = self.pending_bar.pop(eng, None)
        if bar:
            deps.update(bar)
        for r in reads:
            lw = self.last_w.get(r)
            if lw is not None:
                deps.add(lw)
            self.readers.setdefault(r, []).append(ev)
        for w in writes:
            lw = self.last_w.get(w)
            if lw is not None:
                deps.add(lw)
            for rd in self.readers.get(w, ()):
                if rd != ev:
                    deps.add(rd)
            self.readers[w] = []
            self.last_w[w] = ev
        if eng == "tensor":
            deps = {d for d in deps if not (d[0] == "cmp" and d[1] == "tensor")}
        for d in deps:
            if d[0] == "dma" and d[1] in self.once:
                grp = self.open_group.get(d[1])
                if grp:
                    end = self.slot_ord[d[1]] - 1
                    for o in grp:
                        self.group_end[(d[1], o)] = end
                    self.open_group[d[1]] = []
        self.items[eng].append(
            dict(fn=fn, deps=deps, ev=ev, dma=dma, cc=cc, inc=False))
        return ev

    def finish(self, events):
        self.final_waits.extend(events)

    def emit(self, es):
        nc = self.nc
        sem_cmp = {e: es.enter_context(nc.semaphore(f"s_{e}")) for e in ENGINES}
        sem_dma = {s: es.enter_context(nc.semaphore(f"d{i}"))
                   for i, s in enumerate(sorted(self.slot_ord))}
        sem_cc = {ch: es.enter_context(nc.semaphore(f"s_cc_{ch}"))
                  for ch in sorted(self.cc_ord)}

        need = set()
        for e in ENGINES:
            for it in self.items[e]:
                for d in it["deps"]:
                    if d[0] == "cmp":
                        need.add((d[1], d[2]))
        for d in self.final_waits:
            if d[0] == "cmp":
                need.add((d[1], d[2]))
        prefix = {}
        for e in ENGINES:
            c = 0
            vals = []
            for i, it in enumerate(self.items[e]):
                if it["dma"] is not None or it["cc"]:
                    vals.append(None)
                    continue
                if (e, i) in need:
                    it["inc"] = True
                    c += 1
                vals.append(c)
            prefix[e] = vals

        def ev_wait(ev):
            kind = ev[0]
            if kind == "cmp":
                return (sem_cmp[ev[1]], prefix[ev[1]][ev[2]])
            if kind == "dma":
                slot, o = ev[1], ev[2]
                if slot in self.once:
                    end = self.group_end.get((slot, o), o)
                    return (sem_dma[slot], 16 * (end + 1))
                return (sem_dma[slot], 16 * (o + 1))
            return (sem_cc[ev[1]], ev[2] + 1)

        def section(eng_name):
            items = self.items[eng_name]
            waited = {}

            def do_waits(eng, deps):
                ws = {}
                for d in deps:
                    sem, val = ev_wait(d)
                    key = id(sem)
                    if val > ws.get(key, (None, -1))[1]:
                        ws[key] = (sem, val)
                for sem, val in ws.values():
                    if waited.get(id(sem), -1) >= val:
                        continue
                    eng.wait_ge(sem, val)
                    waited[id(sem)] = val

            def run(eng):
                for it in items:
                    do_waits(eng, it["deps"])
                    inst = it["fn"](eng)
                    if it["cc"]:
                        ch = it["cc"] if isinstance(it["cc"], str) else "cc0"
                        inst.then_inc(sem_cc[ch], 1)
                    elif it["dma"] is not None:
                        inst.then_inc(sem_dma[it["dma"]], 16)
                    elif it["inc"]:
                        inst.then_inc(sem_cmp[eng_name], 1)
                if eng_name == "sync":
                    do_waits(eng, self.final_waits)
            return run

        with nc.Block() as block:
            block.sync(section("sync"))
            block.scalar(section("scalar"))
            block.vector(section("vector"))
            block.tensor(section("tensor"))
            block.gpsimd(section("gpsimd"))


def emit_ln(b, x_t, xn_t, cst, scr, ps, tag, pre=None):
    nc = b.nc
    xc, xsq = scr["ln_xc"], scr["ln_xsq"]
    rows = scr["rows"]
    A, Bb = scr["ln_A"], scr["ln_B"]
    R0, R1 = ps["R0"], ps["R1"]
    PB = ps["B"]
    PBname = ps["Bname"]

    mean = rows[0:1, 0:1024]
    var = rows[0:1, 1024:2048]
    if pre is None:
        for dc in range(16):
            b.op("scalar", lambda s, dc=dc: s.activation(
                out=xc[:], in_=x_t[dc][:], func=AF.Copy),
                reads=[f"x{dc}"], writes=[f"{tag}xc"])
            for th in range(2):
                b.op("tensor", lambda t, dc=dc, th=th: t.matmul(
                    (R0 if th == 0 else R1)[0:1, :],
                    cst["ones_bf"][:, 0:1], xc[:, th * 512:(th + 1) * 512],
                    start=(dc == 0), stop=(dc == 15)),
                    reads=[f"{tag}xc", "ones_bf"], writes=[f"psR{th}"])
        sx_src = [((R0 if th == 0 else R1)[0:1, :], f"psR{th}") for th in range(2)]
    else:
        sx_src = pre["sx"]
    for th in range(2):
        ap, rg = sx_src[th]
        b.op("vector", lambda v, th=th, ap=ap: v.tensor_scalar(
            mean[0:1, th * 512:(th + 1) * 512], ap,
            1.0 / 2048.0, None, ALU.mult),
            reads=[rg], writes=[f"{tag}mean"])
    if pre is None:
        for dc in range(16):
            b.op("scalar", lambda s, dc=dc: s.activation(
                out=xsq[:], in_=x_t[dc][:], func=AF.Square),
                reads=[f"x{dc}"], writes=[f"{tag}xsq"])
            for th in range(2):
                b.op("tensor", lambda t, dc=dc, th=th: t.matmul(
                    (R0 if th == 0 else R1)[0:1, :],
                    cst["ones_bf"][:, 0:1], xsq[:, th * 512:(th + 1) * 512],
                    start=(dc == 0), stop=(dc == 15)),
                    reads=[f"{tag}xsq", "ones_bf"], writes=[f"psR{th}"])
        sq_src = [((R0 if th == 0 else R1)[0:1, :], f"psR{th}") for th in range(2)]
    else:
        sq_src = pre["sq"]
    b.op("vector", lambda v: v.tensor_tensor(var, mean, mean, ALU.mult),
        reads=[f"{tag}mean"], writes=[f"{tag}var"])
    for th in range(2):
        ap, rg = sq_src[th]
        b.op("vector", lambda v, th=th, ap=ap: v.scalar_tensor_tensor(
            var[0:1, th * 512:(th + 1) * 512], ap,
            1.0 / 2048.0, var[0:1, th * 512:(th + 1) * 512],
            ALU.mult, ALU.subtract),
            reads=[rg, f"{tag}var"], writes=[f"{tag}var"])
    b.op("scalar", lambda s: s.activation(
        out=var, in_=var, func=AF.Ln, bias=cst["ccol_eps_row"]),
        reads=[f"{tag}var", "cst"], writes=[f"{tag}var"])
    b.op("scalar", lambda s: s.activation(
        out=var, in_=var, func=AF.Exp, scale=-0.5),
        reads=[f"{tag}var"], writes=[f"{tag}var"])
    b.op("vector", lambda v: v.scalar_tensor_tensor(
        mean, mean, -1.0, var, ALU.mult, ALU.mult),
        reads=[f"{tag}mean", f"{tag}var"], writes=[f"{tag}mean"])
    for th in range(2):
        b.op("tensor", lambda t, th=th: t.matmul(
            PB[:, th * 512:(th + 1) * 512], cst["ones_row"],
            var[0:1, th * 512:(th + 1) * 512], start=True, stop=True),
            reads=[f"{tag}var", "cst"], writes=[f"{PBname}:{th}"])
        b.op("vector", lambda v, th=th: v.tensor_copy(
            A[:, th * 512:(th + 1) * 512], PB[:, th * 512:(th + 1) * 512]),
            reads=[f"{PBname}:{th}"], writes=[f"{tag}A"])
        b.op("tensor", lambda t, th=th: t.matmul(
            PB[:, th * 512:(th + 1) * 512], cst["ones_row"],
            mean[0:1, th * 512:(th + 1) * 512], start=True, stop=True),
            reads=[f"{tag}mean"], writes=[f"{PBname}:{th}"])
        b.op("vector", lambda v, th=th: v.tensor_copy(
            Bb[:, th * 512:(th + 1) * 512], PB[:, th * 512:(th + 1) * 512]),
            reads=[f"{PBname}:{th}"], writes=[f"{tag}B"])
    for dc in range(16):
        b.op("vector", lambda v, dc=dc: v.tensor_tensor(
            scr["ln_t"][:], x_t[dc][:], A[:], ALU.mult),
            reads=[f"x{dc}", f"{tag}A"], writes=[f"{tag}t"])
        b.op("vector", lambda v, dc=dc: v.tensor_tensor(
            xn_t[dc][:], scr["ln_t"][:], Bb[:], ALU.add),
            reads=[f"{tag}t", f"{tag}B"], writes=[f"xn{dc}"])


def emit_load_x(b, xin, x_t, xrg="xin_d"):
    for dc in range(16):
        b.op("sync", lambda s, dc=dc: s.dma_start(
            out=x_t[dc][:], in_=xin[dc * 128:(dc + 1) * 128, :]),
            reads=[xrg], writes=[f"x{dc}"], dma=f"x{dc}")


def build_proj(layer, core, ctx=None):
    import contextlib
    if ctx is None:
        nc = bass.Bass("TRN2", target_bir_lowering=False, debug=False)
        es = contextlib.ExitStack()
        b = None
        pref = ""
        dram = None
        xrg_in, xrg_out = "xin_d", "xout_d"
    else:
        nc, b, pref, dram = ctx["nc"], ctx["b"], ctx["pref"], ctx["dram"]
        xrg_in, xrg_out = ctx["x_in_rg"], ctx["x_out_rg"]

    def DR(name, shape, dt, kind):
        if dram is not None:
            return dram[name]
        return nc.dram_tensor(name, shape, dt, kind=kind).ap()

    xin = DR("xin", [D, TOWN], F32, "ExternalInput")
    names = ["wq", "wk", "wv"] + (["wg"] if layer == 1 else [])
    wts = {n: DR(n, [4, 16, 128, 512], F32, "ExternalInput") for n in names}
    if layer == 0:
        bcols = DR("bcols", [128, 48], F32, "ExternalInput")
        bvrow = DR("bvrow", [1, D], F32, "ExternalInput")
    if layer == 1:
        cosin = DR("cosin", [128, 2 * TOWN], F32, "ExternalInput")
        protm = DR("protm", [128, 128], F32, "ExternalInput")
    consts = DR("consts", [128, 144], F32, "ExternalInput")

    q_out = DR("q_out", [NH, 128, TOWN], BF16, "ExternalOutput")
    k_out = DR("k_out", [NH, 128, TOWN], BF16, "ExternalOutput")
    v_out = DR("v_out", [TOWN, D], BF16, "ExternalOutput")
    if layer == 1:
        g_out = DR("g_out", [NH, 128, TOWN], BF16, "ExternalOutput")

    eph = contextlib.ExitStack()
    sb = lambda name, shape, dt: eph.enter_context(
        nc.sbuf_tensor(pref + name, shape, dt))
    x_t = [sb(f"x{i}", [128, TOWN], F32) for i in range(16)]
    xn_t = [sb(f"xn{i}", [128, TOWN], BF16) for i in range(16)]
    wbuf = [sb(f"wbuf{i}", [128, 8192], BF16) for i in range(2)]
    scr = {
        "ln_xc": sb("ln_xc", [128, TOWN], BF16),
        "ln_xsq": sb("ln_xsq", [128, TOWN], BF16),
        "rows": sb("rows", [1, 2048], F32),
        "ln_A": sb("ln_A", [128, TOWN], F32),
        "ln_B": sb("ln_B", [128, TOWN], F32),
        "ln_t": sb("ln_t", [128, TOWN], F32),
    }
    cstbuf = sb("cstbuf", [128, 144], F32)
    ones_bf = sb("ones_bf", [128, 128], BF16)
    qk_sb2 = [sb(f"qk_sb{i}", [128, TOWN], BF16) for i in range(2)]
    v_sb = [sb(f"v_sb{i}", [128, 512], BF16) for i in range(2)]
    if layer == 0:
        bcol_sb = sb("bcol_sb", [128, 48], F32)
        bvrow_sb = sb("bvrow_sb", [1, D], BF16)
        bvb_sb = sb("bvb_sb", [128, D], BF16)
    if layer == 1:
        cs_sb = sb("cs_sb", [128, 2 * TOWN], F32)
        pro_sb = sb("pro_sb", [128, 128], BF16)
        qbf_sb2 = [scr["ln_xc"], sb("qbf1", [128, TOWN], BF16)]
        r1_sb2 = [scr["ln_A"], sb("r1b", [128, TOWN], F32)]
        r2_sb2 = [scr["ln_B"], sb("r2b", [128, TOWN], F32)]
        qbn2 = ["lnxc", "qbf1"]
        r1n2 = ["lnA", "r1b"]
        r2n2 = ["lnB", "r2b"]

    if ctx is None:
        PP = [nc.alloc_psum_tensor(f"P{i}", [128, TOWN], F32) for i in range(4)]
        b = Builder(nc)
    else:
        PP = ctx["P"]
        b.barrier()
    psA, psB, psC = PP[0], PP[1], PP[2]
    psR0 = PP[3][0:1, 0:512]
    psR1 = PP[3][0:1, 512:1024]
    ps = {"R0": psR0, "R1": psR1, "B": psC, "Bname": "psC"}
    cst = {"ones_bf": ones_bf, "ones_row": cstbuf[0:1, 8:136],
           "ccol_eps_row": cstbuf[0:1, 1:2], "negC_col": cstbuf[:, 0:1]}

    b.once_slot("cst", "ones_bf", "bcol", "bvrow", "cs", "pro")
    b.op("sync", lambda s: s.dma_start(out=cstbuf[:], in_=consts),
         writes=["cst"], dma="cst")
    b.op("gpsimd", lambda g: g.dma_start(out=ones_bf[:], in_=consts[:, 8:136]),
         writes=["ones_bf"], dma="ones_bf")
    emit_load_x(b, xin, x_t, xrg_in)
    if layer == 0:
        b.op("sync", lambda s: s.dma_start(out=bcol_sb[:], in_=bcols),
             writes=["bcol"], dma="bcol")
        b.op("gpsimd", lambda g: g.dma_start(out=bvrow_sb[:], in_=bvrow),
             writes=["bvrow"], dma="bvrow")
        for j in range(4):
            b.op("tensor", lambda t, j=j: t.matmul(
                psA[:, 0:512], ones_bf[0:1, 0:128],
                bvrow_sb[0:1, j * 512:(j + 1) * 512], start=True, stop=True),
                reads=["bvrow", "ones_bf"], writes=["psA:0"])
            b.op("vector", lambda v, j=j: v.tensor_copy(
                bvb_sb[:, j * 512:(j + 1) * 512], psA[:, 0:512]),
                reads=["psA:0"], writes=["bvb"])
    if layer == 1:
        b.op("sync", lambda s: s.dma_start(out=cs_sb[:], in_=cosin),
             writes=["cs"], dma="cs")
        b.op("gpsimd", lambda g: g.dma_start(out=pro_sb[:], in_=protm),
             writes=["pro"], dma="pro")

    emit_ln(b, x_t, xn_t, cst, scr, ps, "ln",
            pre=(ctx.get("stats") if ctx is not None else None))

    out_evs = []

    def stream_wgroup(name, g, buf_i):
        return b.op("gpsimd", lambda gg, name=name, g=g, buf_i=buf_i: gg.dma_start(
            out=wbuf[buf_i][:].rearrange("p (c f) -> p c f", c=16),
            in_=wts[name][g].rearrange("c p f -> p c f")),
            reads=[], writes=[f"wbuf{buf_i}"], dma=f"wbuf{buf_i}")

    bufi_box = [0]

    def emit_fm_proj(name, out_ap, bias_off, gate=False, post=None):
        for g in range(4):
            buf_i = bufi_box[0]
            stream_wgroup(name, g, buf_i)
            if post is not None and g in post:
                post[g]()
            for hcl in range(4):
                hc = g * 4 + hcl
                pst = psA if (hc % 2 == 0) else psB
                pname = "psA" if (hc % 2 == 0) else "psB"
                for th in range(2):
                    for dc in range(16):
                        b.op("tensor", lambda t, dc=dc, th=th, hcl=hcl, pst=pst, buf_i=buf_i: t.matmul(
                            pst[:, th * 512:(th + 1) * 512],
                            wbuf[buf_i][:, dc * 512 + hcl * 128: dc * 512 + hcl * 128 + 128],
                            xn_t[dc][:, th * 512:(th + 1) * 512],
                            start=(dc == 0), stop=(dc == 15)),
                            reads=[f"wbuf{buf_i}", f"xn{dc}"],
                            writes=[f"{pname}:{th}"])
                qs = qk_sb2[hc % 2]
                qsn = f"qk_sb{hc % 2}"
                if gate:
                    gsb = r2_sb2[hc % 2]
                    gsn = r2n2[hc % 2]
                    b.op("scalar", lambda s, pst=pst, gsb=gsb: s.activation(
                        out=gsb[:], in_=pst[:, :], func=AF.Sigmoid),
                        reads=[f"{pname}:0", f"{pname}:1"], writes=[gsn])
                    b.op("vector", lambda v, pst=pst, qs=qs, gsb=gsb: v.tensor_tensor(
                        qs[:], pst[:, :], gsb[:], ALU.mult),
                        reads=[f"{pname}:0", f"{pname}:1", gsn], writes=[qsn])
                elif layer == 0:
                    for th in range(2):
                        b.op("scalar", lambda s, th=th, pst=pst, hc=hc, bias_off=bias_off, qs=qs: s.activation(
                            out=qs[:, th * 512:(th + 1) * 512],
                            in_=pst[:, th * 512:(th + 1) * 512], func=AF.Identity,
                            bias=bcol_sb[:, bias_off + hc:bias_off + hc + 1]),
                            reads=[f"{pname}:{th}", "bcol"], writes=[qsn])
                else:
                    hp2 = hc % 2
                    qbf_sb, r1_sb, r2_sb = qbf_sb2[hp2], r1_sb2[hp2], r2_sb2[hp2]
                    qbn, r1n, r2n = qbn2[hp2], r1n2[hp2], r2n2[hp2]
                    for th in range(2):
                        b.op("scalar", lambda s, th=th, pst=pst, qbf_sb=qbf_sb: s.activation(
                            out=qbf_sb[:, th * 512:(th + 1) * 512],
                            in_=pst[:, th * 512:(th + 1) * 512], func=AF.Copy),
                            reads=[f"{pname}:{th}"], writes=[qbn])
                    for th in range(2):
                        b.op("tensor", lambda t, th=th, qbf_sb=qbf_sb, hp2=hp2: t.matmul(
                            (psC if hp2 == 0 else psA)[:, th * 512:(th + 1) * 512], pro_sb[:, :],
                            qbf_sb[:, th * 512:(th + 1) * 512], start=True, stop=True),
                            reads=["pro", qbn],
                            writes=[(f"psC:{th}" if hp2 == 0 else f"psA:{th}")])
                    b.op("vector", lambda v, qbf_sb=qbf_sb, r1_sb=r1_sb: v.tensor_tensor(
                        r1_sb[:], qbf_sb[:], cs_sb[:, 0:TOWN], ALU.mult),
                        reads=[qbn, "cs"], writes=[r1n])
                    b.op("vector", lambda v, r2_sb=r2_sb, hp2=hp2: v.tensor_tensor(
                        r2_sb[:], (psC if hp2 == 0 else psA)[:, :], cs_sb[:, TOWN:2 * TOWN], ALU.mult),
                        reads=[("psC:0" if hp2 == 0 else "psA:0"),
                               ("psC:1" if hp2 == 0 else "psA:1"), "cs"],
                        writes=[r2n])
                    b.op("vector", lambda v, qs=qs, r1_sb=r1_sb, r2_sb=r2_sb: v.tensor_tensor(
                        qs[:], r1_sb[:], r2_sb[:], ALU.add),
                        reads=[r1n, r2n], writes=[qsn])
                ev = b.op("sync", lambda s, out_ap=out_ap, hc=hc, qs=qs: s.dma_start(
                    out=out_ap[hc], in_=qs[:]),
                    reads=[qsn],
                    writes=["kv_d" if name == "wk" else ("g_d" if gate else "q_d")],
                    dma=f"st_{qsn}")
                out_evs.append(ev)
            bufi_box[0] ^= 1

    def emit_v_proj(post=None):
        for g in range(4):
            buf_i = bufi_box[0]
            stream_wgroup("wv", g, buf_i)
            if post is not None and g in post:
                post[g]()
            for tc in range(8):
                pst = psA if (tc % 2 == 0) else psB
                pname = "psA" if (tc % 2 == 0) else "psB"
                vp = tc % 2
                for dc in range(16):
                    b.op("tensor", lambda t, dc=dc, tc=tc, pst=pst, buf_i=buf_i: t.matmul(
                        pst[:, 0:512],
                        xn_t[dc][:, tc * 128:(tc + 1) * 128],
                        wbuf[buf_i][:, dc * 512:(dc + 1) * 512],
                        start=(dc == 0), stop=(dc == 15)),
                        reads=[f"wbuf{buf_i}", f"xn{dc}"], writes=[f"{pname}:0"])
                if layer == 0:
                    b.op("vector", lambda v, g=g, vp=vp, pst=pst: v.tensor_tensor(
                        v_sb[vp][:], pst[:, 0:512],
                        bvb_sb[:, g * 512:(g + 1) * 512], ALU.add),
                        reads=[f"{pname}:0", "bvb"], writes=[f"v_sb{vp}"])
                else:
                    b.op("vector", lambda v, vp=vp, pst=pst: v.tensor_copy(
                        v_sb[vp][:], pst[:, 0:512]),
                        reads=[f"{pname}:0"], writes=[f"v_sb{vp}"])
                ev = b.op("sync", lambda s, g=g, tc=tc, vp=vp: s.dma_start(
                    out=v_out[tc * 128:(tc + 1) * 128, g * 512:(g + 1) * 512],
                    in_=v_sb[vp][:]),
                    reads=[f"v_sb{vp}"], writes=["v_d"], dma=f"st_v{vp}")
                out_evs.append(ev)
            bufi_box[0] ^= 1

    def emit_vg():
        b.op("gpsimd", lambda g: g.collective_compute(
            "AllGather", ALU.bypass,
            replica_groups=[[0, 1], [2, 3], [4, 5], [6, 7]],
            ins=[ctx["v_own"]], outs=[ctx["v_gth"]]),
            reads=["v_d"], writes=["vg_d"], cc="cca")

    def emit_kg():
        b.op("gpsimd", lambda g: g.collective_compute(
            "AllGather", ALU.bypass,
            replica_groups=[[0, 1], [2, 3], [4, 5], [6, 7]],
            ins=[ctx["k_own"]], outs=[ctx["k_gth"]]),
            reads=["kv_d"], writes=["kg_d"], cc="ccb")

    emit_fm_proj("wk", k_out, 16)
    emit_v_proj(post=({0: emit_kg} if ctx is not None else None))
    emit_fm_proj("wq", q_out, 0,
                 post=({0: emit_vg} if ctx is not None else None))
    if layer == 1:
        emit_fm_proj("wg", g_out, 32, gate=True)

    if ctx is None:
        b.finish(out_evs)
        b.emit(es)
        eph.close()
        es.close()
        return nc
    eph.close()
    return out_evs


def build_attn(layer, ctx=None):
    import contextlib
    if ctx is None:
        nc = bass.Bass("TRN2", target_bir_lowering=False, debug=False)
        es = contextlib.ExitStack()
        b = None
        pref = ""
        dram = None
        xrg_in, xrg_out = "xin_d", "xout_d"
        kv_gath = None
    else:
        nc, b, pref, dram = ctx["nc"], ctx["b"], ctx["pref"], ctx["dram"]
        xrg_in, xrg_out = ctx["x_in_rg"], ctx["x_out_rg"]
        kv_gath = ctx["kv_gath"]

    def DR(name, shape, dt, kind):
        if dram is not None:
            return dram.get(name)
        return nc.dram_tensor(name, shape, dt, kind=kind).ap()

    xin = DR("xin", [D, TOWN], F32, "ExternalInput")
    q_in = DR("q_in", [NH, 128, TOWN], BF16, "ExternalInput")
    if kv_gath is None:
        k_in = DR("k_in", [NH, 128, TFULL], BF16, "ExternalInput")
        v_in = DR("v_in", [TFULL, D], BF16, "ExternalInput")
    wo = DR("wo", [4, 16, 128, 512], F32, "ExternalInput")
    consts = DR("consts", [128, 144], F32, "ExternalInput")
    if layer == 0:
        bocols = DR("bocols", [128, 16], F32, "ExternalInput")
    else:
        g_in = DR("g_in", [NH, 128, TOWN], BF16, "ExternalInput")
        normw = DR("normw", [128, 1], F32, "ExternalInput")
        maskA = DR("maskA", [128, 128], F32, "ExternalInput")
        maskB = DR("maskB", [128, 128], F32, "ExternalInput")
    xout = DR("xout", [D, TOWN], F32, "ExternalOutput")

    eph = contextlib.ExitStack()
    sb = lambda name, shape, dt: eph.enter_context(
        nc.sbuf_tensor(pref + name, shape, dt))
    v_t = [sb(f"v{i}", [128, D], BF16) for i in range(16)]
    k_h = [sb(f"k_h{i}", [128, TFULL], BF16) for i in range(2)]
    q_h = [sb(f"q_h{i}", [128, TOWN], BF16) for i in range(2)]
    probs = [sb(f"pr{i}", [128, TOWN], BF16) for i in range(2)]
    ao_t = [sb(f"ao{i}", [128, TOWN], BF16) for i in range(16)]
    wbuf = [sb(f"wbuf{i}", [128, 8192], BF16) for i in range(2)]
    cstbuf = sb("cstbuf", [128, 144], F32)
    ones_bf = sb("ones_bf", [128, 128], BF16)
    invrow = sb("invrow", [1, TOWN], F32)
    invrowb = sb("invrowb", [1, TOWN], BF16)
    osb_sb = sb("osb_sb", [128, TOWN], F32)
    stat_xc = sb("stat_xc", [128, TOWN], BF16)
    stat_sq = sb("stat_sq", [128, TOWN], BF16)
    xc_sb = [sb(f"xc{i}", [128, TOWN], F32) for i in range(2)]
    xo_sb = [sb(f"xo{i}", [128, TOWN], F32) for i in range(2)]
    if layer == 0:
        boc_sb = sb("boc_sb", [128, 16], F32)
    else:
        g_hb = [sb(f"g_h{i}", [128, TOWN], BF16) for i in range(2)]
        nw_sb = sb("nw_sb", [128, 1], F32)
        mA_sb = sb("mA_sb", [128, 128], F32)
        mB_sb = sb("mB_sb", [128, 128], F32)
        mAb_sb = sb("mAb_sb", [128, 128], BF16)
        mBb_sb = sb("mBb_sb", [128, 128], BF16)
        idb_sb = sb("idb_sb", [128, 128], BF16)
        onorm = sb("onorm", [128, TOWN], F32)
        sq_sb = sb("sq_sb", [128, TOWN], BF16)
        rrow = sb("rrow", [1, TOWN], F32)
        rrowb = sb("rrowb", [1, TOWN], BF16)

    if ctx is None:
        PP = [nc.alloc_psum_tensor(f"P{i}", [128, TOWN], F32) for i in range(4)]
        b = Builder(nc)
    else:
        PP = ctx["P"]
        b.barrier()
    SC = [PP[0], PP[1]]
    OUT = PP[2]
    R0 = PP[3][0:1, 0:512]
    R1 = PP[3][0:1, 512:1024]
    b.once_slot("cst", "ones_bf", "boc", "nw", "mA", "mB")

    b.op("sync", lambda s: s.dma_start(out=cstbuf[:], in_=consts),
         writes=["cst"], dma="cst")
    b.op("gpsimd", lambda g: g.dma_start(out=ones_bf[:], in_=consts[:, 8:136]),
         writes=["ones_bf"], dma="ones_bf")
    for i in range(16):
        if kv_gath is None:
            b.op("scalar", lambda s, i=i: s.dma_start(
                out=v_t[i][:], in_=v_in[i * 128:(i + 1) * 128, :]),
                writes=[f"v{i}"], dma=f"v{i}")
        else:
            r, j = i // 8, i % 8
            src_ap = kv_gath[1][r * 2048 + j * 256: r * 2048 + (j + 1) * 256, :]
            b.op("scalar", lambda s, i=i, src_ap=src_ap: s.dma_start(
                out=v_t[i][:], in_=src_ap.rearrange("(p a) c -> p (a c)", a=2)),
                reads=["vg_d"], writes=[f"v{i}"], dma=f"v{i}")
    if layer == 0:
        b.op("sync", lambda s: s.dma_start(out=boc_sb[:], in_=bocols),
             writes=["boc"], dma="boc")
    else:
        b.op("sync", lambda s: s.dma_start(out=nw_sb[:], in_=normw),
             writes=["nw"], dma="nw")
        b.op("sync", lambda s: s.dma_start(out=mA_sb[:], in_=maskA),
             writes=["mA"], dma="mA")
        b.op("sync", lambda s: s.dma_start(out=mB_sb[:], in_=maskB),
             writes=["mB"], dma="mB")
        if ctx is not None:
            b.op("gpsimd", lambda g: g.dma_start(out=mAb_sb[:], in_=maskA),
                 writes=["mAb"], dma="mAb")
            b.op("gpsimd", lambda g: g.dma_start(out=mBb_sb[:], in_=maskB),
                 writes=["mBb"], dma="mBb")
            b.op("gpsimd", lambda g: g.dma_start(out=idb_sb[:], in_=dram["ident"]),
                 writes=["idb"], dma="idb")

    def kc_info(kc):
        if layer == 0:
            return 0, None
        jj = kc % 8
        qf = jj * 128
        return qf, jj

    kcs = list(range(16))

    def pieces(qf):
        if qf < 512:
            return [(qf, 512 - qf), (512, 512)]
        return [(qf, 1024 - qf)]

    touch = {0: [], 1: []}
    for kc in kcs:
        qf, _ = kc_info(kc)
        for (qo, w) in pieces(qf):
            bank = 0 if qo < 512 else 1
            touch[bank].append(kc)
    first_t = {bk: (t[0] if t else None) for bk, t in touch.items()}
    last_t = {bk: (t[-1] if t else None) for bk, t in touch.items()}

    def head_dmas(h):
        b.op("sync", lambda s, h=h: s.dma_start(out=q_h[h % 2][:], in_=q_in[h]),
             reads=["q_d"], writes=[f"q_h{h % 2}"], dma=f"q{h % 2}")
        if kv_gath is None:
            b.op("sync", lambda s, h=h: s.dma_start(out=k_h[h % 2][:], in_=k_in[h]),
                 writes=[f"k_h{h % 2}"], dma=f"k{h % 2}")
        else:
            for r in range(2):
                b.op("sync", lambda s, h=h, r=r: s.dma_start(
                    out=k_h[h % 2][:, r * TOWN:(r + 1) * TOWN],
                    in_=kv_gath[0][r * 2048 + h * 128: r * 2048 + (h + 1) * 128, :]),
                    reads=["kg_d"], writes=[f"k_h{h % 2}"], dma=f"k{h % 2}")
        if layer == 1:
            b.op("sync", lambda s, h=h: s.dma_start(out=g_hb[h % 2][:], in_=g_in[h]),
                 reads=["g_d"], writes=[f"g_h{h % 2}"], dma=f"g{h % 2}")

    def emit_scores(h, kc):
        par = kc % 2
        hb = h % 2
        qf, dq = kc_info(kc)
        for (qo, w) in pieces(qf):
            bank = 0 if qo < 512 else 1
            b.op("tensor", lambda t, kc=kc, hb=hb, qo=qo, w=w, par=par: t.matmul(
                SC[par][:, qo:qo + w], k_h[hb][:, kc * 128:(kc + 1) * 128],
                q_h[hb][:, qo:qo + w], start=True, stop=True),
                reads=[f"k_h{hb}", f"q_h{hb}"], writes=[f"SC{par}:{bank}"])
        if layer == 1 and dq is not None:
            bank = 0 if dq * 128 < 512 else 1
            if kv_gath is not None:
                mt = mAb_sb if kc < 8 else mBb_sb
                mr = "mAb" if kc < 8 else "mBb"
                b.op("tensor", lambda t, dq=dq, par=par, mt=mt: t.matmul(
                    SC[par][:, dq * 128:(dq + 1) * 128], idb_sb[:, :], mt[:],
                    start=False, stop=True, skip_group_check=True),
                    reads=[mr, "idb"], writes=[f"SC{par}:{bank}"])
            else:
                mt = mA_sb if kc < 8 else mB_sb
                mr = "mA" if kc < 8 else "mB"
                b.op("vector", lambda v, dq=dq, par=par, mt=mt: v.tensor_tensor(
                    SC[par][:, dq * 128:(dq + 1) * 128],
                    SC[par][:, dq * 128:(dq + 1) * 128], mt[:], ALU.add),
                    reads=[f"SC{par}:{bank}", mr], writes=[f"SC{par}:{bank}"])
        rd = [f"SC{par}:0", f"SC{par}:1"] if qf < 512 else [f"SC{par}:1"]
        b.op("scalar", lambda s, par=par, qf=qf: s.activation(
            out=probs[par][:, qf:TOWN], in_=SC[par][:, qf:TOWN], func=AF.Exp,
            scale=SCALE_ATT, bias=cstbuf[:, 0:1]),
            reads=rd + ["cst"], writes=[f"pr{par}"])

    def emit_sumout(h, kc):
        par = kc % 2
        qf, _ = kc_info(kc)
        for (qo, w) in pieces(qf):
            bank = 0 if qo < 512 else 1
            Rp = R0 if bank == 0 else R1
            st = (kc == first_t[bank])
            sp = (kc == last_t[bank])
            b.op("tensor", lambda t, qo=qo, w=w, par=par, Rp=Rp, st=st, sp=sp, bank=bank: t.matmul(
                Rp[0:1, qo - bank * 512:qo - bank * 512 + w],
                ones_bf[:, 0:1], probs[par][:, qo:qo + w],
                start=st, stop=sp, skip_group_check=True),
                reads=[f"pr{par}", "ones_bf"], writes=[f"R{bank}"])
            b.op("tensor", lambda t, kc=kc, h=h, qo=qo, w=w, par=par, st=st, sp=sp: t.matmul(
                OUT[:, qo:qo + w], v_t[kc][:, h * 128:(h + 1) * 128],
                probs[par][:, qo:qo + w],
                start=st, stop=sp, skip_group_check=True),
                reads=[f"pr{par}", f"v{kc}"], writes=[f"OUT:{bank}"])

    head_dmas(0)
    for h in range(NH):
        if h + 1 < NH:
            head_dmas(h + 1)
        prev = None
        for kc in kcs:
            emit_scores(h, kc)
            if prev is not None:
                emit_sumout(h, prev)
            prev = kc
        emit_sumout(h, prev)
        b.op("vector", lambda v: v.tensor_copy(osb_sb[:], OUT[:, :]),
             reads=["OUT:0", "OUT:1"], writes=["osb"])
        b.op("scalar", lambda s: s.activation(
            out=invrow[0:1, 0:512], in_=R0[0:1, :], func=AF.Ln),
            reads=["R0"], writes=["invrow"])
        b.op("scalar", lambda s: s.activation(
            out=invrow[0:1, 512:1024], in_=R1[0:1, :], func=AF.Ln),
            reads=["R1"], writes=["invrow"])
        b.op("scalar", lambda s: s.activation(
            out=invrowb[0:1, :], in_=invrow[0:1, :], func=AF.Exp, scale=-1.0),
            reads=["invrow"], writes=["invrowb"])
        for th in range(2):
            b.op("tensor", lambda t, th=th: t.matmul(
                OUT[:, th * 512:(th + 1) * 512], ones_bf[0:1, 0:128],
                invrowb[0:1, th * 512:(th + 1) * 512], start=True, stop=True),
                reads=["invrowb", "ones_bf"], writes=[f"OUT:{th}"])
        if layer == 0:
            b.op("vector", lambda v, h=h: v.tensor_tensor(
                ao_t[h][:], osb_sb[:], OUT[:, :], ALU.mult),
                reads=["osb", "OUT:0", "OUT:1"], writes=[f"ao{h}"])
        else:
            hb = h % 2
            b.op("vector", lambda v: v.tensor_tensor(
                onorm[:], osb_sb[:], OUT[:, :], ALU.mult),
                reads=["osb", "OUT:0", "OUT:1"], writes=["onorm"])
            b.op("scalar", lambda s: s.activation(
                out=sq_sb[:], in_=onorm[:], func=AF.Square),
                reads=["onorm"], writes=["sq"])
            for th in range(2):
                Rp = R0 if th == 0 else R1
                b.op("tensor", lambda t, th=th, Rp=Rp: t.matmul(
                    Rp[0:1, :], ones_bf[:, 0:1], sq_sb[:, th * 512:(th + 1) * 512],
                    start=True, stop=True),
                    reads=["sq", "ones_bf"], writes=[f"R{th}"])
                b.op("scalar", lambda s, th=th, Rp=Rp: s.activation(
                    out=rrow[0:1, th * 512:(th + 1) * 512], in_=Rp[0:1, :],
                    func=AF.Ln, scale=1.0 / 128.0, bias=cstbuf[0:1, 2:3]),
                    reads=[f"R{th}", "cst"], writes=["rrow"])
            b.op("scalar", lambda s: s.activation(
                out=rrowb[0:1, :], in_=rrow[0:1, :], func=AF.Exp, scale=-0.5),
                reads=["rrow"], writes=["rrowb"])
            for th in range(2):
                b.op("tensor", lambda t, th=th: t.matmul(
                    SC[1][:, th * 512:(th + 1) * 512], ones_bf[0:1, 0:128],
                    rrowb[0:1, th * 512:(th + 1) * 512], start=True, stop=True),
                    reads=["rrowb", "ones_bf"], writes=[f"SC1:{th}"])
            b.op("vector", lambda v: v.tensor_tensor(
                onorm[:], onorm[:], SC[1][:, :], ALU.mult),
                reads=["onorm", "SC1:0", "SC1:1"], writes=["onorm"])
            b.op("vector", lambda v, h=h, hb=hb: v.scalar_tensor_tensor(
                ao_t[h][:], onorm[:], nw_sb[:, 0:1], g_hb[hb][:],
                ALU.mult, ALU.mult),
                reads=["onorm", "nw", f"g_h{hb}"], writes=[f"ao{h}"])

    out_evs = []
    for g in range(4):
        b.op("gpsimd", lambda gg, g=g: gg.dma_start(
            out=wbuf[g % 2][:].rearrange("p (c f) -> p c f", c=16),
            in_=wo[g].rearrange("c p f -> p c f")),
            writes=[f"wbuf{g % 2}"], dma=f"wbuf{g % 2}")
        for hcl in range(4):
            dcog = g * 4 + hcl
            par = dcog % 2
            pp = SC[0] if par == 0 else OUT
            ppn = "SC0" if par == 0 else "OUT"
            b.op("sync", lambda s, dcog=dcog, par=par: s.dma_start(
                out=xc_sb[par][:], in_=xin[dcog * 128:(dcog + 1) * 128, :]),
                reads=[xrg_in], writes=[f"xc{par}"], dma=f"xi{par}")
            for th in range(2):
                for hc in range(16):
                    b.op("tensor", lambda t, g=g, hcl=hcl, th=th, hc=hc, pp=pp: t.matmul(
                        pp[:, th * 512:(th + 1) * 512],
                        wbuf[g % 2][:, hc * 512 + hcl * 128: hc * 512 + hcl * 128 + 128],
                        ao_t[hc][:, th * 512:(th + 1) * 512],
                        start=(hc == 0), stop=(hc == 15)),
                        reads=[f"wbuf{g % 2}", f"ao{hc}"], writes=[f"{ppn}:{th}"])
            if layer == 0:
                b.op("vector", lambda v, dcog=dcog, par=par, pp=pp, ppn=ppn: v.scalar_tensor_tensor(
                    xo_sb[par][:], pp[:, :], boc_sb[:, dcog:dcog + 1],
                    xc_sb[par][:], ALU.add, ALU.add),
                    reads=[f"{ppn}:0", f"{ppn}:1", "boc", f"xc{par}"],
                    writes=[f"xo{par}"])
            else:
                b.op("vector", lambda v, par=par, pp=pp, ppn=ppn: v.tensor_tensor(
                    xo_sb[par][:], pp[:, :], xc_sb[par][:], ALU.add),
                    reads=[f"{ppn}:0", f"{ppn}:1", f"xc{par}"],
                    writes=[f"xo{par}"])
            if ctx is not None:
                b.op("scalar", lambda s, par=par: s.activation(
                    out=stat_xc[:], in_=xo_sb[par][:], func=AF.Copy),
                    reads=[f"xo{par}"], writes=["stat_xc"])
                b.op("scalar", lambda s, par=par: s.activation(
                    out=stat_sq[:], in_=xo_sb[par][:], func=AF.Square),
                    reads=[f"xo{par}"], writes=["stat_sq"])
                for th in range(2):
                    b.op("tensor", lambda t, th=th, dcog=dcog: t.matmul(
                        (R0 if th == 0 else R1)[0:1, :],
                        ones_bf[:, 0:1], stat_xc[:, th * 512:(th + 1) * 512],
                        start=(dcog == 0), stop=(dcog == 15), skip_group_check=True),
                        reads=["stat_xc", "ones_bf"], writes=[f"R{th}"])
                    b.op("tensor", lambda t, th=th, dcog=dcog: t.matmul(
                        SC[1][0:1, th * 512:(th + 1) * 512],
                        ones_bf[:, 0:1], stat_sq[:, th * 512:(th + 1) * 512],
                        start=(dcog == 0), stop=(dcog == 15), skip_group_check=True),
                        reads=["stat_sq", "ones_bf"], writes=[f"SC1:{th}"])
            ev = b.op("sync", lambda s, dcog=dcog, par=par: s.dma_start(
                out=xout[dcog * 128:(dcog + 1) * 128, :], in_=xo_sb[par][:]),
                reads=[f"xo{par}"], writes=[xrg_out], dma=f"st_xo{par}")
            out_evs.append(ev)

    if ctx is None:
        b.finish(out_evs)
        b.emit(es)
        eph.close()
        es.close()
        return nc
    eph.close()
    return out_evs


def build_moe(ctx=None):
    import contextlib
    if ctx is None:
        nc = bass.Bass("TRN2", target_bir_lowering=False, debug=False)
        es = contextlib.ExitStack()
        b = None
        pref = ""
        dram = None
        xrg_in, xrg_out = "xin_d", "xout_d"
    else:
        nc, b, pref, dram = ctx["nc"], ctx["b"], ctx["pref"], ctx["dram"]
        xrg_in, xrg_out = ctx["x_in_rg"], ctx["x_out_rg"]

    def DR(name, shape, dt, kind):
        if dram is not None:
            return dram[name]
        return nc.dram_tensor(name, shape, dt, kind=kind).ap()

    xin = DR("xin", [D, TOWN], F32, "ExternalInput")
    wg = DR("wg", [16, 128, 8], F32, "ExternalInput")
    mbias = DR("mbias", [8, 1], F32, "ExternalInput")
    w1 = DR("w1", [8, 4, 16, 128, 256], F32, "ExternalInput")
    w3 = DR("w3", [8, 4, 16, 128, 256], F32, "ExternalInput")
    w2 = DR("w2", [8, 4, 8, 128, 512], F32, "ExternalInput")
    ident = DR("ident", [128, 128], F32, "ExternalInput")
    consts = DR("consts", [128, 144], F32, "ExternalInput")
    xout = DR("xout", [D, TOWN], F32, "ExternalOutput")

    eph = contextlib.ExitStack()
    sb = lambda name, shape, dt: eph.enter_context(
        nc.sbuf_tensor(pref + name, shape, dt))
    xn_t = [sb(f"xn{i}", [128, TOWN], BF16) for i in range(16)]
    xb = [sb(f"xb{i}", [128, TOWN], F32) for i in range(2)]
    xc = sb("lxc", [128, TOWN], BF16)
    xsq = sb("lxsq", [128, TOWN], BF16)
    rows = sb("rows", [1, 2048], F32)
    A_sb = sb("lnA", [128, TOWN], F32)
    B_sb = sb("lnB", [128, TOWN], F32)
    T_sb = sb("lnT", [128, TOWN], F32)
    rw_sb = sb("rw", [8, TOWN], F32)
    cstbuf = sb("cstbuf", [128, 144], F32)
    ones_bf = sb("ones_bf", [128, 128], BF16)
    id_sb = sb("id_sb", [128, 128], F32)
    mb_sb = sb("mb_sb", [8, 1], F32)
    wgt_sb = sb("wgt_sb", [128, 16 * 8], F32)
    sct = sb("sct", [128, 64], F32)
    rwt = sb("rwt", [128, 64], F32)
    ewt = sb("ewt", [128, 64], F32)
    s2_sb = sb("s2t", [128, 8], F32)
    col1 = sb("col1", [128, 1], F32)
    col2 = sb("col2", [128, 1], F32)
    Bf = sb("Bf", [8, 1], F32)
    ew_fm = sb("ew_fm", [8, TOWN], F32)
    ewrow = sb("ewrow", [1, TOWN], F32)
    ew_bc = [sb(f"ewb{e}", [128, TOWN], BF16) for e in range(8)]
    w1buf = [sb(f"w1b{i}", [128, 4096], BF16) for i in range(2)]
    w3buf = [sb(f"w3b{i}", [128, 4096], BF16) for i in range(2)]
    w2buf = [sb(f"w2b{i}", [128, 4096], BF16) for i in range(2)]
    s1_sb = sb("s1", [128, TOWN], BF16)
    t3_sb = sb("t3", [128, TOWN], BF16)
    g_t = [sb(f"g{i}", [128, TOWN], BF16) for i in range(8)]
    op_t = [sb(f"op{i}", [128, TOWN], BF16) for i in range(16)]

    if ctx is None:
        P = [nc.alloc_psum_tensor(f"P{i}", [128, TOWN], F32) for i in range(4)]
        b = Builder(nc)
    else:
        P = ctx["P"]
        b.barrier()
    b.once_slot("cst", "ones_bf", "id", "mb", "wgt")
    b.op("sync", lambda s: s.dma_start(out=cstbuf[:], in_=consts),
         writes=["cst"], dma="cst")
    b.op("gpsimd", lambda g: g.dma_start(out=ones_bf[:], in_=consts[:, 8:136]),
         writes=["ones_bf"], dma="ones_bf")
    b.op("sync", lambda s: s.dma_start(out=id_sb[:], in_=ident),
         writes=["id"], dma="id")
    b.op("sync", lambda s: s.dma_start(out=mb_sb[:], in_=mbias),
         writes=["mb"], dma="mb")
    b.op("sync", lambda s: s.dma_start(
        out=wgt_sb[:].rearrange("p (c e) -> p c e", c=16),
        in_=wg.rearrange("c p e -> p c e")), writes=["wgt"], dma="wgt")

    stats = ctx.get("stats") if ctx is not None else None
    mean = rows[0:1, 0:1024]
    var = rows[0:1, 1024:2048]
    if stats is not None:
        for th in range(2):
            ap, rg = stats["sx"][th]
            b.op("vector", lambda v, th=th, ap=ap: v.tensor_scalar(
                mean[0:1, th * 512:(th + 1) * 512], ap,
                1.0 / 2048.0, None, ALU.mult),
                reads=[rg], writes=["mean"])
        b.op("vector", lambda v: v.tensor_tensor(var, mean, mean, ALU.mult),
             reads=["mean"], writes=["var"])
        for th in range(2):
            ap, rg = stats["sq"][th]
            b.op("vector", lambda v, th=th, ap=ap: v.scalar_tensor_tensor(
                var[0:1, th * 512:(th + 1) * 512], ap,
                1.0 / 2048.0, var[0:1, th * 512:(th + 1) * 512], ALU.mult, ALU.subtract),
                reads=[rg, "var"], writes=["var"])
    for dc in range(16):
        par = dc % 2
        b.op("sync", lambda s, dc=dc, par=par: s.dma_start(
            out=xb[par][:], in_=xin[dc * 128:(dc + 1) * 128, :]),
            reads=[xrg_in], writes=[f"xb{par}"], dma=f"x{par}")
        if stats is None:
            b.op("scalar", lambda s, par=par: s.activation(
                out=xc[:], in_=xb[par][:], func=AF.Copy),
                reads=[f"xb{par}"], writes=["xc"])
            b.op("scalar", lambda s, par=par: s.activation(
                out=xsq[:], in_=xb[par][:], func=AF.Square),
                reads=[f"xb{par}"], writes=["xsq"])
            for th in range(2):
                b.op("tensor", lambda t, dc=dc, th=th: t.matmul(
                    P[0][0:1, th * 512:(th + 1) * 512],
                    ones_bf[:, 0:1], xc[:, th * 512:(th + 1) * 512],
                    start=(dc == 0), stop=(dc == 15), skip_group_check=True),
                    reads=["xc", "ones_bf"], writes=[f"P0:{th}"])
                b.op("tensor", lambda t, dc=dc, th=th: t.matmul(
                    P[1][0:1, th * 512:(th + 1) * 512],
                    ones_bf[:, 0:1], xsq[:, th * 512:(th + 1) * 512],
                    start=(dc == 0), stop=(dc == 15), skip_group_check=True),
                    reads=["xsq", "ones_bf"], writes=[f"P1:{th}"])
        for th in range(2):
            b.op("tensor", lambda t, dc=dc, th=th, par=par: t.matmul(
                P[2][0:8, th * 512:(th + 1) * 512],
                wgt_sb[:, dc * 8:(dc + 1) * 8], xb[par][:, th * 512:(th + 1) * 512],
                start=(dc == 0), stop=(dc == 15), skip_group_check=True),
                reads=["wgt", f"xb{par}"], writes=[f"P2:{th}"])
        b.op("tensor", lambda t, dc=dc: t.matmul(
            P[3][0:8, 0:1], wgt_sb[:, dc * 8:(dc + 1) * 8], cstbuf[:, 8:9],
            start=(dc == 0), stop=(dc == 15), skip_group_check=True),
            reads=(["wgt", "cst"] + ([stats["sx"][0][1], stats["sx"][1][1]]
                                     if stats is not None else [])),
            writes=["P3:0"])

    if stats is None:
        for th in range(2):
            b.op("vector", lambda v, th=th: v.tensor_scalar(
                mean[0:1, th * 512:(th + 1) * 512], P[0][0:1, th * 512:(th + 1) * 512],
                1.0 / 2048.0, None, ALU.mult),
                reads=[f"P0:{th}"], writes=["mean"])
        b.op("vector", lambda v: v.tensor_tensor(var, mean, mean, ALU.mult),
             reads=["mean"], writes=["var"])
        for th in range(2):
            b.op("vector", lambda v, th=th: v.scalar_tensor_tensor(
                var[0:1, th * 512:(th + 1) * 512], P[1][0:1, th * 512:(th + 1) * 512],
                1.0 / 2048.0, var[0:1, th * 512:(th + 1) * 512], ALU.mult, ALU.subtract),
                reads=[f"P1:{th}", "var"], writes=["var"])
    b.op("scalar", lambda s: s.activation(
        out=var, in_=var, func=AF.Ln, bias=cstbuf[0:1, 1:2]),
        reads=["var", "cst"], writes=["var"])
    b.op("scalar", lambda s: s.activation(
        out=var, in_=var, func=AF.Exp, scale=-0.5),
        reads=["var"], writes=["var"])
    b.op("vector", lambda v: v.tensor_copy(Bf[:], P[3][0:8, 0:1]),
         reads=["P3:0"], writes=["Bf"])
    b.op("vector", lambda v: v.tensor_tensor(mean, mean, var, ALU.mult),
         reads=["mean", "var"], writes=["mean"])
    b.op("vector", lambda v: v.tensor_copy(T_sb[0:8, :], P[2][0:8, :]),
         reads=["P2:0", "P2:1"], writes=["lnT"])
    for th in range(2):
        b.op("tensor", lambda t, th=th: t.matmul(
            P[3][0:8, th * 512:(th + 1) * 512], cstbuf[0:1, 8:16],
            var[0:1, th * 512:(th + 1) * 512], start=True, stop=True),
            reads=["var", "cst"], writes=[f"P3:{th}"])
    b.op("vector", lambda v: v.tensor_tensor(
        A_sb[0:8, :], T_sb[0:8, :], P[3][0:8, :], ALU.mult),
        reads=["lnT", "P3:0", "P3:1"], writes=["lnA"])
    for th in range(2):
        b.op("tensor", lambda t, th=th: t.matmul(
            P[3][0:8, th * 512:(th + 1) * 512], cstbuf[0:1, 8:16],
            mean[0:1, th * 512:(th + 1) * 512], start=True, stop=True),
            reads=["mean", "cst"], writes=[f"P3:{th}"])
    b.op("vector", lambda v: v.tensor_scalar(
        B_sb[0:8, :], P[3][0:8, :], Bf[:, 0:1], None, ALU.mult),
        reads=["P3:0", "P3:1", "Bf"], writes=["lnB"])
    b.op("vector", lambda v: v.tensor_tensor(
        A_sb[0:8, :], A_sb[0:8, :], B_sb[0:8, :], ALU.subtract),
        reads=["lnA", "lnB"], writes=["lnA"])
    b.op("scalar", lambda s: s.activation(
        out=rw_sb[:], in_=A_sb[0:8, :], func=AF.Exp, scale=-1.0),
        reads=["lnA"], writes=["rw"])
    b.op("vector", lambda v: v.tensor_scalar(
        rw_sb[:], rw_sb[:], 1.0, None, ALU.add),
        reads=["rw"], writes=["rw"])
    b.op("vector", lambda v: v.reciprocal(rw_sb[:], rw_sb[:]),
         reads=["rw"], writes=["rw"])
    b.op("vector", lambda v: v.tensor_scalar(
        T_sb[0:8, :], rw_sb[:], mb_sb[0:8, 0:1], None, ALU.add),
        reads=["rw", "mb"], writes=["lnT"])

    for tc in range(8):
        b.op("tensor", lambda t, tc=tc: t.transpose(
            P[3][:, 0:8], T_sb[0:8, tc * 128:(tc + 1) * 128], id_sb[0:8, 0:8]),
            reads=["lnT", "id"], writes=["P3:0"])
        b.op("vector", lambda v, tc=tc: v.tensor_copy(
            sct[:, tc * 8:(tc + 1) * 8], P[3][:, 0:8]),
            reads=["P3:0"], writes=["sct"])
        b.op("tensor", lambda t, tc=tc: t.transpose(
            P[3][:, 0:8], rw_sb[0:8, tc * 128:(tc + 1) * 128], id_sb[0:8, 0:8]),
            reads=["rw", "id"], writes=["P3:0"])
        b.op("vector", lambda v, tc=tc: v.tensor_copy(
            rwt[:, tc * 8:(tc + 1) * 8], P[3][:, 0:8]),
            reads=["P3:0"], writes=["rwt"])
    for tc in range(8):
        s_ = sct[:, tc * 8:(tc + 1) * 8]
        r_ = rwt[:, tc * 8:(tc + 1) * 8]
        e_ = ewt[:, tc * 8:(tc + 1) * 8]
        b.op("vector", lambda v, s_=s_: v.reduce_max(
            col1[:, 0:1], s_, axis=mybir.AxisListType.X),
            reads=["sct"], writes=["col1"])
        b.op("vector", lambda v, s_=s_, e_=e_: v.tensor_scalar(
            e_, s_, col1[:, 0:1], None, ALU.is_ge),
            reads=["sct", "col1"], writes=["ewt"])
        b.op("vector", lambda v, s_=s_, e_=e_: v.scalar_tensor_tensor(
            s2_sb[:], e_, -1e30, s_, ALU.mult, ALU.add),
            reads=["ewt", "sct"], writes=["s2t"])
        b.op("vector", lambda v: v.reduce_max(
            col2[:, 0:1], s2_sb[:], axis=mybir.AxisListType.X),
            reads=["s2t"], writes=["col2"])
        b.op("vector", lambda v: v.tensor_scalar(
            s2_sb[:], s2_sb[:], col2[:, 0:1], None, ALU.is_ge),
            reads=["s2t", "col2"], writes=["s2t"])
        b.op("vector", lambda v, e_=e_: v.tensor_tensor(
            e_, e_, s2_sb[:], ALU.add),
            reads=["ewt", "s2t"], writes=["ewt"])
        b.op("vector", lambda v, r_=r_, e_=e_: v.tensor_tensor(
            e_, e_, r_, ALU.mult),
            reads=["ewt", "rwt"], writes=["ewt"])
        b.op("vector", lambda v, e_=e_: v.reduce_sum(
            col1[:, 0:1], e_, axis=mybir.AxisListType.X),
            reads=["ewt"], writes=["col1"])
        b.op("vector", lambda v: v.reciprocal(col1[:, 0:1], col1[:, 0:1]),
             reads=["col1"], writes=["col1"])
        b.op("vector", lambda v, e_=e_: v.tensor_scalar(
            e_, e_, col1[:, 0:1], None, ALU.mult),
            reads=["ewt", "col1"], writes=["ewt"])
    for tc in range(8):
        b.op("tensor", lambda t, tc=tc: t.transpose(
            P[3][0:8, 0:128], ewt[:, tc * 8:(tc + 1) * 8], id_sb[:, 0:128]),
            reads=["ewt", "id"], writes=["P3:0"])
        b.op("vector", lambda v, tc=tc: v.tensor_copy(
            ew_fm[0:8, tc * 128:(tc + 1) * 128], P[3][0:8, 0:128]),
            reads=["P3:0"], writes=["ew_fm"])
    for e in range(8):
        b.op("gpsimd", lambda g, e=e: g.dma_start(
            out=ewrow[0:1, :], in_=ew_fm[e:e + 1, :]),
            reads=["ew_fm"], writes=["ewrow"], dma="ewrow")
        for th in range(2):
            b.op("tensor", lambda t, th=th: t.matmul(
                P[3][:, th * 512:(th + 1) * 512], cstbuf[0:1, 8:136],
                ewrow[0:1, th * 512:(th + 1) * 512], start=True, stop=True),
                reads=["ewrow", "cst"], writes=[f"P3:{th}"])
        b.op("vector", lambda v, e=e: v.tensor_copy(
            ew_bc[e][:], P[3][:, :]),
            reads=["P3:0", "P3:1"], writes=[f"ewb{e}"])

    for th in range(2):
        b.op("tensor", lambda t, th=th: t.matmul(
            P[0][:, th * 512:(th + 1) * 512], cstbuf[0:1, 8:136],
            var[0:1, th * 512:(th + 1) * 512], start=True, stop=True),
            reads=["var", "cst"], writes=[f"P0:{th}"])
        b.op("vector", lambda v, th=th: v.tensor_copy(
            A_sb[:, th * 512:(th + 1) * 512], P[0][:, th * 512:(th + 1) * 512]),
            reads=[f"P0:{th}"], writes=["lnA"])
        b.op("tensor", lambda t, th=th: t.matmul(
            P[1][:, th * 512:(th + 1) * 512], cstbuf[0:1, 8:136],
            mean[0:1, th * 512:(th + 1) * 512], start=True, stop=True),
            reads=["mean", "cst"], writes=[f"P1:{th}"])
        b.op("vector", lambda v, th=th: v.tensor_scalar(
            B_sb[:, th * 512:(th + 1) * 512], P[1][:, th * 512:(th + 1) * 512],
            -1.0, None, ALU.mult),
            reads=[f"P1:{th}"], writes=["lnB"])
    for dc in range(16):
        par = dc % 2
        b.op("sync", lambda s, dc=dc, par=par: s.dma_start(
            out=xb[par][:], in_=xin[dc * 128:(dc + 1) * 128, :]),
            reads=[xrg_in], writes=[f"xb{par}"], dma=f"x{par}")
        b.op("vector", lambda v, par=par: v.tensor_tensor(
            T_sb[:], xb[par][:], A_sb[:], ALU.mult),
            reads=[f"xb{par}", "lnA"], writes=["lnT"])
        b.op("vector", lambda v, dc=dc: v.tensor_tensor(
            xn_t[dc][:], T_sb[:], B_sb[:], ALU.add),
            reads=["lnT", "lnB"], writes=[f"xn{dc}"])

    for e in range(8):
        for gq in range(4):
            wpar = gq % 2
            b.op("gpsimd", lambda g, e=e, gq=gq, wpar=wpar: g.dma_start(
                out=w1buf[wpar][:].rearrange("p (c f) -> p c f", c=16),
                in_=w1[e, gq].rearrange("c p f -> p c f")),
                writes=[f"w1b{wpar}"], dma=f"w1b{wpar}")
            b.op("gpsimd", lambda g, e=e, gq=gq, wpar=wpar: g.dma_start(
                out=w3buf[wpar][:].rearrange("p (c f) -> p c f", c=16),
                in_=w3[e, gq].rearrange("c p f -> p c f")),
                writes=[f"w3b{wpar}"], dma=f"w3b{wpar}")
            for fcl in range(2):
                fc = gq * 2 + fcl
                for th in range(2):
                    for dc in range(16):
                        b.op("tensor", lambda t, dc=dc, th=th, fcl=fcl, wpar=wpar: t.matmul(
                            P[0][:, th * 512:(th + 1) * 512],
                            w1buf[wpar][:, dc * 256 + fcl * 128: dc * 256 + fcl * 128 + 128],
                            xn_t[dc][:, th * 512:(th + 1) * 512],
                            start=(dc == 0), stop=(dc == 15)),
                            reads=[f"w1b{wpar}", f"xn{dc}"], writes=[f"P0:{th}"])
                    for dc in range(16):
                        b.op("tensor", lambda t, dc=dc, th=th, fcl=fcl, wpar=wpar: t.matmul(
                            P[1][:, th * 512:(th + 1) * 512],
                            w3buf[wpar][:, dc * 256 + fcl * 128: dc * 256 + fcl * 128 + 128],
                            xn_t[dc][:, th * 512:(th + 1) * 512],
                            start=(dc == 0), stop=(dc == 15)),
                            reads=[f"w3b{wpar}", f"xn{dc}"], writes=[f"P1:{th}"])
                b.op("scalar", lambda s: s.activation(
                    out=s1_sb[:], in_=P[0][:, :], func=AF.Silu),
                    reads=["P0:0", "P0:1"], writes=["s1"])
                b.op("vector", lambda v, e=e: v.tensor_tensor(
                    t3_sb[:], P[1][:, :], ew_bc[e][:], ALU.mult),
                    reads=["P1:0", "P1:1", f"ewb{e}"], writes=["t3"])
                b.op("vector", lambda v, fc=fc: v.tensor_tensor(
                    g_t[fc][:], s1_sb[:], t3_sb[:], ALU.mult),
                    reads=["s1", "t3"], writes=[f"g{fc}"])
        for g4 in range(4):
            wpar = g4 % 2
            b.op("gpsimd", lambda g, e=e, g4=g4, wpar=wpar: g.dma_start(
                out=w2buf[wpar][:].rearrange("p (c f) -> p c f", c=8),
                in_=w2[e, g4].rearrange("c p f -> p c f")),
                writes=[f"w2b{wpar}"], dma=f"w2b{wpar}")
            for dco in range(4):
                dcog = g4 * 4 + dco
                pp = 2 + (dco % 2)
                for th in range(2):
                    for fc in range(8):
                        b.op("tensor", lambda t, fc=fc, th=th, dco=dco, wpar=wpar, pp=pp: t.matmul(
                            P[pp][:, th * 512:(th + 1) * 512],
                            w2buf[wpar][:, fc * 512 + dco * 128: fc * 512 + dco * 128 + 128],
                            g_t[fc][:, th * 512:(th + 1) * 512],
                            start=(fc == 0), stop=(fc == 7)),
                            reads=[f"w2b{wpar}", f"g{fc}"], writes=[f"P{pp}:{th}"])
                if e == 0:
                    b.op("vector", lambda v, dcog=dcog, pp=pp: v.tensor_copy(
                        op_t[dcog][:], P[pp][:, :]),
                        reads=[f"P{pp}:0", f"P{pp}:1"], writes=[f"op{dcog}"])
                else:
                    b.op("vector", lambda v, dcog=dcog, pp=pp: v.tensor_tensor(
                        op_t[dcog][:], P[pp][:, :], op_t[dcog][:], ALU.add),
                        reads=[f"P{pp}:0", f"P{pp}:1", f"op{dcog}"],
                        writes=[f"op{dcog}"])

    out_evs = []
    for dcog in range(16):
        par = dcog % 2
        b.op("sync", lambda s, dcog=dcog, par=par: s.dma_start(
            out=xb[par][:], in_=xin[dcog * 128:(dcog + 1) * 128, :]),
            reads=[xrg_in], writes=[f"xb{par}"], dma=f"x{par}")
        b.op("vector", lambda v, dcog=dcog, par=par: v.tensor_tensor(
            xb[par][:], xb[par][:], op_t[dcog][:], ALU.add),
            reads=[f"xb{par}", f"op{dcog}"], writes=[f"xb{par}"])
        if ctx is not None:
            b.op("scalar", lambda s, par=par: s.activation(
                out=xc[:], in_=xb[par][:], func=AF.Copy),
                reads=[f"xb{par}"], writes=["xc"])
            b.op("scalar", lambda s, par=par: s.activation(
                out=xsq[:], in_=xb[par][:], func=AF.Square),
                reads=[f"xb{par}"], writes=["xsq"])
            for th in range(2):
                b.op("tensor", lambda t, th=th, dcog=dcog: t.matmul(
                    P[3][0:1, th * 512:(th + 1) * 512],
                    ones_bf[:, 0:1], xc[:, th * 512:(th + 1) * 512],
                    start=(dcog == 0), stop=(dcog == 15), skip_group_check=True),
                    reads=["xc", "ones_bf"], writes=[f"P3:{th}"])
                b.op("tensor", lambda t, th=th, dcog=dcog: t.matmul(
                    P[1][0:1, th * 512:(th + 1) * 512],
                    ones_bf[:, 0:1], xsq[:, th * 512:(th + 1) * 512],
                    start=(dcog == 0), stop=(dcog == 15), skip_group_check=True),
                    reads=["xsq", "ones_bf"], writes=[f"P1:{th}"])
        ev = b.op("sync", lambda s, dcog=dcog, par=par: s.dma_start(
            out=xout[dcog * 128:(dcog + 1) * 128, :], in_=xb[par][:]),
            reads=[f"xb{par}"], writes=[xrg_out], dma=f"st_xo{par}")
        out_evs.append(ev)

    if ctx is None:
        b.finish(out_evs)
        b.emit(es)
        eph.close()
        es.close()
        return nc
    eph.close()
    return out_evs


def emit_ffn_scan_early(b, w1pa, w2pa, cc_in, cc_out, scanb, acc1, acc2):
    nb = len(scanb)
    for m, (pa, acc, accn) in enumerate([(w1pa, acc1, "facc1"), (w2pa, acc2, "facc2")]):
        for i in range(16):
            par = i % nb
            b.op("gpsimd", lambda s, pa=pa, i=i, par=par: s.dma_start(
                out=scanb[par][:, 0:1024], in_=pa[:, i * 1024:(i + 1) * 1024]),
                writes=[f"fscan{par}"], dma=f"fscan{par}")
            b.op("vector", lambda v, par=par, acc=acc, i=i: v.tensor_reduce(
                acc[:, i:i + 1], scanb[par][:, 0:1024], mybir.AxisListType.X, ALU.add,
                apply_absolute_value=True),
                reads=[f"fscan{par}"], writes=[accn])
        b.op("vector", lambda v, acc=acc: v.tensor_reduce(
            acc[:, 16:17], acc[:, 0:16], mybir.AxisListType.X, ALU.add),
            reads=[accn], writes=[accn])
    for m, acc, accn in [(0, acc1, "facc1"), (1, acc2, "facc2")]:
        b.op("gpsimd", lambda g, m=m, acc=acc: g.dma_start(
            out=cc_in.ap()[m:m + 1, :], in_=acc[:, 16:17]),
            reads=[accn], writes=["ccin_d"], dma="ccin")
    b.op("gpsimd", lambda g: g.collective_compute(
        "AllReduce", ALU.add, replica_groups=[list(range(8))],
        ins=[cc_in.ap()], outs=[cc_out.ap()]),
        reads=["ccin_d"], writes=["ccout_d"], cc="ccr")


def emit_ffn_scales(b, cc_out, cstbuf, srow, allred, inv1c, inv2c, P,
                    extra_reads=()):
    b.op("sync", lambda s: s.dma_start(out=allred[0:1, 0:256], in_=cc_out.ap()),
         reads=["ccout_d"], writes=["fallred"], dma="fallred")
    for m in range(2):
        b.op("vector", lambda v, m=m: v.reduce_sum(
            srow[0:1, m:m + 1], allred[0:1, m * 128:(m + 1) * 128],
            axis=mybir.AxisListType.X),
            reads=["fallred"], writes=["fsrow"])
    b.op("vector", lambda v: v.tensor_scalar(
        srow[0:1, 2:4], srow[0:1, 0:2], 1.0 / 16777216.0, 1e-8, ALU.mult, ALU.max),
        reads=["fsrow"], writes=["fsrow2"])
    b.op("vector", lambda v: v.reciprocal(srow[0:1, 2:4], srow[0:1, 2:4]),
         reads=["fsrow2"], writes=["fsrow2"])
    for m, col in [(0, inv1c), (1, inv2c)]:
        b.op("tensor", lambda t, m=m: t.matmul(
            P[3][:, 0:1], cstbuf[0:1, 8:136], srow[0:1, 2 + m:3 + m],
            start=True, stop=True),
            reads=["fsrow2", "cst"] + list(extra_reads), writes=["P3:0"])
        b.op("vector", lambda v, col=col, m=m: v.tensor_copy(col[:, 0:1], P[3][:, 0:1]),
             reads=["P3:0"], writes=[f"finv{m + 1}c"])


def build_ffn(ctx=None):
    import contextlib
    if ctx is None:
        nc = bass.Bass("TRN2", target_bir_lowering=False, debug=False)
        es = contextlib.ExitStack()
        b = None
        pref = ""
        dram = None
        xrg_in, xrg_out = "xin_d", "xout_d"
    else:
        nc, b, pref, dram = ctx["nc"], ctx["b"], ctx["pref"], ctx["dram"]
        xrg_in, xrg_out = ctx["x_in_rg"], ctx["x_out_rg"]

    def DR(name, shape, dt, kind):
        if dram is not None:
            return dram[name]
        return nc.dram_tensor(name, shape, dt, kind=kind).ap()

    xin = DR("xin", [D, TOWN], F32, "ExternalInput")
    w1 = DR("w1", [16, 16, 128, 512], F32, "ExternalInput")
    w2 = DR("w2", [4, 64, 128, 512], F32, "ExternalInput")
    w1pa = DR("w1pa", [128, 16384], F32, "ExternalInput")
    w2pa = DR("w2pa", [128, 16384], F32, "ExternalInput")
    b1c = DR("b1c", [128, 64], F32, "ExternalInput")
    b2c = DR("b2c", [128, 16], F32, "ExternalInput")
    consts = DR("consts", [128, 144], F32, "ExternalInput")
    xout = DR("xout", [D, TOWN], F32, "ExternalOutput")
    if dram is not None:
        h_dram, cc_in, cc_out = dram["h_dram"], dram["cc_in"], dram["cc_out"]
    else:
        h_dram = nc.dram_tensor("h_dram", [64, 128, TOWN], BF16)
        cc_in = nc.dram_tensor("cc_in", [2, 128], F32)
        cc_out = nc.dram_tensor("cc_out", [2, 128], F32)

    eph = contextlib.ExitStack()
    sb = lambda name, shape, dt: eph.enter_context(
        nc.sbuf_tensor(pref + name, shape, dt))
    xn_t = [sb(f"xn{i}", [128, TOWN], BF16) for i in range(16)]
    xb = [sb(f"xb{i}", [128, TOWN], F32) for i in range(2)]
    xo_sb = [sb(f"xo{i}", [128, TOWN], F32) for i in range(2)]
    xc = sb("lxc", [128, TOWN], BF16)
    xsq = sb("lxsq", [128, TOWN], BF16)
    rows = sb("rows", [1, 2048], F32)
    A_sb = sb("lnA", [128, TOWN], F32)
    B_sb = sb("lnB", [128, TOWN], F32)
    T_sb = sb("lnT", [128, TOWN], F32)
    cstbuf = sb("cstbuf", [128, 144], F32)
    ones_bf = sb("ones_bf", [128, 128], BF16)
    acc1 = sb("acc1", [128, 17], F32)
    acc2 = sb("acc2", [128, 17], F32)
    srow = sb("srow", [1, 4], F32)
    allred = sb("allred", [1, 256], F32)
    inv1c = sb("inv1c", [128, 1], F32)
    inv2c = sb("inv2c", [128, 1], F32)
    w1f = [sb(f"w1f{i}", [128, 4096], F32) for i in range(2)]
    q2h = [sb(f"q2h{i}", [128, 4096], BF16) for i in range(4)]
    scanb = w1f
    w2f = w1f
    q2b = q2h
    btmp = sb("btmp", [128, 4096], BF16)
    h2_sb = [sb(f"h2{i}", [128, TOWN], BF16) for i in range(2)]
    h_in = [sb(f"hin{i}", [128, TOWN], BF16) for i in range(8)]
    b1_sb = sb("b1_sb", [128, 64], F32)
    b2_sb = sb("b2_sb", [128, 16], F32)

    if ctx is None:
        P = [nc.alloc_psum_tensor(f"P{i}", [128, TOWN], F32) for i in range(4)]
        b = Builder(nc)
    else:
        P = ctx["P"]
        b.barrier()
    b.once_slot("cst", "ones_bf", "b1", "b2")
    b.op("sync", lambda s: s.dma_start(out=cstbuf[:], in_=consts),
         writes=["cst"], dma="cst")
    b.op("gpsimd", lambda g: g.dma_start(out=ones_bf[:], in_=consts[:, 8:136]),
         writes=["ones_bf"], dma="ones_bf")
    b.op("sync", lambda s: s.dma_start(out=b1_sb[:], in_=b1c),
         writes=["b1"], dma="b1")
    b.op("sync", lambda s: s.dma_start(out=b2_sb[:], in_=b2c),
         writes=["b2"], dma="b2")

    stats = ctx.get("stats") if ctx is not None else None
    mean = rows[0:1, 0:1024]
    var = rows[0:1, 1024:2048]
    if ctx is None or "scales_done" not in ctx:
        emit_ffn_scan_early(b, w1pa, w2pa, cc_in, cc_out, scanb, acc1, acc2)
    if stats is not None:
        for th in range(2):
            ap, rg = stats["sx"][th]
            b.op("vector", lambda v, th=th, ap=ap: v.tensor_scalar(
                mean[0:1, th * 512:(th + 1) * 512], ap,
                1.0 / 2048.0, None, ALU.mult),
                reads=[rg], writes=["mean"])
        b.op("vector", lambda v: v.tensor_tensor(var, mean, mean, ALU.mult),
             reads=["mean"], writes=["var"])
        for th in range(2):
            ap, rg = stats["sq"][th]
            b.op("vector", lambda v, th=th, ap=ap: v.scalar_tensor_tensor(
                var[0:1, th * 512:(th + 1) * 512], ap,
                1.0 / 2048.0, var[0:1, th * 512:(th + 1) * 512], ALU.mult, ALU.subtract),
                reads=[rg, "var"], writes=["var"])
    emit_ffn_scales(b, cc_out, cstbuf, srow, allred, inv1c, inv2c, P,
                    extra_reads=([stats["sx"][0][1], stats["sx"][1][1]]
                                 if stats is not None else ()))

    if stats is not None:
        pass
    else:
        for dc in range(16):
            par = dc % 2
            b.op("sync", lambda s, dc=dc, par=par: s.dma_start(
                out=xb[par][:], in_=xin[dc * 128:(dc + 1) * 128, :]),
                reads=[xrg_in], writes=[f"xb{par}"], dma=f"x{par}")
            b.op("scalar", lambda s, par=par: s.activation(
                out=xc[:], in_=xb[par][:], func=AF.Copy),
                reads=[f"xb{par}"], writes=["xc"])
            b.op("scalar", lambda s, par=par: s.activation(
                out=xsq[:], in_=xb[par][:], func=AF.Square),
                reads=[f"xb{par}"], writes=["xsq"])
            for th in range(2):
                b.op("tensor", lambda t, dc=dc, th=th: t.matmul(
                    P[0][0:1, th * 512:(th + 1) * 512],
                    ones_bf[:, 0:1], xc[:, th * 512:(th + 1) * 512],
                    start=(dc == 0), stop=(dc == 15), skip_group_check=True),
                    reads=["xc", "ones_bf"], writes=[f"P0:{th}"])
                b.op("tensor", lambda t, dc=dc, th=th: t.matmul(
                    P[1][0:1, th * 512:(th + 1) * 512],
                    ones_bf[:, 0:1], xsq[:, th * 512:(th + 1) * 512],
                    start=(dc == 0), stop=(dc == 15), skip_group_check=True),
                    reads=["xsq", "ones_bf"], writes=[f"P1:{th}"])
        for th in range(2):
            b.op("vector", lambda v, th=th: v.tensor_scalar(
                mean[0:1, th * 512:(th + 1) * 512], P[0][0:1, th * 512:(th + 1) * 512],
                1.0 / 2048.0, None, ALU.mult),
                reads=[f"P0:{th}"], writes=["mean"])
        b.op("vector", lambda v: v.tensor_tensor(var, mean, mean, ALU.mult),
             reads=["mean"], writes=["var"])
        for th in range(2):
            b.op("vector", lambda v, th=th: v.scalar_tensor_tensor(
                var[0:1, th * 512:(th + 1) * 512], P[1][0:1, th * 512:(th + 1) * 512],
                1.0 / 2048.0, var[0:1, th * 512:(th + 1) * 512], ALU.mult, ALU.subtract),
                reads=[f"P1:{th}", "var"], writes=["var"])
    b.op("scalar", lambda s: s.activation(
        out=var, in_=var, func=AF.Ln, bias=cstbuf[0:1, 1:2]),
        reads=["var", "cst"], writes=["var"])
    b.op("scalar", lambda s: s.activation(
        out=var, in_=var, func=AF.Exp, scale=-0.5),
        reads=["var"], writes=["var"])
    for th in range(2):
        b.op("tensor", lambda t, th=th: t.matmul(
            P[0][:, th * 512:(th + 1) * 512], cstbuf[0:1, 8:136],
            var[0:1, th * 512:(th + 1) * 512], start=True, stop=True),
            reads=["var", "cst"], writes=[f"P0:{th}"])
        b.op("vector", lambda v, th=th: v.tensor_copy(
            A_sb[:, th * 512:(th + 1) * 512], P[0][:, th * 512:(th + 1) * 512]),
            reads=[f"P0:{th}"], writes=["lnA"])
        b.op("tensor", lambda t, th=th: t.matmul(
            P[1][:, th * 512:(th + 1) * 512], cstbuf[0:1, 8:136],
            mean[0:1, th * 512:(th + 1) * 512], start=True, stop=True),
            reads=["mean", "cst"], writes=[f"P1:{th}"])
        b.op("vector", lambda v, th=th: v.tensor_scalar(
            B_sb[:, th * 512:(th + 1) * 512], P[1][:, th * 512:(th + 1) * 512],
            -1.0, None, ALU.mult),
            reads=[f"P1:{th}"], writes=["lnB"])
    for dc in range(16):
        par = dc % 2
        b.op("sync", lambda s, dc=dc, par=par: s.dma_start(
            out=xb[par][:], in_=xin[dc * 128:(dc + 1) * 128, :]),
            reads=[xrg_in], writes=[f"xb{par}"], dma=f"x{par}")
        b.op("vector", lambda v, par=par: v.tensor_tensor(
            T_sb[:], xb[par][:], A_sb[:], ALU.mult),
            reads=[f"xb{par}", "lnA"], writes=["lnT"])
        b.op("vector", lambda v, dc=dc: v.tensor_tensor(
            xn_t[dc][:], T_sb[:], B_sb[:], ALU.add),
            reads=["lnT", "lnB"], writes=[f"xn{dc}"])

    def quant(src_buf, src_rg, dst_buf, dst_rg, invcol, invrg):
        b.op("scalar", lambda s: s.activation(
            out=dst_buf[:], in_=src_buf[:], func=AF.Sign,
            scale=invcol[:, 0:1], bias=cstbuf[:, 4:5]),
            reads=[src_rg, invrg, "cst"], writes=[dst_rg])
        b.op("scalar", lambda s: s.activation(
            out=btmp[:], in_=src_buf[:], func=AF.Sign,
            scale=invcol[:, 0:1], bias=cstbuf[:, 3:4]),
            reads=[src_rg, invrg, "cst"], writes=["btmp"])
        b.op("vector", lambda v: v.tensor_tensor(
            dst_buf[:], dst_buf[:], btmp[:], ALU.add),
            reads=[dst_rg, "btmp"], writes=[dst_rg])

    for fg in range(16):
        qb = (fg % 2) * 2
        for half in range(2):
            par = half
            b.op("gpsimd", lambda s, fg=fg, half=half, par=par: s.dma_start(
                out=w1f[par][:].rearrange("p (c f) -> p c f", c=8),
                in_=w1[fg, half * 8:(half + 1) * 8].rearrange("c p f -> p c f")),
                writes=[f"w1f{par}"], dma=f"w1f{par}")
            quant(w1f[par], f"w1f{par}", q2h[qb + par], f"q2h{qb + par}", inv1c, "finv1c")
        for fsub in range(4):
            fc = fg * 4 + fsub
            pp = fsub % 2
            for th in range(2):
                for dc in range(16):
                    b.op("tensor", lambda t, dc=dc, th=th, fsub=fsub, pp=pp, qb=qb: t.matmul(
                        P[pp][:, th * 512:(th + 1) * 512],
                        q2h[qb + dc // 8][:, (dc % 8) * 512 + fsub * 128: (dc % 8) * 512 + fsub * 128 + 128],
                        xn_t[dc][:, th * 512:(th + 1) * 512],
                        start=(dc == 0), stop=(dc == 15)),
                        reads=[f"q2h{qb + dc // 8}", f"xn{dc}"], writes=[f"P{pp}:{th}"])
            hp = fc % 2
            b.op("scalar", lambda s, fc=fc, pp=pp, hp=hp: s.activation(
                out=h2_sb[hp][:], in_=P[pp][:, :], func=AF.Gelu,
                scale=0.5, bias=b1_sb[:, fc:fc + 1]),
                reads=[f"P{pp}:0", f"P{pp}:1", "b1"], writes=[f"h2{hp}"])
            b.op("sync", lambda s, fc=fc, hp=hp: s.dma_start(
                out=h_dram[fc], in_=h2_sb[hp][:]),
                reads=[f"h2{hp}"], writes=[f"hd{fc}"], dma=f"st_h{hp}")

    out_evs = []
    for S in range(4):
        for f8 in range(8):
            par = f8 % 2
            qp = f8 % 4
            b.op("gpsimd", lambda s, S=S, f8=f8, par=par: s.dma_start(
                out=w2f[par][:].rearrange("p (c f) -> p c f", c=8),
                in_=w2[S, f8 * 8:(f8 + 1) * 8].rearrange("c p f -> p c f")),
                writes=[f"w1f{par}"], dma=f"w1f{par}")
            quant(w2f[par], f"w1f{par}", q2b[qp], f"q2h{qp}", inv2c, "finv2c")
            if f8 < 7:
                for fc_l in range(8):
                    fcg = f8 * 8 + fc_l
                    hp = fcg % 8
                    b.op("sync", lambda s, fcg=fcg, hp=hp: s.dma_start(
                        out=h_in[hp][:], in_=h_dram[fcg]),
                        reads=[f"hd{fcg}"], writes=[f"hin{hp}"], dma=f"hi{hp}")
                    for dco in range(4):
                        for th in range(2):
                            b.op("tensor", lambda t, fc_l=fc_l, dco=dco, th=th, qp=qp, hp=hp, fcg=fcg: t.matmul(
                                P[dco][:, th * 512:(th + 1) * 512],
                                q2b[qp][:, fc_l * 512 + dco * 128: fc_l * 512 + dco * 128 + 128],
                                h_in[hp][:, th * 512:(th + 1) * 512],
                                start=(fcg == 0), stop=False),
                                reads=[f"q2h{qp}", f"hin{hp}"], writes=[f"P{dco}:{th}"])
            else:
                for fc_l in range(8):
                    fcg = f8 * 8 + fc_l
                    hp = fcg % 8
                    b.op("sync", lambda s, fcg=fcg, hp=hp: s.dma_start(
                        out=h_in[hp][:], in_=h_dram[fcg]),
                        reads=[f"hd{fcg}"], writes=[f"hin{hp}"], dma=f"hi{hp}")
                for dco in range(4):
                    for fc_l in range(8):
                        fcg = f8 * 8 + fc_l
                        hp = fcg % 8
                        for th in range(2):
                            b.op("tensor", lambda t, fc_l=fc_l, dco=dco, th=th, qp=qp, hp=hp: t.matmul(
                                P[dco][:, th * 512:(th + 1) * 512],
                                q2b[qp][:, fc_l * 512 + dco * 128: fc_l * 512 + dco * 128 + 128],
                                h_in[hp][:, th * 512:(th + 1) * 512],
                                start=False, stop=(fc_l == 7)),
                                reads=[f"q2h{qp}", f"hin{hp}"], writes=[f"P{dco}:{th}"])
        for dco in range(4):
            dcog = S * 4 + dco
            par = dco % 2
            b.op("sync", lambda s, dcog=dcog, par=par: s.dma_start(
                out=xb[par][:], in_=xin[dcog * 128:(dcog + 1) * 128, :]),
                reads=[xrg_in], writes=[f"xb{par}"], dma=f"x{par}")
            b.op("vector", lambda v, dcog=dcog, par=par: v.tensor_scalar(
                xb[par][:], xb[par][:], b2_sb[:, dcog:dcog + 1], None, ALU.add),
                reads=[f"xb{par}", "b2"], writes=[f"xb{par}"])
            b.op("vector", lambda v, dco=dco, par=par: v.scalar_tensor_tensor(
                xo_sb[par][:], P[dco][:, :], 0.5, xb[par][:], ALU.mult, ALU.add),
                reads=[f"P{dco}:0", f"P{dco}:1", f"xb{par}"], writes=[f"xo{par}"])
            ev = b.op("sync", lambda s, dcog=dcog, par=par: s.dma_start(
                out=xout[dcog * 128:(dcog + 1) * 128, :], in_=xo_sb[par][:]),
                reads=[f"xo{par}"], writes=[xrg_out], dma=f"st_xo{par}")
            out_evs.append(ev)

    if ctx is None:
        b.finish(out_evs)
        b.emit(es)
        eph.close()
        es.close()
        return nc
    eph.close()
    return out_evs


def build_mega():
    import contextlib
    nc = bass.Bass("TRN2", target_bir_lowering=False, debug=False)
    es = contextlib.ExitStack()

    def EI(name, shape, dt):
        return nc.dram_tensor(name, shape, dt, kind="ExternalInput").ap()

    xin0 = EI("xin0", [D, TOWN], F32)
    consts = EI("consts", [128, 144], F32)
    ident = EI("ident", [128, 128], F32)
    l0 = {n: EI(f"l0_{n}", [4, 16, 128, 512], F32) for n in ("wq", "wk", "wv", "wo")}
    bcols = EI("bcols", [128, 48], F32)
    bvrow = EI("bvrow", [1, D], F32)
    bocols = EI("bocols", [128, 16], F32)
    wg_m = EI("wg_m", [16, 128, 8], F32)
    mbias = EI("mbias", [8, 1], F32)
    w1m = EI("w1m", [8, 4, 16, 128, 256], F32)
    w3m = EI("w3m", [8, 4, 16, 128, 256], F32)
    w2m = EI("w2m", [8, 4, 8, 128, 512], F32)
    l1 = {n: EI(f"l1_{n}", [4, 16, 128, 512], F32) for n in ("wq", "wk", "wv", "wg", "wo")}
    cosin = EI("cosin", [128, 2 * TOWN], F32)
    protm = EI("protm", [128, 128], F32)
    normw = EI("normw", [128, 1], F32)
    maskA = EI("maskA", [128, 128], F32)
    maskB = EI("maskB", [128, 128], F32)
    w1f_ = EI("w1f_", [16, 16, 128, 512], F32)
    w2f_ = EI("w2f_", [4, 64, 128, 512], F32)
    w1pa = EI("w1pa", [128, 16384], F32)
    w2pa = EI("w2pa", [128, 16384], F32)
    b1c = EI("b1c", [128, 64], F32)
    b2c = EI("b2c", [128, 16], F32)

    xout_ext = nc.dram_tensor("xout_ext", [D, TOWN], F32, kind="ExternalOutput").ap()

    xa = nc.dram_tensor("xa", [D, TOWN], F32).ap()
    xbd = nc.dram_tensor("xbd", [D, TOWN], F32).ap()
    q_st = nc.dram_tensor("q_st", [NH, 128, TOWN], BF16).ap()
    g_st = nc.dram_tensor("g_st", [NH, 128, TOWN], BF16).ap()
    k_own = nc.dram_tensor("k_own", [2048, TOWN], BF16)
    v_own = nc.dram_tensor("v_own", [2048, TOWN], BF16)
    k_gth = nc.dram_tensor("k_gth", [4096, TOWN], BF16)
    v_gth = nc.dram_tensor("v_gth", [4096, TOWN], BF16)
    h_dram = nc.dram_tensor("h_dram", [64, 128, TOWN], BF16)
    cc_in = nc.dram_tensor("cc_in", [2, 128], F32)
    cc_out = nc.dram_tensor("cc_out", [2, 128], F32)

    kv_k = k_own.ap().rearrange("(h p) t -> h p t", h=16)
    kv_v = v_own.ap().rearrange("(a b) c -> a (b c)", b=2)

    P = [nc.alloc_psum_tensor(f"P{i}", [128, TOWN], F32) for i in range(4)]
    b = Builder(nc)

    m_scan = [nc.alloc_sbuf_tensor(f"m_scan{i}", [128, 1024], F32) for i in range(2)]
    m_acc1 = nc.alloc_sbuf_tensor("m_acc1", [128, 17], F32)
    m_acc2 = nc.alloc_sbuf_tensor("m_acc2", [128, 17], F32)

    out_evs = None
    xseq = {0: (xin0, "x_ext", xa, "xa_d"),
            1: (xbd, "xbd_d", xa, "xa_d")}
    attn_stats = {"sx": [(P[3][0:1, 0:512], "R0"), (P[3][0:1, 512:1024], "R1")],
                  "sq": [(P[1][0:1, 0:512], "SC1:0"), (P[1][0:1, 512:1024], "SC1:1")]}
    moe_stats = {"sx": [(P[3][0:1, 0:512], "P3:0"), (P[3][0:1, 512:1024], "P3:1")],
                 "sq": [(P[1][0:1, 0:512], "P1:0"), (P[1][0:1, 512:1024], "P1:1")]}
    for layer in (0, 1):
        x_in, x_in_rg, x_mid, x_mid_rg = xseq[layer]
        dram = {"xin": x_in, "consts": consts,
                "q_out": q_st, "k_out": kv_k, "v_out": kv_v}
        if layer == 0:
            dram.update(wq=l0["wq"], wk=l0["wk"], wv=l0["wv"],
                        bcols=bcols, bvrow=bvrow)
        else:
            dram.update(wq=l1["wq"], wk=l1["wk"], wv=l1["wv"], wg=l1["wg"],
                        cosin=cosin, protm=protm, g_out=g_st)
        pctx = dict(nc=nc, b=b, P=P, pref=f"p{layer}_",
                    dram=dram, x_in_rg=x_in_rg,
                    x_out_rg="unused", kv_gath=None,
                    k_own=k_own.ap(), v_own=v_own.ap(),
                    k_gth=k_gth.ap(), v_gth=v_gth.ap())
        if layer == 1:
            pctx["stats"] = moe_stats
        build_proj(layer, 0, ctx=pctx)

        dram = {"xin": x_in, "q_in": q_st, "consts": consts,
                "wo": l0["wo"] if layer == 0 else l1["wo"],
                "xout": x_mid}
        if layer == 0:
            dram["bocols"] = bocols
        else:
            dram.update(g_in=g_st, normw=normw, maskA=maskA, maskB=maskB,
                        ident=ident)
        if layer == 0:
            emit_ffn_scan_early(b, w1pa, w2pa, cc_in, cc_out, m_scan,
                                m_acc1, m_acc2)
        build_attn(layer, ctx=dict(nc=nc, b=b, P=P, pref=f"a{layer}_",
                                   dram=dram, x_in_rg=x_in_rg,
                                   x_out_rg=x_mid_rg,
                                   kv_gath=(k_gth.ap(), v_gth.ap())))
        if layer == 0:
            dram = {"xin": x_mid, "wg": wg_m, "mbias": mbias, "w1": w1m,
                    "w3": w3m, "w2": w2m, "ident": ident, "consts": consts,
                    "xout": xbd}
            build_moe(ctx=dict(nc=nc, b=b, P=P, pref="m_", dram=dram,
                               x_in_rg=x_mid_rg, x_out_rg="xbd_d",
                               stats=attn_stats))
        else:
            dram = {"xin": x_mid, "w1": w1f_, "w2": w2f_, "w1pa": w1pa,
                    "w2pa": w2pa, "b1c": b1c, "b2c": b2c, "consts": consts,
                    "xout": xout_ext, "h_dram": h_dram, "cc_in": cc_in,
                    "cc_out": cc_out}
            out_evs = build_ffn(ctx=dict(nc=nc, b=b, P=P, pref="f_", dram=dram,
                                         x_in_rg=x_mid_rg, x_out_rg="xfin_d",
                                         scales_done=True, stats=attn_stats))

    b.finish(out_evs)
    b.emit(es)
    es.close()
    return nc


def _own_token_idx(core):
    blocks = EVEN_BLOCKS if core % 2 == 0 else ODD_BLOCKS
    return np.concatenate([np.arange(blk * 128, blk * 128 + 128) for blk in blocks])


def _consts_array():
    c = np.zeros((128, 144), dtype=np.float32)
    c[:, 0] = -CSHIFT
    c[:, 1] = 1e-5
    c[:, 2] = 1e-5
    c[:, 3] = 0.5
    c[:, 4] = -0.5
    c[:, 8:136] = 1.0
    return c


def _block_w(wT):
    return np.ascontiguousarray(
        wT.reshape(16, 128, 4, 512).transpose(2, 0, 1, 3))


def _rope_tables(core):
    idx = _own_token_idx(core).astype(np.float32)
    inv_freq = 1.0 / (10000.0 ** (np.arange(0, HD, 2, dtype=np.float32) / HD))
    freqs = np.outer(idx, inv_freq)
    emb = np.concatenate([freqs, freqs], axis=1)
    cos = np.cos(emb).T.astype(np.float32)
    sin = np.sin(emb).T.astype(np.float32)
    return np.ascontiguousarray(np.concatenate([cos, sin], axis=1))


def _rot_matrix():
    P = np.zeros((128, 128), dtype=np.float32)
    for m in range(64):
        P[m + 64, m] = -1.0
    for m in range(64, 128):
        P[m - 64, m] = 1.0
    return P


def _prep_shared(inp):
    d = {}
    d["l0_wq"] = _block_w(np.ascontiguousarray(inp["L0_attn_Wq"].T))
    d["l0_wk"] = _block_w(np.ascontiguousarray(inp["L0_attn_Wk"].T))
    d["l0_wv"] = _block_w(np.ascontiguousarray(inp["L0_attn_Wv"].T))
    d["l0_wo"] = _block_w(np.ascontiguousarray(inp["L0_attn_Wo"].T))
    bcols = np.zeros((128, 48), np.float32)
    bcols[:, 0:16] = inp["L0_attn_bq"].reshape(16, 128).T
    bcols[:, 16:32] = inp["L0_attn_bk"].reshape(16, 128).T
    d["l0_bcols"] = bcols
    d["l0_bvrow"] = np.ascontiguousarray(inp["L0_attn_bv"].reshape(1, -1))
    d["l0_bocols"] = np.ascontiguousarray(inp["L0_attn_bo"].reshape(16, 128).T)
    d["wg"] = np.ascontiguousarray(inp["L0_ffn_Wg"].T.reshape(16, 128, 8))
    d["mbias"] = np.ascontiguousarray(inp["L0_ffn_bias"].reshape(8, 1))
    d["w1"] = np.ascontiguousarray(inp["L0_ffn_W1"].transpose(0, 2, 1).reshape(8, 16, 128, 4, 256).transpose(0, 3, 1, 2, 4))
    d["w3"] = np.ascontiguousarray(inp["L0_ffn_W3"].transpose(0, 2, 1).reshape(8, 16, 128, 4, 256).transpose(0, 3, 1, 2, 4))
    d["w2"] = np.ascontiguousarray(inp["L0_ffn_W2"].transpose(0, 2, 1).reshape(8, 8, 128, 4, 512).transpose(0, 3, 1, 2, 4))
    d["l1_wq"] = _block_w(np.ascontiguousarray(inp["L1_attn_wq"].T))
    d["l1_wk"] = _block_w(np.ascontiguousarray(inp["L1_attn_wk"].T))
    d["l1_wv"] = _block_w(np.ascontiguousarray(inp["L1_attn_wv"].T))
    d["l1_wg"] = _block_w(np.ascontiguousarray(inp["L1_attn_wg"].T))
    d["l1_wo"] = _block_w(np.ascontiguousarray(inp["L1_attn_wo"].T))
    d["normw"] = np.ascontiguousarray(inp["L1_attn_norm_w"].reshape(128, 1))
    fw1 = inp["L1_ffn_w1"]
    fw2 = inp["L1_ffn_w2"]
    d["f_w1"] = np.ascontiguousarray(fw1.T.reshape(16, 128, 16, 512).transpose(2, 0, 1, 3))
    d["f_w2"] = np.ascontiguousarray(fw2.T.reshape(64, 128, 4, 512).transpose(2, 0, 1, 3))
    n = 16777216 // 8
    d["f_w1pa"] = [np.ascontiguousarray(fw1.reshape(-1)[c * n:(c + 1) * n].reshape(128, -1)) for c in range(8)]
    d["f_w2pa"] = [np.ascontiguousarray(fw2.reshape(-1)[c * n:(c + 1) * n].reshape(128, -1)) for c in range(8)]
    d["f_b1c"] = np.ascontiguousarray(inp["L1_ffn_b1"].reshape(64, 128).T)
    d["f_b2c"] = np.ascontiguousarray(inp["L1_ffn_b2"].reshape(16, 128).T)
    d["consts"] = _consts_array()
    d["ident"] = np.eye(128, dtype=np.float32)
    tri = np.where(np.arange(128)[:, None] > np.arange(128)[None, :],
                   np.float32(-1e30), np.float32(0.0))
    d["tri"] = tri
    d["fullmask"] = np.full((128, 128), -1e30, np.float32)
    d["zeromask"] = np.zeros((128, 128), np.float32)
    return d


_NC_CACHE = {}


def _get_nc(name, fn):
    if name not in _NC_CACHE:
        _NC_CACHE[name] = fn()
    return _NC_CACHE[name]


FUSED = True


def kernel(**inputs):
    if FUSED:
        return kernel_fused(**inputs)
    return kernel_unfused(**inputs)


def kernel_fused(**inputs):
    P = _prep_shared(inputs)
    nc = _get_nc("mega", build_mega)
    in_maps = []
    for c in range(8):
        im = {
            "xin0": np.ascontiguousarray(
                inputs["L0_attn_x"][c // 2][_own_token_idx(c)].T),
            "consts": P["consts"], "ident": P["ident"],
            "l0_wq": P["l0_wq"], "l0_wk": P["l0_wk"], "l0_wv": P["l0_wv"],
            "l0_wo": P["l0_wo"], "bcols": P["l0_bcols"], "bvrow": P["l0_bvrow"],
            "bocols": P["l0_bocols"],
            "wg_m": P["wg"], "mbias": P["mbias"], "w1m": P["w1"],
            "w3m": P["w3"], "w2m": P["w2"],
            "l1_wq": P["l1_wq"], "l1_wk": P["l1_wk"], "l1_wv": P["l1_wv"],
            "l1_wg": P["l1_wg"], "l1_wo": P["l1_wo"],
            "cosin": _rope_tables(c), "protm": _rot_matrix(),
            "normw": P["normw"],
            "maskA": P["tri"] if c % 2 == 0 else P["zeromask"],
            "maskB": P["fullmask"] if c % 2 == 0 else P["tri"],
            "w1f_": P["f_w1"], "w2f_": P["f_w2"],
            "w1pa": P["f_w1pa"][c], "w2pa": P["f_w2pa"][c],
            "b1c": P["f_b1c"], "b2c": P["f_b2c"],
        }
        in_maps.append(im)
    res = run_bass_kernel_spmd(nc, in_maps, core_ids=list(range(8)))
    out = np.empty((4, TFULL, D), dtype=np.float32)
    for c in range(8):
        out[c // 2, _own_token_idx(c), :] = np.asarray(res.results[c]["xout_ext"]).T
    return out


def kernel_unfused(**inputs):
    P = _prep_shared(inputs)
    x_fm = [np.ascontiguousarray(
        inputs["L0_attn_x"][c // 2][_own_token_idx(c)].T) for c in range(8)]

    def run(nc, in_maps):
        res = run_bass_kernel_spmd(nc, in_maps, core_ids=list(range(8)))
        return res.results

    for layer in (0, 1):
        nc = _get_nc(f"proj{layer}", lambda layer=layer: build_proj(layer, 0))
        ims = []
        for c in range(8):
            im = {"xin": x_fm[c], "consts": P["consts"]}
            if layer == 0:
                im.update(wq=P["l0_wq"], wk=P["l0_wk"], wv=P["l0_wv"],
                          bcols=P["l0_bcols"], bvrow=P["l0_bvrow"])
            else:
                im.update(wq=P["l1_wq"], wk=P["l1_wk"], wv=P["l1_wv"],
                          wg=P["l1_wg"], cosin=_rope_tables(c), protm=_rot_matrix())
            ims.append(im)
        r = run(nc, ims)
        kf, vf, qf, gf = {}, {}, {}, {}
        for p in range(4):
            e, o = 2 * p, 2 * p + 1
            kf[p] = np.concatenate([r[e]["k_out"], r[o]["k_out"]], axis=2)
            vf[p] = np.concatenate([r[e]["v_out"], r[o]["v_out"]], axis=0)
        for c in range(8):
            qf[c] = r[c]["q_out"]
            if layer == 1:
                gf[c] = r[c]["g_out"]
        nc = _get_nc(f"attn{layer}", lambda layer=layer: build_attn(layer))
        ims = []
        for c in range(8):
            im = {"xin": x_fm[c], "q_in": qf[c], "k_in": kf[c // 2],
                  "v_in": vf[c // 2], "consts": P["consts"],
                  "wo": P["l0_wo"] if layer == 0 else P["l1_wo"]}
            if layer == 0:
                im["bocols"] = P["l0_bocols"]
            else:
                im.update(g_in=gf[c], normw=P["normw"],
                          maskA=P["tri"] if c % 2 == 0 else P["zeromask"],
                          maskB=P["fullmask"] if c % 2 == 0 else P["tri"])
            ims.append(im)
        r = run(nc, ims)
        x_fm = [np.asarray(r[c]["xout"]) for c in range(8)]
        if layer == 0:
            nc = _get_nc("moe", build_moe)
            ims = [{"xin": x_fm[c], "wg": P["wg"], "mbias": P["mbias"],
                    "w1": P["w1"], "w3": P["w3"], "w2": P["w2"],
                    "ident": P["ident"], "consts": P["consts"]}
                   for c in range(8)]
        else:
            nc = _get_nc("ffn", build_ffn)
            ims = [{"xin": x_fm[c], "w1": P["f_w1"], "w2": P["f_w2"],
                    "w1pa": P["f_w1pa"][c], "w2pa": P["f_w2pa"][c],
                    "b1c": P["f_b1c"], "b2c": P["f_b2c"],
                    "consts": P["consts"]} for c in range(8)]
        r = run(nc, ims)
        x_fm = [np.asarray(r[c]["xout"]) for c in range(8)]

    out = np.empty((4, TFULL, D), dtype=np.float32)
    for c in range(8):
        out[c // 2, _own_token_idx(c), :] = x_fm[c].T
    return out
